# version 13
# speedup vs baseline: 1.2711x; 1.0663x over previous
import numpy as np
import ml_dtypes
from contextlib import ExitStack
import concourse.bass as bass
import concourse.mybir as mybir
from concourse.alu_op_type import AluOpType as Op
from concourse.bass_utils import run_bass_kernel_spmd
from concourse.bass import IndirectOffsetOnAxis

F32 = mybir.dt.float32
BF16 = mybir.dt.bfloat16
I32 = mybir.dt.int32
U32 = mybir.dt.uint32
AF = mybir.ActivationFunctionType
AX = mybir.AxisListType.X
BF = ml_dtypes.bfloat16

NCORES = 8
B, S, D = 2, 8192, 1024
T = B * S
TPC = T // NCORES
NT = TPC // 128

ENGINES = ("sync", "vector", "scalar", "tensor", "gpsimd")


class Prog:

    SEMS = ("Cv", "Cs", "Ct", "Cg", "Cy", "Dy", "Ds", "Dg", "CC")
    STEP = {"Cv": 1, "Cs": 1, "Ct": 1, "Cg": 1, "Cy": 1, "Dy": 16, "Ds": 16, "Dg": 16, "CC": 1}

    def __init__(self):
        self.nc = bass.Bass("TRN2", target_bir_lowering=False)
        self.ops = []
        self._es = ExitStack()
        self._scopes = []

    def _sem_of(self, engine, kind):
        if kind == "cc":
            return "CC"
        if kind in ("dma", "sdma"):
            return {"sync": "Dy", "scalar": "Ds", "gpsimd": "Dg", "vector": "Dy", "tensor": "Dy"}[engine]
        return {"vector": "Cv", "scalar": "Cs", "tensor": "Ct", "gpsimd": "Cg", "sync": "Cy"}[engine]

    def din(self, name, shape, dtype):
        return self.nc.dram_tensor(name, list(shape), dtype, kind="ExternalInput").ap()

    def dout(self, name, shape, dtype):
        return self.nc.dram_tensor(name, list(shape), dtype, kind="ExternalOutput").ap()

    def dint(self, name, shape, dtype):
        return self.nc.dram_tensor(name, list(shape), dtype).ap()

    def sb(self, name, shape, dtype):
        es = self._scopes[-1] if self._scopes else self._es
        return es.enter_context(self.nc.sbuf_tensor(name, list(shape), dtype))

    def ps(self, name, shape, dtype=F32):
        es = self._scopes[-1] if self._scopes else self._es
        assert len(shape) == 2
        esz = {F32: 4, BF16: 2, I32: 4, U32: 4}[dtype]
        cols = max(shape[1], 2048 // esz)
        h = es.enter_context(self.nc.psum_tensor(name, [shape[0], cols], dtype))
        return h[:, 0:shape[1]]

    from contextlib import contextmanager as _cm

    @_cm
    def scope(self):
        es = ExitStack()
        self._scopes.append(es)
        try:
            with es:
                yield
        finally:
            self._scopes.pop()
            self.ops.append(dict(engine=None, kind="barrier", fn=None, reads=[], writes=[]))

    @staticmethod
    def _region(ap):
        if ap is None:
            return None
        t = getattr(ap, "tensor", None)
        if t is None:
            if hasattr(ap, "ap") and hasattr(ap, "axis"):
                return Prog._region(ap.ap)
            return None
        off = ap.offset
        if not isinstance(off, int):
            return (t.name, 0, 1 << 62)
        span = 1
        for st_, ct in ap.ap:
            span += (ct - 1) * abs(st_)
        return (t.name, off, off + span)

    @staticmethod
    def _collect(args, kwargs, first_is_write=True):
        reads, writes = [], []
        seen_first = [False]

        def add(x, force_read=False):
            r = Prog._region(x)
            if r is None:
                return
            if not seen_first[0] and first_is_write and not force_read:
                writes.append(r)
                seen_first[0] = True
            else:
                reads.append(r)

        for a in args:
            if hasattr(a, "tensor") or (hasattr(a, "ap") and hasattr(a, "axis")):
                add(a)
        for k, v in kwargs.items():
            if hasattr(v, "tensor") or (hasattr(v, "ap") and hasattr(v, "axis")):
                if k in ("accum_out", "out", "out_offset"):
                    writes.append(Prog._region(v))
                else:
                    reads.append(Prog._region(v))
        return reads, writes

    def emit(self, engine, fn, kind="comp", reads=None, writes=None):
        assert engine in ENGINES
        self.ops.append(dict(engine=engine, kind=kind, fn=fn,
                             reads=reads or [], writes=writes or []))

    def raw(self, engine, fn):
        self.emit(engine, fn, "raw")

    def dma(self, out, in_, q="sync", **kw):
        kind = "sdma" if q == "gpsimd" else "dma"
        r_out = self._region(out)
        r_in = self._region(in_)
        self.emit(q, (lambda e, out=out, in_=in_, kw=kw: e.dma_start(out=out, in_=in_, **kw)),
                  kind, reads=[r_in], writes=[r_out])

    def idma(self, q, **kw):
        reads, writes = self._collect((), kw, first_is_write=False)
        self.emit(q, (lambda e, kw=kw: e.indirect_dma_start(**kw)), "sdma",
                  reads=reads, writes=writes)

    def op(self, engine, meth, *a, **kw):
        reads, writes = self._collect(a, kw)
        if meth == "matmul":
            reads = reads + writes
        self.emit(engine, (lambda e, a=a, kw=kw: getattr(e, meth)(*a, **kw)), "comp",
                  reads=reads, writes=writes)

    def v(self, meth, *a, **kw):
        self.op("vector", meth, *a, **kw)

    def a(self, meth, *a, **kw):
        self.op("scalar", meth, *a, **kw)

    def t(self, meth, *a, **kw):
        self.op("tensor", meth, *a, **kw)

    def g(self, meth, *a, **kw):
        self.op("gpsimd", meth, *a, **kw)

    def cc(self, kind, op, ins, outs):
        self.emit("gpsimd", (lambda e, k=kind, o=op, i=ins, u=outs:
                             e.collective_compute(k, o, replica_groups=[list(range(NCORES))],
                                                  ins=i, outs=u)), "cc",
                  reads=[self._region(x) for x in ins],
                  writes=[self._region(x) for x in outs])

    NLANES = 8

    def finish(self):
        nc = self.nc
        L = self.NLANES
        def step_of(base):
            return 16 if base in ("Dy", "Ds", "Dg") else 1

        ordinal = {}
        counts = {}
        qcount = {}
        last_access = {}
        req = []
        gate = {}
        barrier_req = {}
        for i, o in enumerate(self.ops):
            if o["kind"] == "barrier":
                req.append(None)
                barrier_req = dict(counts)
                last_access = {}
                continue
            if o["kind"] == "raw":
                req.append(dict(barrier_req))
                continue
            base = self._sem_of(o["engine"], o["kind"])
            if o["kind"] in ("dma", "sdma"):
                qn = qcount.get(base, 0)
                qcount[base] = qn + 1
                key = (base, qn % L)
            elif o["kind"] == "cc":
                key = (base, 0)
            else:
                key = (base, 0)
            myreq = dict(barrier_req)
            my_engine = o["engine"]
            my_comp = o["kind"] == "comp"

            def need(j):
                oj = self.ops[j]
                if my_comp and oj["kind"] == "comp" and oj["engine"] == my_engine == "tensor":
                    return
                k2, o2 = ordinal[j]
                myreq[k2] = max(myreq.get(k2, 0), o2 + 1)

            for (nm, lo, hi) in o["reads"]:
                for (l2, h2, w2, j) in last_access.get(nm, ()):
                    if w2 and l2 < hi and lo < h2:
                        need(j)
            for (nm, lo, hi) in o["writes"]:
                for (l2, h2, w2, j) in last_access.get(nm, ()):
                    if l2 < hi and lo < h2:
                        need(j)
            k = counts.get(key, 0)
            ordinal[i] = (key, k)
            counts[key] = k + 1
            if o["kind"] in ("dma", "sdma", "cc"):
                gate[i] = (key, k)
            req.append(myreq)
            for (nm, lo, hi) in o["writes"]:
                last_access.setdefault(nm, []).append((lo, hi, True, i))
            for (nm, lo, hi) in o["reads"]:
                last_access.setdefault(nm, []).append((lo, hi, False, i))
        totals = dict(counts)

        with self._es:
            sems = {key: self._es.enter_context(nc.semaphore(f"S{key[0]}{key[1]}"))
                    for key in totals}
            blk = self._es.enter_context(nc.Block())

            def section(eng_name):
                def body(eng):
                    hw = {key: 0 for key in totals}
                    def wait(key, k):
                        if k > hw[key]:
                            eng.wait_ge(sems[key], step_of(key[0]) * k)
                            hw[key] = k
                    for i, o in enumerate(self.ops):
                        if o["kind"] == "barrier" or o["engine"] != eng_name:
                            continue
                        if o["kind"] == "raw":
                            for key, k in (req[i] or {}).items():
                                wait(key, k)
                            o["fn"](eng)
                            continue
                        for key, k in req[i].items():
                            wait(key, k)
                        if i in gate:
                            wait(gate[i][0], gate[i][1])
                        inst = o["fn"](eng)
                        key = ordinal[i][0]
                        inst.then_inc(sems[key], step_of(key[0]))
                    if eng_name == "sync":
                        for key, tot in totals.items():
                            wait(key, tot)
                return body

            for name in ENGINES:
                getattr(blk, name)(section(name))
        return nc


def shard_rows(x, n=NCORES):
    x = np.ascontiguousarray(x)
    return np.split(x, n, axis=0)


def run(prog, in_maps, cores=None):
    cores = cores if cores is not None else list(range(len(in_maps)))
    res = run_bass_kernel_spmd(prog.nc, in_maps, core_ids=cores)
    return res.results



def emit_s1(p, g, x, xn_out, xnb_out, xnlo_out, rows):
    with p.scope():
        xt = p.sb(g + "xt", [128, D], F32)
        sq = p.sb(g + "sq", [128, D], F32)
        xnt = p.sb(g + "xnt", [128, D], F32)
        xnbt = p.sb(g + "xnbt", [128, D], BF16)
        xnlot = p.sb(g + "xnlot", [128, D], BF16)
        st = p.sb(g + "st", [128, 8], F32)
        nb = p.sb(g + "nb", [128, 1], F32)
        nt = rows // 128
        for i in range(nt):
            r = slice(i * 128, (i + 1) * 128)
            p.dma(xt[:], x[r, :])
            p.a("activation", sq[:], xt[:], AF.Square, accum_out=st[:, 1:2])
            p.v("tensor_reduce", st[:, 0:1], xt[:], AX, Op.add)
            p.v("tensor_scalar", st[:, 2:3], st[:, 0:1], 1.0 / D, None, Op.mult)
            p.v("tensor_scalar", st[:, 3:4], st[:, 1:2], 1.0 / D, None, Op.mult)
            p.v("tensor_tensor", st[:, 4:5], st[:, 2:3], st[:, 2:3], Op.mult)
            p.v("tensor_tensor", st[:, 5:6], st[:, 3:4], st[:, 4:5], Op.subtract)
            p.v("tensor_scalar", st[:, 5:6], st[:, 5:6], 1e-5, None, Op.add)
            p.a("activation", st[:, 6:7], st[:, 5:6], AF.Sqrt)
            p.v("reciprocal", st[:, 7:8], st[:, 6:7])
            p.v("tensor_scalar", nb[:], st[:, 2:3], -1.0, None, Op.mult)
            p.v("tensor_scalar", xnt[:], xt[:], nb[:], st[:, 7:8], Op.add, Op.mult)
            p.v("tensor_copy", xnbt[:], xnt[:])
            p.v("tensor_tensor", xnlot[:], xnt[:], xnbt[:], Op.subtract)
            p.dma(xn_out[r, :], xnt[:])
            p.dma(xnb_out[r, :], xnbt[:])
            p.dma(xnlo_out[r, :], xnlot[:])


def build_s1(rows):
    p = Prog()
    x = p.din("x", [rows, D], F32)
    xn_out = p.dout("xn", [rows, D], F32)
    xnb_out = p.dout("xnb", [rows, D], BF16)
    xnlo_out = p.dout("xnlo", [rows, D], BF16)
    emit_s1(p, "s1_", x, xn_out, xnb_out, xnlo_out, rows)
    p.finish()
    return p


def stage1(x_full):
    p = build_s1(TPC)
    shards = shard_rows(x_full.reshape(T, D))
    res = run(p, [{"x": s} for s in shards])
    xn = np.concatenate([r["xn"] for r in res], 0)
    xnb = np.concatenate([r["xnb"] for r in res], 0)
    xnlo = np.concatenate([r["xnlo"] for r in res], 0)
    return xn, xnb, xnlo





SQS = 1.0 / float(np.sqrt(np.float32(S)))
NEG = -1.0e9

def attn_masks():
    i = np.arange(128)[:, None]
    j = np.arange(512)[None, :]
    bm = np.stack([np.where(128 * pat + i >= j, 0.0, NEG) for pat in range(4)])
    am = np.stack([np.where(j < 128 * r + i, 0.0, NEG) for r in range(4)])
    return bm.astype(BF), am.astype(BF)

def emit_s2(p, g, xnb, Whead, bhead, bmask_in, amask_in, identb_in, identf_in, zout):
    with p.scope():
        wraw = p.sb(g + "wraw", [64, 3, D], F32)
        wbf = p.sb(g + "wbf", [64, 3, D], BF16)
        wT = p.sb(g + "wT", [128, 8, 3, 128], BF16)
        bqt = p.sb(g + "bqt", [128, 2], F32)
        bvr = p.sb(g + "bvr", [1, 128], F32)
        ones1 = p.sb(g + "ones1", [1, 128], F32)
        bmask = p.sb(g + "bmask_s", [128, 4, 512], BF16)
        amask = p.sb(g + "amask_s", [128, 4, 512], BF16)
        identb = p.sb(g + "identb_s", [128, 128], BF16)
        identf = p.sb(g + "identf_s", [128, 128], F32)
        xnT = p.sb(g + "xnT", [128, 2, 8, 512], BF16)
        qTs = p.sb(g + "qTs", [128, S], BF16)
        kTs = p.sb(g + "kTs", [128, S], BF16)
        vtok = p.sb(g + "vtok", [128, 2, 64, 65], BF16)
        zs = p.sb(g + "zs", [65, 512], F32)
        ztok = p.sb(g + "ztok", [128, 66], F32)
        escr = p.sb(g + "escr", [128, 2, 2, 512], BF16)
        pbuf = p.sb(g + "pbuf", [128, 2, 4, 512], BF16)
        dcols = p.sb(g + "dcols", [128, 2, 17], F32)
        dlow = p.sb(g + "dlow", [128, 2, 64], F32)
        rec = p.sb(g + "rec", [128, 1], F32)

        pq = [p.ps(g + f"pq{i}", [128, 512]) for i in range(4)]
        pz = [p.ps(g + f"pz{i}", [65, 512]) for i in range(2)]
        pj = p.ps(g + "pj", [128, 512])
        pvp = pq[3][:, 0:128]
        ptr = pj[:, 0:65]

        for aa in range(4):
            p.dma(bmask[:, aa, :], bmask_in[aa])
            p.dma(amask[:, aa, :], amask_in[aa])
        p.dma(identb[:], identb_in)
        p.dma(identf[:], identf_in)
        p.op("vector", "memset", ones1[:], 1.0)
        p.op("vector", "memset", vtok[:], 0.0)

        for jp in range(2):
            jA, jB = 2 * jp, 2 * jp + 1
            for jj, pos in ((jA, 0), (jB, 1)):
                for m in range(3):
                    p.dma(wraw[:, m, :], Whead[jj, m])
                p.v("tensor_copy", wbf[:], wraw[:])
                for m in range(3):
                    for dc in range(8):
                        p.dma(wT[:, dc, m, pos * 64:(pos + 1) * 64],
                              wbf[:, m, dc * 128:(dc + 1) * 128], q="scalar", transpose=True)
                p.dma(bqt[pos * 64:(pos + 1) * 64, 0:1], bhead[jj, 0].unsqueeze(1))
                p.dma(bqt[pos * 64:(pos + 1) * 64, 1:2], bhead[jj, 1].unsqueeze(1))
                p.dma(bvr[:, pos * 64:(pos + 1) * 64], bhead[jj, 2].unsqueeze(0))
            p.op("vector", "memset", vtok[:, :, :, 64:65], 1.0)
            for ch in range(16):
                trows = slice(ch * 512, (ch + 1) * 512)
                par = ch % 2
                for dc in range(8):
                    p.dma(xnT[:, par, dc, :], xnb[trows, dc * 128:(dc + 1) * 128],
                          q="scalar", transpose=True)
                for m, dst in ((0, qTs), (1, kTs)):
                    for dc in range(8):
                        p.t("matmul", pj[:], wT[:, dc, m, :], xnT[:, par, dc, :],
                            start=(dc == 0), stop=(dc == 7))
                    p.v("tensor_scalar", dst[:, ch * 512:(ch + 1) * 512], pj[:],
                        bqt[:, m:m + 1], None, Op.add)
                for sub in range(4):
                    kt = ch * 4 + sub
                    for dc in range(8):
                        p.t("matmul", pvp, xnT[:, par, dc, sub * 128:(sub + 1) * 128],
                            wT[:, dc, 2, :], start=(dc == 0), stop=False)
                    p.t("matmul", pvp, ones1[:], bvr[:], start=False, stop=True)
                    p.v("tensor_copy", vtok[:, 0, kt, 0:64], pvp[:, 0:64])
                    p.v("tensor_copy", vtok[:, 1, kt, 0:64], pvp[:, 64:128])
            aw = []
            for qt in range(64):
                for ks in range(qt // 4 + 1):
                    aw.append((qt, ks))

            def emit_qk_A(qt, ks):
                nsup = qt // 4
                par = ks % 2
                if ks == nsup:
                    rr = qt % 4
                    p.t("matmul", pq[2 * par][:], identb[:], amask[:, rr, :], start=True, stop=False)
                    p.t("matmul", pq[2 * par + 1][:], identb[:], amask[:, rr, :], start=True, stop=False)
                    st0 = False
                else:
                    st0 = True
                p.t("matmul", pq[2 * par][:], qTs[0:64, qt * 128:(qt + 1) * 128],
                    kTs[0:64, ks * 512:(ks + 1) * 512], start=st0, stop=True,
                    tile_position=(0, 0))
                p.t("matmul", pq[2 * par + 1][:], qTs[64:128, qt * 128:(qt + 1) * 128],
                    kTs[64:128, ks * 512:(ks + 1) * 512], start=st0, stop=True,
                    tile_position=(64, 0))

            emit_qk_A(*aw[0])
            for i, (qt, ks) in enumerate(aw):
                if i + 1 < len(aw):
                    emit_qk_A(*aw[i + 1])
                par = ks % 2
                p.a("activation", escr[:, 0, ks % 2, :], pq[2 * par][:], AF.Exp, scale=SQS,
                    accum_out=dcols[:, 0, ks:ks + 1])
                p.a("activation", escr[:, 1, ks % 2, :], pq[2 * par + 1][:], AF.Exp, scale=SQS,
                    accum_out=dcols[:, 1, ks:ks + 1])
                if ks == qt // 4:
                    for hh in range(2):
                        p.v("tensor_reduce", dlow[:, hh, qt:qt + 1], dcols[:, hh, 0:qt // 4 + 1], AX, Op.add)
            for qs in range(16):
                kts = list(range(4 * qs, 64))

                def emit_kq_B(kt, qs=qs):
                    pat = kt - 4 * qs
                    par = kt % 2
                    if pat <= 3:
                        p.t("matmul", pq[2 * par][:], identb[:], bmask[:, pat, :], start=True, stop=False)
                        p.t("matmul", pq[2 * par + 1][:], identb[:], bmask[:, pat, :], start=True, stop=False)
                    p.t("matmul", pq[2 * par][:], kTs[0:64, kt * 128:(kt + 1) * 128],
                        qTs[0:64, qs * 512:(qs + 1) * 512], start=(pat > 3), stop=True,
                        tile_position=(0, 0))
                    p.t("matmul", pq[2 * par + 1][:], kTs[64:128, kt * 128:(kt + 1) * 128],
                        qTs[64:128, qs * 512:(qs + 1) * 512], start=(pat > 3), stop=True,
                        tile_position=(64, 0))

                emit_kq_B(kts[0])
                for i, kt in enumerate(kts):
                    if i + 1 < len(kts):
                        emit_kq_B(kts[i + 1])
                    par = kt % 2
                    par4 = kt % 4
                    p.a("activation", pbuf[:, 0, par4, :], pq[2 * par][:], AF.Exp, scale=SQS)
                    p.a("activation", pbuf[:, 1, par4, :], pq[2 * par + 1][:], AF.Exp, scale=SQS)
                    p.t("matmul", pz[0][:], vtok[:, 0, kt, :], pbuf[:, 0, par4, :],
                        start=(kt == 4 * qs), stop=(kt == 63))
                    p.t("matmul", pz[1][:], vtok[:, 1, kt, :], pbuf[:, 1, par4, :],
                        start=(kt == 4 * qs), stop=(kt == 63))
                for hh, jj in ((0, jA), (1, jB)):
                    p.v("tensor_copy", zs[:], pz[hh][:])
                    for q4 in range(4):
                        qt = qs * 4 + q4
                        p.t("transpose", ptr, zs[:, q4 * 128:(q4 + 1) * 128], identf[0:65, 0:65])
                        p.v("tensor_copy", ztok[:, 0:65], ptr)
                        p.v("tensor_tensor", ztok[:, 65:66], ztok[:, 64:65], dlow[:, hh, qt:qt + 1], Op.add)
                        p.v("reciprocal", rec[:], ztok[:, 65:66])
                        p.v("tensor_scalar", ztok[:, 0:64], ztok[:, 0:64], rec[:], None, Op.mult)
                        p.dma(zout[jj, qt * 128:(qt + 1) * 128, :], ztok[:, 0:64])

def build_s2():
    p = Prog()
    xnb = p.din("xnb", [S, D], BF16)
    Whead = p.din("Whead", [4, 3, 64, D], F32)
    bhead = p.din("bhead", [4, 3, 64], F32)
    bmask_in = p.din("bmask", [4, 128, 512], BF16)
    amask_in = p.din("amask", [4, 128, 512], BF16)
    identb_in = p.din("identb", [128, 128], BF16)
    identf_in = p.din("identf", [128, 128], F32)
    zout = p.dout("z4", [4, S, 64], F32)
    emit_s2(p, "s2_", xnb, Whead, bhead, bmask_in, amask_in, identb_in, identf_in, zout)
    p.finish()
    return p

def stage2(xnb_full, Wq, bq, Wk, bk, Wv, bv):
    bm, am = attn_masks()
    identb = np.eye(128).astype(BF)
    identf = np.eye(128, dtype=np.float32)
    xnb3 = xnb_full.reshape(B, S, D)
    in_maps = []
    for c in range(NCORES):
        hs = [(c * 4 + j) % 16 for j in range(4)]
        b = (c * 4) // 16
        Wh = np.stack([np.stack([W[h * 64:(h + 1) * 64] for W in (Wq, Wk, Wv)]) for h in hs])
        bh_ = np.stack([np.stack([bb[h * 64:(h + 1) * 64] for bb in (bq, bk, bv)]) for h in hs])
        in_maps.append({"xnb": np.ascontiguousarray(xnb3[b]), "Whead": np.ascontiguousarray(Wh),
                        "bhead": np.ascontiguousarray(bh_), "bmask": bm, "amask": am,
                        "identb": identb, "identf": identf})
    p = build_s2()
    res = run(p, in_maps)
    z = np.zeros((B, S, 16, 64), np.float32)
    for c in range(NCORES):
        b = (c * 4) // 16
        for j in range(4):
            h = (c * 4 + j) % 16
            z[b, :, h, :] = res[c]["z4"][j]
    return z.reshape(B, S, D)




def emit_s3(p, g, xn, z16, gam, bet, s_out, rows):
    with p.scope():
        xt = p.sb(g + "xt", [128, D], F32)
        zt = p.sb(g + "zt", [128, D], F32)
        sq = p.sb(g + "sq", [128, D], F32)
        ot = p.sb(g + "ot", [128, D], F32)
        st = p.sb(g + "st", [128, 8], F32)
        nb = p.sb(g + "nb", [128, 1], F32)
        grow = p.sb(g + "grow", [1, 2, D], F32)
        gbc = p.sb(g + "gbc", [128, 2, D], F32)
        pb = p.ps(g + "pb", [128, 512])
        ones1 = p.sb(g + "ones1", [1, 128], F32)
        p.op("vector", "memset", ones1[:], 1.0)
        p.dma(grow[:, 0, :], gam.unsqueeze(0))
        p.dma(grow[:, 1, :], bet.unsqueeze(0))
        for m in range(2):
            for hf in range(2):
                p.t("matmul", pb[:], ones1[:], grow[:, m, hf * 512:(hf + 1) * 512], start=True, stop=True)
                p.v("tensor_copy", gbc[:, m, hf * 512:(hf + 1) * 512], pb[:])
        nt = rows // 128
        for i in range(nt):
            r = slice(i * 128, (i + 1) * 128)
            p.dma(xt[:], xn[r, :])
            for h16 in range(16):
                p.dma(zt[:, h16 * 64:(h16 + 1) * 64], z16[h16, r, :])
            p.v("tensor_tensor", xt[:], xt[:], zt[:], Op.add)
            p.a("activation", sq[:], xt[:], AF.Square, accum_out=st[:, 1:2])
            p.v("tensor_reduce", st[:, 0:1], xt[:], AX, Op.add)
            p.v("tensor_scalar", st[:, 2:3], st[:, 0:1], 1.0 / D, None, Op.mult)
            p.v("tensor_scalar", st[:, 3:4], st[:, 1:2], 1.0 / D, None, Op.mult)
            p.v("tensor_tensor", st[:, 4:5], st[:, 2:3], st[:, 2:3], Op.mult)
            p.v("tensor_tensor", st[:, 5:6], st[:, 3:4], st[:, 4:5], Op.subtract)
            p.v("tensor_scalar", st[:, 5:6], st[:, 5:6], 1e-5, None, Op.add)
            p.a("activation", st[:, 6:7], st[:, 5:6], AF.Sqrt)
            p.v("reciprocal", st[:, 7:8], st[:, 6:7])
            p.v("tensor_scalar", nb[:], st[:, 2:3], -1.0, None, Op.mult)
            p.v("tensor_scalar", ot[:], xt[:], nb[:], st[:, 7:8], Op.add, Op.mult)
            p.v("tensor_tensor", ot[:], ot[:], gbc[:, 0, :], Op.mult)
            p.v("tensor_tensor", ot[:], ot[:], gbc[:, 1, :], Op.add)
            p.dma(s_out[r, :], ot[:])


def build_s3(rows):
    p = Prog()
    xn = p.din("xn", [rows, D], F32)
    zi = p.din("z", [16, rows, 64], F32)
    gam = p.din("gamma", [D], F32)
    bet = p.din("beta", [D], F32)
    s_out = p.dout("s0", [rows, D], F32)
    emit_s3(p, "s3_", xn, zi, gam, bet, s_out, rows)
    p.finish()
    return p


def stage3(xn, z, gamma, beta):
    p = build_s3(TPC)
    xs = shard_rows(xn.reshape(T, D))
    z4 = z.reshape(B, S, 16, 64)
    zsh = [np.ascontiguousarray(z4[c // 4, (c % 4) * TPC:(c % 4 + 1) * TPC].transpose(1, 0, 2))
           for c in range(NCORES)]
    res = run(p, [{"xn": a, "z": b, "gamma": gamma, "beta": beta} for a, b in zip(xs, zsh)])
    return np.concatenate([r["s0"] for r in res], 0)



def build_s4(NTT=T // 128):
    p = Prog()
    xnb = p.din("xnb", [T, D], BF16)
    xnlo = p.din("xnlo", [T, D], BF16)
    gw = p.din("gw", [64, D], F32)
    iota_tok = p.din("iota_tok", [128, 64], F32)
    idin = p.din("identf", [128, 128], F32)
    table = p.dout("table", [T, 1], U32)
    gbidx = p.dout("gbidx", [T, 4], U32)
    wts_o = p.dout("wts", [T, 4], F32)

    gwraw = p.sb("gwraw", [64, D], F32)
    gwbf = p.sb("gwbf", [64, D], BF16)
    gwlosc = p.sb("gwlosc", [64, D], F32)
    gwlo = p.sb("gwlo", [64, D], BF16)
    gwT = p.sb("gwT", [128, 8, 64], BF16)
    gwloT = p.sb("gwloT", [128, 8, 64], BF16)
    xnT = p.sb("xnT", [128, 8, 128], BF16)
    xnloT = p.sb("xnloT", [128, 8, 128], BF16)
    lwork = p.sb("lwork", [128, 64], F32)
    iot = p.sb("iot", [128, 64], F32)
    oh = p.sb("oh", [128, 64], F32)
    mx = p.sb("mx", [128, 4], F32)
    idx_all = p.sb("idx_all", [128, T // 128, 4], F32)
    ew = p.sb("ew", [128, 8], F32)
    wtt = p.sb("wtt", [128, 4], F32)
    selT_all = p.sb("selT_all", [64, T], F32)
    zerT = p.sb("zerT", [64, 2048], F32)
    posT = p.sb("posT", [64, T], F32)
    carry = p.sb("carry", [64, 1], F32)
    sel = p.sb("sel", [128, 64], F32)
    pos_tok = p.sb("pos_tok", [128, 64], F32)
    poscol = p.sb("poscol", [128, 4], F32)
    base = p.sb("base", [128, 1], F32)
    km = p.sb("km", [128, 1], F32)
    ge = p.sb("ge", [128, 1], F32)
    t1 = p.sb("t1", [128, 1], F32)
    gidxf = p.sb("gidxf", [128, 1], F32)
    gidxu = p.sb("gidxu", [128, 1], U32)
    gbu = p.sb("gbu", [128, 4], U32)
    tid = p.sb("tid", [128, 1], U32)
    identf = p.sb("identf2", [128, 128], F32)
    zl = p.sb("zl", [128, 128], U32)
    pl = p.ps("pl", [128, 64])
    pt2 = p.ps("pt2", [64, 128])
    ptp = p.ps("ptp", [128, 64])

    p.dma(identf[:], idin)
    p.dma(iot[:], iota_tok)
    p.op("vector", "memset", zerT[:], 0.0)
    p.op("vector", "memset", zl[:], 0)
    p.dma(table.rearrange("(q a) x -> q (a x)", q=128), zl[:])
    p.dma(gwraw[:], gw)
    p.v("tensor_copy", gwbf[:], gwraw[:])
    p.v("tensor_tensor", gwlosc[:], gwraw[:], gwbf[:], Op.subtract)
    p.v("tensor_copy", gwlo[:], gwlosc[:])
    for dc in range(8):
        p.dma(gwT[:, dc, :], gwbf[:, dc * 128:(dc + 1) * 128], q="scalar", transpose=True)
        p.dma(gwloT[:, dc, :], gwlo[:, dc * 128:(dc + 1) * 128], q="scalar", transpose=True)

    breg = p.nc.gpsimd.alloc_register()
    p.raw("gpsimd", lambda e: e.reg_mov(breg, T - 1))
    for tt in range(NTT):
        r = slice(tt * 128, (tt + 1) * 128)
        for dc in range(8):
            p.dma(xnT[:, dc, :], xnb[r, dc * 128:(dc + 1) * 128], q="scalar", transpose=True)
            p.dma(xnloT[:, dc, :], xnlo[r, dc * 128:(dc + 1) * 128], q="scalar", transpose=True)
        for dc in range(8):
            p.t("matmul", pl[:], xnT[:, dc, :], gwT[:, dc, :], start=(dc == 0), stop=False)
            p.t("matmul", pl[:], xnT[:, dc, :], gwloT[:, dc, :], start=False, stop=False)
            p.t("matmul", pl[:], xnloT[:, dc, :], gwT[:, dc, :], start=False, stop=(dc == 7))
        p.v("tensor_copy", lwork[:], pl[:])
        p.op("vector", "memset", sel[:], 0.0)
        for k in range(4):
            p.v("tensor_reduce", mx[:, k:k + 1], lwork[:], AX, Op.max)
            p.v("tensor_scalar", oh[:], lwork[:], mx[:, k:k + 1], None, Op.is_equal)
            p.v("scalar_tensor_tensor", lwork[:], oh[:], -1e30, lwork[:], Op.mult, Op.add)
            p.v("tensor_tensor", sel[:], sel[:], oh[:], Op.add)
            p.v("tensor_tensor", oh[:], oh[:], iot[:], Op.mult)
            p.v("tensor_reduce", idx_all[:, tt, k:k + 1], oh[:], AX, Op.add)
        p.v("tensor_scalar", ew[:, 6:7], mx[:, 0:1], -1.0, None, Op.mult)
        p.v("tensor_scalar", ew[:, 0:4], mx[:], ew[:, 6:7], None, Op.add)
        p.a("activation", ew[:, 0:4], ew[:, 0:4], AF.Exp)
        p.v("tensor_reduce", ew[:, 4:5], ew[:, 0:4], AX, Op.add)
        p.v("reciprocal", ew[:, 5:6], ew[:, 4:5])
        p.v("tensor_scalar", wtt[:], ew[:, 0:4], ew[:, 5:6], None, Op.mult)
        p.dma(wts_o[r, :], wtt[:])
        p.t("transpose", pt2[:], sel[:, 0:64], identf[:])
        p.v("tensor_copy", selT_all[:, r], pt2[:])
    p.op("vector", "memset", carry[:], 0.0)
    for ch in range(T // 2048):
        r = slice(ch * 2048, (ch + 1) * 2048)
        p.v("tensor_tensor_scan", posT[:, r], zerT[:], selT_all[:, r], carry[:], Op.add, Op.add)
        p.v("tensor_copy", carry[:], posT[:, ch * 2048 + 2047:ch * 2048 + 2048])
    for tt in range(NTT):
        r = slice(tt * 128, (tt + 1) * 128)
        p.t("transpose", ptp[:], posT[:, r], identf[0:64, 0:64])
        p.v("tensor_copy", pos_tok[:, 0:64], ptp[:])
        p.g("iota", tid[:], pattern=[[0, 1]], base=tt * 128, channel_multiplier=1)
        for k in range(4):
            p.v("tensor_scalar", oh[:], iot[:], idx_all[:, tt, k:k + 1], None, Op.is_equal)
            p.v("tensor_tensor", oh[:], oh[:], pos_tok[:, 0:64], Op.mult)
            p.v("tensor_reduce", poscol[:, k:k + 1], oh[:], AX, Op.add)
        for k in range(4):
            p.v("tensor_scalar", poscol[:, k:k + 1], poscol[:, k:k + 1], -1.0, None, Op.add)
            p.v("tensor_scalar", base[:], idx_all[:, tt, k:k + 1], 256.0, None, Op.mult)
            p.v("tensor_tensor", base[:], base[:], poscol[:, k:k + 1], Op.add)
            p.v("tensor_scalar", ge[:], poscol[:, k:k + 1], 256.0, None, Op.is_ge)
            p.v("scalar_tensor_tensor", gidxf[:], ge[:], 100000.0, base[:], Op.mult, Op.add)
            p.v("tensor_copy", gidxu[:], gidxf[:])
            p.idma("gpsimd", out=table, out_offset=IndirectOffsetOnAxis(ap=gidxu[:], axis=0),
                   in_=tid[:], in_offset=None, bounds_check=breg, oob_is_err=False)
            p.v("tensor_scalar", km[:], poscol[:, k:k + 1], 256.0, None, Op.is_lt)
            p.v("tensor_tensor", t1[:], base[:], km[:], Op.mult)
            p.v("scalar_tensor_tensor", gidxf[:], ge[:], float(T), t1[:], Op.mult, Op.add)
            p.v("tensor_copy", gbu[:, k:k + 1], gidxf[:])
        p.dma(gbidx[r, :], gbu[:])
    p.finish()
    return p

def stage4(xnb_full, xnlo_full, gate_w, ntt=T // 128):
    p = build_s4(ntt)
    iota_tok = np.broadcast_to(np.arange(64, dtype=np.float32), (128, 64)).copy()
    identf = np.eye(128, dtype=np.float32)
    im = [{"xnb": xnb_full, "xnlo": xnlo_full, "gw": gate_w, "iota_tok": iota_tok, "identf": identf}
          for _ in range(NCORES)]
    res = run(p, im)
    r = res[0]
    return r["table"][:, 0], r["gbidx"], r["wts"]





def emit_dequant(p, c_t, scexp, t0, t1, t2, dec, W):
    for nib in (0, 1):
        if nib == 0:
            p.v("tensor_scalar", t1[:, 0:W], c_t[:], 7, 22, Op.bitwise_and, Op.logical_shift_left)
            p.v("tensor_scalar", t0[:, 0:W], c_t[:], 8, 28, Op.bitwise_and, Op.logical_shift_left)
        else:
            p.v("tensor_scalar", t1[:, 0:W], c_t[:], 4, 7, Op.logical_shift_right, Op.bitwise_and)
            p.v("tensor_scalar", t1[:, 0:W], t1[:, 0:W], 22, None, Op.logical_shift_left)
            p.v("tensor_scalar", t0[:, 0:W], c_t[:], 128, 24, Op.bitwise_and, Op.logical_shift_left)
        p.v("tensor_scalar", t1[:, 0:W], t1[:, 0:W], 0x3F000000, None, Op.add)
        p.v("tensor_tensor", t0[:, 0:W], t0[:, 0:W], scexp[:, 0:W].bitcast(I32), Op.bitwise_or)
        f0 = t1[:, 0:W].bitcast(F32)
        p.v("tensor_scalar", t2[:, 0:W], f0, 1.0, -1.0, Op.min, Op.add)
        p.v("tensor_tensor", t2[:, 0:W], t2[:, 0:W], f0, Op.add)
        p.v("tensor_tensor", dec[:, nib:2 * W:2], t2[:, 0:W], t0[:, 0:W].bitcast(F32), Op.mult)

def emit_scale_expand(p, scl, scexp, G8):
    p.v("tensor_copy", scexp[:, 0:G8 * 64].rearrange("p (g k) -> p g k", k=64),
        scl[:, 0:G8].unsqueeze(2).broadcast_to([128, G8, 64]))

def emit_s5(p, g, xnb, tbl, gp, gs, up, us, dp, ds, idb_in, eo_out):
    with p.scope():
        ixt = p.sb(g + "ixt", [128, 2], U32)
        buftok = p.sb(g + "buftok", [128, 2, D], BF16)
        bufT = p.sb(g + "bufT", [128, 8, 256], BF16)
        identb = p.sb(g + "identb_s", [128, 128], BF16)
        pk = p.sb(g + "pk", [128, 512], I32)
        scl = p.sb(g + "scl", [128, 8], F32)
        scexp = p.sb(g + "scexp", [128, 512], F32)
        t0 = p.sb(g + "t0", [128, 512], I32)
        t1 = p.sb(g + "t1", [128, 512], I32)
        t2 = p.sb(g + "t2", [128, 512], F32)
        dec = p.sb(g + "dec", [128, 1024], BF16)
        wgT = p.sb(g + "wgT", [128, 8, 768], BF16)
        wuT = p.sb(g + "wuT", [128, 8, 768], BF16)
        wdT = p.sb(g + "wdT", [128, 6, 1024], BF16)
        gsl = p.sb(g + "gsl", [128, 256], F32)
        hsb = p.sb(g + "hsb", [128, 6, 256], BF16)
        eos = p.sb(g + "eos", [128, 1024], BF16)
        ptb = p.ps(g + "ptb", [128, 128], BF16)
        pg = p.ps(g + "pg", [128, 256])
        pu = p.ps(g + "pu", [128, 256])
        pdn = p.ps(g + "pdn", [128, 512])

        p.dma(identb[:], idb_in)
        for ee in range(8):
            for h in range(2):
                p.dma(ixt[:, h:h + 1], tbl[ee, h * 128:(h + 1) * 128].unsqueeze(1))
            for h in range(2):
                p.idma("gpsimd", out=buftok[:, h, :], out_offset=None, in_=xnb,
                       in_offset=IndirectOffsetOnAxis(ap=ixt[:, h:h + 1], axis=0))
            for h in range(2):
                for dc in range(8):
                    p.t("transpose", ptb[:], buftok[:, h, dc * 128:(dc + 1) * 128], identb[:])
                    p.v("tensor_copy", bufT[:, dc, h * 128:(h + 1) * 128], ptb[:])
            for (pkd, sdd, wdst) in ((gp, gs, wgT), (up, us, wuT)):
                for ic in range(6):
                    p.dma(pk[:], pkd[ee, ic * 128:(ic + 1) * 128, :])
                    p.dma(scl[:], sdd[ee, ic * 128:(ic + 1) * 128, :])
                    emit_scale_expand(p, scl, scexp, 8)
                    emit_dequant(p, pk[:], scexp, t0, t1, t2, dec, 512)
                    for dc in range(8):
                        p.t("transpose", ptb[:], dec[:, dc * 128:(dc + 1) * 128], identb[:])
                        p.v("tensor_copy", wdst[:, dc, ic * 128:(ic + 1) * 128], ptb[:])
            for dc in range(8):
                p.dma(pk[:, 0:384], dp[ee, dc * 128:(dc + 1) * 128, :])
                p.dma(scl[:, 0:6], ds[ee, dc * 128:(dc + 1) * 128, :])
                emit_scale_expand(p, scl, scexp, 6)
                emit_dequant(p, pk[:, 0:384], scexp, t0, t1, t2, dec, 384)
                for ic in range(6):
                    p.t("transpose", ptb[:], dec[:, ic * 128:(ic + 1) * 128], identb[:])
                    p.v("tensor_copy", wdT[:, ic, dc * 128:(dc + 1) * 128], ptb[:])
            for ic in range(6):
                for dc in range(8):
                    p.t("matmul", pg[:], wgT[:, dc, ic * 128:(ic + 1) * 128], bufT[:, dc, :],
                        start=(dc == 0), stop=(dc == 7))
                for dc in range(8):
                    p.t("matmul", pu[:], wuT[:, dc, ic * 128:(ic + 1) * 128], bufT[:, dc, :],
                        start=(dc == 0), stop=(dc == 7))
                p.a("activation", gsl[:], pg[:], AF.Silu)
                p.v("tensor_tensor", hsb[:, ic, :], gsl[:], pu[:], Op.mult)
            for c2 in range(2):
                for dh in range(2):
                    for ic in range(6):
                        p.t("matmul", pdn[:], hsb[:, ic, c2 * 128:(c2 + 1) * 128],
                            wdT[:, ic, dh * 512:(dh + 1) * 512], start=(ic == 0), stop=(ic == 5))
                    p.v("tensor_copy", eos[:, dh * 512:(dh + 1) * 512], pdn[:])
                p.dma(eo_out[ee, c2 * 128:(c2 + 1) * 128, :], eos[:])


def build_s5():
    p = Prog()
    xnb = p.din("xnb", [T, D], BF16)
    tbl = p.din("tbl", [8, 256], U32)
    gp = p.din("gp", [8, 768, 512], I32)
    gs = p.din("gs", [8, 768, 8], F32)
    up = p.din("up", [8, 768, 512], I32)
    us = p.din("us", [8, 768, 8], F32)
    dp = p.din("dp", [8, 1024, 384], I32)
    ds = p.din("ds", [8, 1024, 6], F32)
    idb_in = p.din("identb", [128, 128], BF16)
    eo_out = p.dout("eo", [8, 256, D], BF16)
    emit_s5(p, "s5_", xnb, tbl, gp, gs, up, us, dp, ds, idb_in, eo_out)
    p.finish()
    return p


def stage5(xnb_full, table, packs):
    gp, gs_, up, us, dp, ds = packs
    p = build_s5()
    idb = np.eye(128).astype(BF)
    t64 = table.reshape(64, 256).astype(np.uint32)
    im = []
    for c in range(NCORES):
        sl = slice(c * 8, (c + 1) * 8)
        im.append({"xnb": xnb_full, "tbl": np.ascontiguousarray(t64[sl]),
                   "gp": np.ascontiguousarray(gp[sl]), "gs": np.ascontiguousarray(gs_[sl]),
                   "up": np.ascontiguousarray(up[sl]), "us": np.ascontiguousarray(us[sl]),
                   "dp": np.ascontiguousarray(dp[sl]), "ds": np.ascontiguousarray(ds[sl]),
                   "identb": idb})
    res = run(p, im)
    eo = np.concatenate([r["eo"] for r in res], 0)
    return eo





def emit_s6(p, g, eoall, gbix, wtsi, xi, s0i, xnbr, sgp, sgs, sup, sus, sdp, sds, idb_in, x1_out, rows):
    with p.scope():
        identb = p.sb(g + "identb_s", [128, 128], BF16)
        pk = p.sb(g + "pk", [128, 512], I32)
        scl = p.sb(g + "scl", [128, 8], F32)
        scexp = p.sb(g + "scexp", [128, 512], F32)
        t0 = p.sb(g + "t0", [128, 512], I32)
        t1 = p.sb(g + "t1", [128, 512], I32)
        t2 = p.sb(g + "t2", [128, 512], F32)
        dec = p.sb(g + "dec", [128, 1024], BF16)
        wsgT = p.sb(g + "wsgT", [128, 8, 768], BF16)
        wsuT = p.sb(g + "wsuT", [128, 8, 768], BF16)
        wsdT = p.sb(g + "wsdT", [128, 6, 1024], BF16)
        xnT = p.sb(g + "xnT", [128, 8, 512], BF16)
        gsl = p.sb(g + "gsl", [128, 512], F32)
        hshT = p.sb(g + "hshT", [128, 6, 512], BF16)
        shseg = p.sb(g + "shseg", [128, 512], BF16)
        sh_tok = p.sb(g + "sh_tok", [128, 4, D], BF16)
        acc = p.sb(g + "acc", [128, D], F32)
        gbuf = p.sb(g + "gbuf", [128, D], BF16)
        ixc = p.sb(g + "ixc", [128, 4], U32)
        wtc = p.sb(g + "wtc", [128, 4], F32)
        g0t = p.sb(g + "g0t", [128, D], F32)
        xt = p.sb(g + "xt", [128, D], F32)
        s0t = p.sb(g + "s0t", [128, D], F32)
        ptb = p.ps(g + "ptb", [128, 128], BF16)
        pg = p.ps(g + "pg", [128, 512])
        pu = p.ps(g + "pu", [128, 512])
        pdn = p.ps(g + "pdn", [128, 512])

        p.dma(identb[:], idb_in)
        for (pkd, sdd, wdst) in ((sgp, sgs, wsgT), (sup, sus, wsuT)):
            for ic in range(6):
                p.dma(pk[:], pkd[ic * 128:(ic + 1) * 128, :])
                p.dma(scl[:], sdd[ic * 128:(ic + 1) * 128, :])
                emit_scale_expand(p, scl, scexp, 8)
                emit_dequant(p, pk[:], scexp, t0, t1, t2, dec, 512)
                for dc in range(8):
                    p.t("transpose", ptb[:], dec[:, dc * 128:(dc + 1) * 128], identb[:])
                    p.v("tensor_copy", wdst[:, dc, ic * 128:(ic + 1) * 128], ptb[:])
        for dc in range(8):
            p.dma(pk[:, 0:384], sdp[dc * 128:(dc + 1) * 128, :])
            p.dma(scl[:, 0:6], sds[dc * 128:(dc + 1) * 128, :])
            emit_scale_expand(p, scl, scexp, 6)
            emit_dequant(p, pk[:, 0:384], scexp, t0, t1, t2, dec, 384)
            for ic in range(6):
                p.t("transpose", ptb[:], dec[:, ic * 128:(ic + 1) * 128], identb[:])
                p.v("tensor_copy", wsdT[:, ic, dc * 128:(dc + 1) * 128], ptb[:])
        nch = rows // 512
        for ch in range(nch):
            trows = slice(ch * 512, (ch + 1) * 512)
            for dc in range(8):
                p.dma(xnT[:, dc, :], xnbr[trows, dc * 128:(dc + 1) * 128], q="scalar", transpose=True)
            for ic in range(6):
                for dc in range(8):
                    p.t("matmul", pg[:], wsgT[:, dc, ic * 128:(ic + 1) * 128], xnT[:, dc, :],
                        start=(dc == 0), stop=(dc == 7))
                for dc in range(8):
                    p.t("matmul", pu[:], wsuT[:, dc, ic * 128:(ic + 1) * 128], xnT[:, dc, :],
                        start=(dc == 0), stop=(dc == 7))
                p.a("activation", gsl[:], pg[:], AF.Silu)
                p.v("tensor_tensor", hshT[:, ic, :], gsl[:], pu[:], Op.mult)
            for dc8 in range(8):
                for ic in range(6):
                    p.t("matmul", pdn[:], wsdT[:, ic, dc8 * 128:(dc8 + 1) * 128], hshT[:, ic, :],
                        start=(ic == 0), stop=(ic == 5))
                p.v("tensor_copy", shseg[:], pdn[:])
                for sub in range(4):
                    p.t("transpose", ptb[:], shseg[:, sub * 128:(sub + 1) * 128], identb[:])
                    p.v("tensor_copy", sh_tok[:, sub, dc8 * 128:(dc8 + 1) * 128], ptb[:])
            for sub in range(4):
                tt = ch * 4 + sub
                r = slice(tt * 128, (tt + 1) * 128)
                p.v("tensor_copy", acc[:], sh_tok[:, sub, :])
                p.dma(ixc[:], gbix[r, :])
                p.dma(wtc[:], wtsi[r, :])
                for k in range(4):
                    p.idma("gpsimd", out=gbuf[:], out_offset=None, in_=eoall,
                           in_offset=IndirectOffsetOnAxis(ap=ixc[:, k:k + 1], axis=0))
                    p.v("scalar_tensor_tensor", acc[:], gbuf[:], wtc[:, k:k + 1], acc[:], Op.mult, Op.add)
                p.a("activation", g0t[:], acc[:], AF.Silu)
                p.dma(xt[:], xi[r, :])
                p.dma(s0t[:], s0i[r, :])
                p.v("tensor_tensor", g0t[:], g0t[:], s0t[:], Op.mult)
                p.v("tensor_tensor", g0t[:], g0t[:], xt[:], Op.add)
                p.dma(x1_out[r, :], g0t[:])


def build_s6(rows):
    p = Prog()
    eoall = p.din("eoall", [T + 1, D], BF16)
    gbix = p.din("gbix", [rows, 4], U32)
    wtsi = p.din("wtsi", [rows, 4], F32)
    xi = p.din("x", [rows, D], F32)
    s0i = p.din("s0", [rows, D], F32)
    xnbr = p.din("xnbr", [rows, D], BF16)
    sgp = p.din("sgp", [768, 512], I32)
    sgs = p.din("sgs", [768, 8], F32)
    sup = p.din("sup", [768, 512], I32)
    sus = p.din("sus", [768, 8], F32)
    sdp = p.din("sdp", [1024, 384], I32)
    sds = p.din("sds", [1024, 6], F32)
    idb_in = p.din("identb", [128, 128], BF16)
    x1_out = p.dout("x1", [rows, D], F32)
    emit_s6(p, "s6_", eoall, gbix, wtsi, xi, s0i, xnbr, sgp, sgs, sup, sus, sdp, sds, idb_in, x1_out, rows)
    p.finish()
    return p


def stage6(eo, gbidx, wts, x, s0, xnb, shpacks):
    sgp, sgs, sup, sus, sdp, sds = shpacks
    eoall = np.concatenate([eo.reshape(T, D), np.zeros((1, D), eo.dtype)], 0)
    p = build_s6(TPC)
    idb = np.eye(128).astype(BF)
    gsh = shard_rows(gbidx); wsh = shard_rows(wts)
    xsh = shard_rows(x.reshape(T, D)); ssh = shard_rows(s0.reshape(T, D)); bsh = shard_rows(xnb.reshape(T, D))
    im = [{"eoall": eoall, "gbix": gsh[c], "wtsi": wsh[c], "x": xsh[c], "s0": ssh[c],
           "xnbr": bsh[c], "sgp": sgp, "sgs": sgs, "sup": sup, "sus": sus,
           "sdp": sdp, "sds": sds, "identb": idb} for c in range(NCORES)]
    res = run(p, im)
    return np.concatenate([r["x1"] for r in res], 0)





def emit_s8(p, g, xb, Wh, bac, adt, lnw_in, onw_in, msu_in, miu_in, idf_in, idb_in, og_out):
    with p.scope():
        NC = S // 128
        identf = p.sb(g + "identf_s", [128, 128], F32)
        identb = p.sb(g + "identb_s", [128, 128], BF16)
        msu = p.sb(g + "msu_s", [128, 128], F32)
        miu = p.sb(g + "miu_s", [128, 128], F32)
        ones1 = p.sb(g + "ones1", [1, 128], F32)
        onwB = p.sb(g + "onwB", [128, 128], F32)
        lnwt = p.sb(g + "lnwt", [128, 8], F32)
        pmm2 = [[p.ps(g + f"pmm{j}_{cp}", [128, 128]) for cp in range(2)] for j in range(2)]
        pmm = [pmm2[0][0], pmm2[1][0]]
        pA = [p.ps(g + f"pA{j}", [128, 128]) for j in range(2)]
        pB = pA
        pxs = [p.ps(g + f"px{j}", [128, 256]) for j in range(2)]
        ppool = pxs
        pr1 = pmm[0][0:1, 0:128]

        p.dma(identf[:], idf_in)
        p.dma(identb[:], idb_in)
        p.dma(msu[:], msu_in)
        p.dma(miu[:], miu_in)
        p.op("vector", "memset", ones1[:], 1.0)
        for c8 in range(8):
            p.dma(lnwt[:, c8:c8 + 1], lnw_in[c8 * 128:(c8 + 1) * 128].unsqueeze(1))
        p.dma(onwB[0:1, :], onw_in)
        p.t("matmul", pmm[0][:], ones1[:], onwB[0:1, :], start=True, stop=True)
        p.v("tensor_copy", onwB[:], pmm[0][:])

        J = []
        for j in range(2):
            d = {}
            sb = lambda nm, shp, dt: p.sb(g + nm + str(j), shp, dt)
            d["nAcol"] = sb("nAcol", [128, 1], F32)
            d["dtbcol"] = sb("dtbcol", [128, 1], F32)
            d["rawq"] = sb("rawq", [128, NC, 128], BF16)
            d["rawk"] = sb("rawk", [128, NC, 128], BF16)
            d["rawv"] = sb("rawv", [128, NC, 128], BF16)
            d["rawg"] = sb("rawg", [128, NC, 128], BF16)
            d["q_tok"] = d["rawq"]
            d["k_tok"] = d["rawk"]
            d["v_tok"] = d["rawv"]
            d["g_tok"] = d["rawg"]
            d["sg"] = sb("sg", [128, 128], F32)
            d["bt_all"] = sb("bt_all", [128, NC], F32)
            d["st8"] = sb("st8", [128, 8], F32)
            d["sx"] = sb("sx", [128, 1], F32)
            d["G_tok"] = sb("G_tok", [128, NC], F32)
            d["carry"] = sb("carry", [1, NC + 1], F32)
            d["growt"] = sb("growt", [1, 128], F32)
            for nm, shp in (("Uf", [128, 128]), ("S0f", [128, 128]), ("S0s", [128, 128])):
                d[nm] = sb(nm, shp, F32)
            d["ogt2"] = sb("ogt2", [128, 2, 128], BF16)
            for nm, shp in (("grel", [1, 128]), ("gpn", [1, 1]), ("gcoln", [128, 1]),
                            ("gamr", [1, 128]), ("gamc", [128, 1]), ("decr", [1, 128]),
                            ("decc", [128, 1]), ("gCc", [128, 1])):
                d[nm] = sb(nm, shp, F32)
            d["px"] = pxs[j]
            d["otokR"] = sb("otokR", [128, 2, 128], F32)
            d["tb1"] = sb("tb1", [128, 2, 256], BF16)
            d["qf32"] = sb("qf32", [128, 128], F32)
            J.append(d)

        def emit_sectionA():
            for j, d in enumerate(J):
                for m in range(4):
                    for dc in range(8):
                        p.dma(d["sg"][:], Wh[j, m, dc * 128:(dc + 1) * 128, :], q=("sync", "scalar")[dc % 2])
                        p.v("tensor_scalar", d["sg"][:], d["sg"][:], lnwt[:, dc:dc + 1], None, Op.mult)
                        p.v("tensor_copy", d["wcb"][:, dc, m, :], d["sg"][:])
                for dc in range(8):
                    p.dma(d["sg"][:, 0:2], bac[j, dc * 128:(dc + 1) * 128, :])
                    p.v("tensor_scalar", d["sg"][:, 0:2], d["sg"][:, 0:2], lnwt[:, dc:dc + 1], None, Op.mult)
                    p.v("tensor_copy", d["bab"][:, dc, :], d["sg"][:, 0:2])
                p.dma(d["adts"][:], adt[j].unsqueeze(0))
                p.a("activation", d["asc"][:, 0:1], d["adts"][:, 0:1], AF.Exp)
                p.v("tensor_scalar", d["asc"][:, 1:2], d["asc"][:, 0:1], -1.0, None, Op.mult)
                p.t("matmul", pmm[j][0:128, 0:1], ones1[:], d["asc"][:, 1:2], start=True, stop=True)
                p.v("tensor_copy", d["nAcol"][:], pmm[j][0:128, 0:1])
                p.t("matmul", pmm[j][0:128, 0:1], ones1[:], d["adts"][:, 1:2], start=True, stop=True)
                p.v("tensor_copy", d["dtbcol"][:], pmm[j][0:128, 0:1])

            for tt in range(NC):
                for j, d in enumerate(J):
                    r = slice(tt * 128, (tt + 1) * 128)
                    par = tt % 2
                    p.dma(d["xrow"][:, 0, :], xb[r, :], q=("sync", "scalar")[j])
                    p.a("activation", sqshared_s8[:], d["xrow"][:, 0, :], AF.Square, accum_out=d["st8"][:, 0:1])
                    p.v("tensor_scalar", d["st8"][:, 1:2], d["st8"][:, 0:1], 1.0 / D, 1e-6, Op.mult, Op.add)
                    p.a("activation", d["st8"][:, 2:3], d["st8"][:, 1:2], AF.Ln)
                    p.a("activation", d["rstd_all"][:, tt:tt + 1], d["st8"][:, 2:3], AF.Exp, scale=-0.5)
                    for dc in range(8):
                        p.dma(d["xnTc"][:, par, dc, :], xb[r, dc * 128:(dc + 1) * 128],
                              q=("sync", "scalar")[(dc + j) % 2], transpose=True)
                    for m, dst in ((0, "rawq"), (1, "rawk"), (2, "rawv"), (3, "rawg")):
                        for dc in range(8):
                            p.t("matmul", ppool[j][:, 0:128], d["xnTc"][:, par, dc, :], d["wcb"][:, dc, m, :],
                                start=(dc == 0), stop=(dc == 7))
                        p.v("tensor_scalar", d[dst][:, tt, :], ppool[j][:, 0:128],
                            d["rstd_all"][:, tt:tt + 1], None, Op.mult)
                    for dc in range(8):
                        p.t("matmul", ppool[j][:, 128:130], d["xnTc"][:, par, dc, :], d["bab"][:, dc, :],
                            start=(dc == 0), stop=(dc == 7))
                    p.v("tensor_scalar", d["bt_all"][:, tt:tt + 1], ppool[j][:, 128:129],
                        d["rstd_all"][:, tt:tt + 1], None, Op.mult)
                    p.v("tensor_scalar", d["arw"][:, tt:tt + 1], ppool[j][:, 129:130],
                        d["rstd_all"][:, tt:tt + 1], None, Op.mult)
            for tt in range(NC):
                for j, d in enumerate(J):
                    for src_ in ("rawq", "rawk", "rawv", "rawg"):
                        p.a("activation", d["sg"][:], d[src_][:, tt, :], AF.Sigmoid)
                        p.v("tensor_tensor", d[src_][:, tt, :], d[src_][:, tt, :], d["sg"][:], Op.mult)
                    p.a("activation", d["bt_all"][:, tt:tt + 1], d["bt_all"][:, tt:tt + 1], AF.Sigmoid)
            for tt in range(NC):
                for j, d in enumerate(J):
                    for nm in ("q_tok", "k_tok"):
                        p.a("activation", d["sg"][:], d[nm][:, tt, :], AF.Square, accum_out=d["st8"][:, 4:5])
                        p.v("tensor_scalar", d["st8"][:, 5:6], d["st8"][:, 4:5], 1e-6, None, Op.add)
                        p.a("activation", d["st8"][:, 6:7], d["st8"][:, 5:6], AF.Ln)
                        p.a("activation", d["st8"][:, 7:8], d["st8"][:, 6:7], AF.Exp, scale=-0.5)
                        p.v("tensor_scalar", d[nm][:, tt, :], d[nm][:, tt, :], d["st8"][:, 7:8], None, Op.mult)
                    p.v("tensor_scalar", d["sx"][:], d["arw"][:, tt:tt + 1], d["dtbcol"][:], None, Op.add)
                    p.a("activation", d["st8"][:, 4:5], d["sx"][:], AF.Abs)
                    p.a("activation", d["st8"][:, 5:6], d["st8"][:, 4:5], AF.Exp, scale=-1.0)
                    p.a("activation", d["st8"][:, 6:7], d["st8"][:, 5:6], AF.Ln, bias=1.0)
                    p.a("activation", d["st8"][:, 7:8], d["sx"][:], AF.Relu)
                    p.v("tensor_tensor", d["sx"][:], d["st8"][:, 6:7], d["st8"][:, 7:8], Op.add)
                    p.v("tensor_scalar", d["sx"][:], d["sx"][:], d["nAcol"][:], None, Op.mult)
                    p.t("transpose", pr1, d["sx"][:], identf[:])
                    if tt == 0:
                        p.op("vector", "memset", d["carry"][:, 0:1], 0.0)
                    p.v("tensor_tensor_scan", d["growt"][:], d["carry"][:, 0:1].broadcast_to([1, 128]) if False else d["growt"][:],
                        pr1, d["carry"][:, tt:tt + 1], Op.add, Op.add) if False else None
                    p.op("vector", "memset", d["sg"][0:1, 0:128].bitcast(F32), 0.0)
                    p.v("tensor_tensor_scan", d["growt"][:], d["sg"][0:1, 0:128].bitcast(F32), pr1,
                        d["carry"][:, tt:tt + 1], Op.add, Op.add)
                    p.v("tensor_copy", d["carry"][:, tt + 1:tt + 2], d["growt"][:, 127:128])
                    p.t("transpose", pmm[1][0:128, 0:1], d["growt"][:], identf[0:1, 0:1])
                    p.v("tensor_copy", d["G_tok"][:, tt:tt + 1], pmm[1][0:128, 0:1])
            for j, d in enumerate(J):
                p.op("vector", "memset", d["S0f"][:], 0.0)

        def emit_sectionB():
            def emit_B1(c, j):
                d = J[j]
                s8r = c % 4
                cp = c % 2
                PM = pmm2[j][cp]
                KT, QTb = d["KTb"][:, cp, :], d["QTb"][:, cp, :]
                QT = d["QT"][:, cp, :]
                KbT, ktf, vtf, kbf = d["KbTb"][:, cp, :], d["ktf"][:, cp, :], d["vtf"][:, cp, :], d["kbfb"][:, cp, :]
                Gdiff, expSU, expIU = d["Gdiff"][:, cp, :], d["expSU"][:, cp, :], d["expIU"][:, cp, :]
                LtN, Lcur, LcurT = d["LtNb"][:, cp, :], d["Lcurb"][:, cp, :], d["LcurTb"][:, cp, :]
                rhs0, Xf = d["rhs0"][:, cp, :], d["Xfb"][:, cp, :]
                tb1 = d["tb1"][:, cp, :]
                grel, gamr, decr = d["grel2"][0:1, cp, :], d["gamr2"][0:1, cp, :], d["decr2"][0:1, cp, :]
                gpn = d["gpn2"][0:1, cp, :]
                gcoln, gamc, decc = d["gcoln2"][:, cp, :], d["gamc2"][:, cp, :], d["decc2"][:, cp, :]
                gamB = d["gamB"][:, cp, :]
                PMc = PM[0:128, 0:1]
                PMr = PM[0:1, 0:128]
                p.v("tensor_copy", ktf, d["k_tok"][:, c, :])
                p.v("tensor_copy", vtf, d["v_tok"][:, c, :])
                p.t("matmul", PM[:], d["k_tok"][:, c, :], identb[:], start=True, stop=True)
                p.v("tensor_copy", KT, PM[:])
                p.t("matmul", PM[:], d["q_tok"][:, c, :], identb[:], start=True, stop=True)
                p.v("tensor_copy", QT, PM[:])
                p.v("tensor_copy", QTb, PM[:])
                p.v("tensor_scalar", gpn, d["carry"][:, c:c + 1], -1.0, None, Op.mult)
                p.t("matmul", PMc, ones1[:], gpn, start=True, stop=True)
                p.v("tensor_copy", gamB, PMc)
                p.v("tensor_scalar", gcoln, d["G_tok"][:, c:c + 1], gamB, None, Op.add)
                p.a("activation", gamc, gcoln, AF.Exp)
                p.t("transpose", PMr, gcoln, identf[:])
                p.v("tensor_copy", grel, PMr)
                p.a("activation", gamr, grel, AF.Exp)
                p.v("tensor_scalar", gcoln, gcoln, -1.0, None, Op.mult)
                p.t("matmul", PM[:], ones1[:], grel, start=True, stop=True)
                p.v("tensor_scalar", Gdiff, PM[:], gcoln, None, Op.add)
                p.v("tensor_tensor", expSU, Gdiff, msu[:], Op.add)
                p.a("activation", expSU, expSU, AF.Exp)
                p.v("tensor_tensor", expIU, Gdiff, miu[:], Op.add)
                p.a("activation", expIU, expIU, AF.Exp)
                p.v("tensor_scalar", kbf, ktf, d["bt_all"][:, c:c + 1], None, Op.mult)
                p.t("matmul", PM[:], kbf, identb[:], start=True, stop=True)
                p.v("tensor_copy", KbT, PM[:])
                p.t("matmul", PM[:], KT, KbT, start=True, stop=True)
                p.v("tensor_tensor", Lcur, expSU, PM[:], Op.mult)
                p.v("tensor_scalar", LtN, Lcur, -1.0, None, Op.mult)
                p.t("matmul", PM[:], KT, QTb, start=True, stop=True)
                p.v("tensor_tensor", d["PqtR"][:, s8r, :], expIU, PM[:], Op.mult)
                p.v("tensor_scalar", rhs0[:, 0:128], ktf, d["bt_all"][:, c:c + 1],
                    gamc, Op.mult, Op.mult)
                p.v("tensor_scalar", rhs0[:, 128:256], vtf, d["bt_all"][:, c:c + 1], None, Op.mult)
                p.v("tensor_copy", Xf, rhs0)
                p.t("matmul", d["px"][:], LtN, Xf, start=True, stop=True)
                p.a("activation", tb1, d["px"][:], AF.Copy)
                p.v("tensor_tensor", Xf, Xf, tb1, Op.add)
                p.t("matmul", d["px"][:, 0:128], Lcur, identb[:], start=True, stop=True)
                p.v("tensor_copy", LcurT, d["px"][:, 0:128])
                for lev in range(6):
                    p.t("matmul", PM[:], LcurT, Lcur, start=True, stop=True)
                    p.t("matmul", d["px"][:, 0:128], Lcur, LcurT, start=True, stop=True)
                    p.v("tensor_copy", Lcur, PM[:])
                    p.v("tensor_copy", LcurT, d["px"][:, 0:128])
                    p.t("matmul", d["px"][:], LcurT, Xf, start=True, stop=True)
                    p.a("activation", tb1, d["px"][:], AF.Copy)
                    p.v("tensor_tensor", Xf, Xf, tb1, Op.add)
                p.v("tensor_copy", d["ZR"][:, s8r, :], Xf[:, 128:256])
                p.t("matmul", d["px"][:, 0:128], Xf[:, 0:128], identb[:], start=True, stop=True)
                p.a("activation", d["WtnR"][:, s8r, :], d["px"][:, 0:128], AF.Copy, scale=-1.0)
                p.t("matmul", PM[:], ones1[:], gamr, start=True, stop=True)
                p.v("tensor_tensor", d["QTgR"][:, s8r, :], QT, PM[:], Op.mult)
                p.v("tensor_scalar", decr, grel, -1.0, grel[0:1, 127:128], Op.mult, Op.add)
                p.a("activation", decr, decr, AF.Exp)
                p.t("transpose", PMc, decr, identf[0:1, 0:1])
                p.v("tensor_copy", decc, PMc)
                p.t("matmul", PMc, ones1[:], gamr[0:1, 127:128], start=True, stop=True)
                p.v("tensor_copy", d["gCcR"][:, s8r:s8r + 1], PMc)
                p.v("tensor_scalar", d["kdecR"][:, s8r, :], ktf, decc, None, Op.mult)

            def emit_B2(c, j):
                d = J[j]
                s8r = c % 4
                opar = c % 2
                r = slice(c * 128, (c + 1) * 128)
                p.t("matmul", pA[j][:], d["WtnR"][:, s8r, :], d["S0f"][:], start=True, stop=False)
                p.t("matmul", pA[j][:], identf[:], d["ZR"][:, s8r, :], start=False, stop=True)
                p.a("activation", d["Uf"][:], pA[j][:], AF.Copy)
                p.t("matmul", pA[j][:], d["QTgR"][:, s8r, :], d["S0f"][:], start=True, stop=False)
                p.t("matmul", pA[j][:], d["PqtR"][:, s8r, :], d["Uf"][:], start=False, stop=True)
                p.a("activation", d["otokR"][:, opar, :], pA[j][:], AF.Copy)
                p.a("activation", d["S0s"][:], d["S0f"][:], AF.Copy, scale=d["gCcR"][:, s8r:s8r + 1])
                p.t("matmul", pA[j][:], d["kdecR"][:, s8r, :], d["Uf"][:], start=True, stop=False)
                p.t("matmul", pA[j][:], identf[:], d["S0s"][:], start=False, stop=True)
                p.a("activation", d["S0f"][:], pA[j][:], AF.Copy)
                p.a("activation", d["sg"][:], d["otokR"][:, opar, :], AF.Square, accum_out=d["st8"][:, 0:1])
                p.g("tensor_scalar", d["st8"][:, 1:2], d["st8"][:, 0:1], 1.0 / 128.0, 1e-6, Op.mult, Op.add)
                p.a("activation", d["st8"][:, 2:3], d["st8"][:, 1:2], AF.Ln)
                p.a("activation", d["st8"][:, 3:4], d["st8"][:, 2:3], AF.Exp, scale=-0.5)
                p.g("tensor_scalar", d["otokR"][:, opar, :], d["otokR"][:, opar, :], d["st8"][:, 3:4], None, Op.mult)
                p.g("tensor_tensor", d["otokR"][:, opar, :], d["otokR"][:, opar, :], onwB[:], Op.mult)
                p.g("tensor_tensor", d["ogt2"][:, opar, :], d["otokR"][:, opar, :], d["g_tok"][:, c, :], Op.mult)
                p.dma(og_out[j, r, :], d["ogt2"][:, opar, :], q=("sync", "scalar")[j])

            LOOKAHEAD = 3
            for c in range(NC + LOOKAHEAD):
                if c < NC:
                    for j in range(2):
                        emit_B1(c, j)
                if c >= LOOKAHEAD:
                    for j in range(2):
                        emit_B2(c - LOOKAHEAD, j)


        with p.scope():
            for j, d in enumerate(J):
                d["wcb"] = p.sb(g + "wcb" + str(j), [128, 8, 4, 128], BF16)
                d["bab"] = p.sb(g + "bab" + str(j), [128, 8, 2], BF16)
                d["adts"] = p.sb(g + "adts" + str(j), [1, 2], F32)
                d["asc"] = p.sb(g + "asc" + str(j), [1, 4], F32)
                d["xnTc"] = p.sb(g + "xnTc" + str(j), [128, 2, 8, 128], BF16)
                d["xrow"] = p.sb(g + "xrow" + str(j), [128, 1, D], BF16)
                d["arw"] = p.sb(g + "arw" + str(j), [128, NC], F32)
                d["rstd_all"] = p.sb(g + "rstd_all" + str(j), [128, NC], F32)
            global sqshared_s8
            sqshared_s8 = p.sb(g + "sqshared", [128, D], BF16)
            emit_sectionA()
        for j, d in enumerate(J):
            for nm, shp in (("QT", [128, 2, 128]),
                            ("ktf", [128, 2, 128]), ("vtf", [128, 2, 128]),
                            ("Gdiff", [128, 2, 128]), ("expSU", [128, 2, 128]), ("expIU", [128, 2, 128]),
                            ("rhs0", [128, 2, 256]), ("gamB", [128, 2, 1])):
                d[nm] = p.sb(g + nm + str(j), shp, F32)
            for nm, shp in (("KTb", [128, 2, 128]), ("QTb", [128, 2, 128]), ("KbTb", [128, 2, 128]),
                            ("kbfb", [128, 2, 128]), ("LtNb", [128, 2, 128]), ("Lcurb", [128, 2, 128]),
                            ("LcurTb", [128, 2, 128]), ("Xfb", [128, 2, 256])):
                d[nm] = p.sb(g + nm + str(j), shp, BF16)
            for nm in ("grel", "gamr", "decr"):
                d[nm + "2"] = p.sb(g + nm + "2" + str(j), [1, 2, 128], F32)
            d["gpn2"] = p.sb(g + "gpn2" + str(j), [1, 2, 1], F32)
            for nm in ("gcoln2", "gamc2", "decc2"):
                d[nm] = p.sb(g + nm + str(j), [128, 2, 1], F32)
            d["WtnR"] = p.sb(g + "WtnR" + str(j), [128, 4, 128], F32)
            d["ZR"] = p.sb(g + "ZR" + str(j), [128, 4, 128], F32)
            d["PqtR"] = p.sb(g + "PqtR" + str(j), [128, 4, 128], F32)
            d["QTgR"] = p.sb(g + "QTgR" + str(j), [128, 4, 128], F32)
            d["kdecR"] = p.sb(g + "kdecR" + str(j), [128, 4, 128], F32)
            d["gCcR"] = p.sb(g + "gCcR" + str(j), [128, 4], F32)
        emit_sectionB()

def build_s8():
    p = Prog()
    xb = p.din("xb", [S, D], BF16)
    Wh = p.din("Wh", [2, 4, D, 128], F32)
    bac = p.din("bac", [2, D, 2], F32)
    adt = p.din("adt", [2, 2], F32)
    lnw_in = p.din("lnw", [D], F32)
    onw_in = p.din("onw", [1, 128], F32)
    msu_in = p.din("msu", [128, 128], F32)
    miu_in = p.din("miu", [128, 128], F32)
    idf_in = p.din("identf", [128, 128], F32)
    idb_in = p.din("identb", [128, 128], BF16)
    og_out = p.dout("og", [2, S, 128], BF16)
    emit_s8(p, "s8_", xb, Wh, bac, adt, lnw_in, onw_in, msu_in, miu_in, idf_in, idb_in, og_out)
    p.finish()
    return p


def stage8(xn2b_full, Wq1, Wk1, Wv1, Wg1, bw, aw, A_log, dt_bias, lnw, onw):
    x3 = xn2b_full.reshape(B, S, D)
    msu = np.where(np.arange(128)[:, None] < np.arange(128)[None, :], 0.0, NEG).astype(np.float32)
    miu = np.where(np.arange(128)[:, None] <= np.arange(128)[None, :], 0.0, NEG).astype(np.float32)
    identf = np.eye(128, dtype=np.float32)
    idb = np.eye(128).astype(BF)
    p = build_s8()
    im = []
    for c in range(NCORES):
        bhs = [2 * c, 2 * c + 1]
        b = bhs[0] // 8
        Whl, bacl, adtl = [], [], []
        for bh in bhs:
            h = bh % 8
            cs = slice(h * 128, (h + 1) * 128)
            Whl.append(np.stack([Wq1[:, cs], Wk1[:, cs], Wv1[:, cs], Wg1[:, cs]]))
            bacl.append(np.stack([bw[:, h], aw[:, h]], 1))
            adtl.append(np.array([A_log[h], dt_bias[h]], np.float32))
        im.append({"xb": np.ascontiguousarray(x3[b]), "Wh": np.ascontiguousarray(np.stack(Whl)),
                   "bac": np.ascontiguousarray(np.stack(bacl)), "adt": np.stack(adtl),
                   "lnw": lnw, "onw": onw.reshape(1, 128), "msu": msu, "miu": miu,
                   "identf": identf, "identb": idb})
    res = run(p, im)
    og = np.zeros((B, S, 8, 128), np.float32)
    for c in range(NCORES):
        for jj in range(2):
            bh = 2 * c + jj
            og[bh // 8, :, bh % 8, :] = res[c]["og"][jj].astype(np.float32)
    return og.reshape(B, S, D)





def emit_s9(p, g, og8, xn2r, ow_in, ln2_in, fcw_in, pw_in, idb_in, idf_in, s1_out, x1b_out, rows):
    with p.scope():
        identb = p.sb(g + "identb_s", [128, 128], BF16)
        ones1 = p.sb(g + "ones1", [1, 128], F32)
        wld = p.sb(g + "wld", [128, 1024], F32)
        lnwt = p.sb(g + "lnwt", [128, 8], F32)
        owb = p.sb(g + "owb", [128, 8, D], BF16)
        fcb = p.sb(g + "fcb", [128, 8, 4 * D], BF16)
        pwb = p.sb(g + "pwb", [128, 32, D], BF16)
        ogT = p.sb(g + "ogT", [128, 8, 128], BF16)
        xt = p.sb(g + "xt", [128, D], F32)
        x1b = p.sb(g + "x1bt", [128, D], F32)
        x1bb = p.sb(g + "x1bb", [128, D], BF16)
        x1bs = p.sb(g + "x1bs", [128, 8, 128], BF16)
        sqscr = p.sb(g + "sqscr", [128, D], F32)
        st9 = p.sb(g + "st9", [128, 8], F32)
        rrow = p.sb(g + "rrow", [1, 128], F32)
        rstB = p.sb(g + "rstB", [128, 128], F32)
        rstBb = p.sb(g + "rstBb", [128, 128], BF16)
        ub = p.sb(g + "ub", [128, 32, 128], BF16)
        yt = p.sb(g + "yt", [128, D], F32)
        pj = p.ps(g + "pj", [128, 512])
        ptb = p.ps(g + "ptb", [128, 128], BF16)
        pc1 = p.ps(g + "pc1", [1, 128])
        pB = p.ps(g + "pB", [128, 128])

        p.dma(identb[:], idb_in)
        p.op("vector", "memset", ones1[:], 1.0)
        for c8 in range(8):
            p.dma(lnwt[:, c8:c8 + 1], ln2_in[c8 * 128:(c8 + 1) * 128].unsqueeze(1))
        for dc in range(8):
            p.dma(wld[:], ow_in[dc * 128:(dc + 1) * 128, :])
            p.v("tensor_copy", owb[:, dc, :], wld[:])
        for dc in range(8):
            p.dma(wld[:], fcw_in[dc * 128:(dc + 1) * 128, 0:1024])
            p.v("tensor_scalar", wld[:], wld[:], lnwt[:, dc:dc + 1], None, Op.mult)
            p.v("tensor_copy", fcb[:, dc, 0:1024], wld[:])
            for hh in range(1, 4):
                p.dma(wld[:], fcw_in[dc * 128:(dc + 1) * 128, hh * 1024:(hh + 1) * 1024])
                p.v("tensor_scalar", wld[:], wld[:], lnwt[:, dc:dc + 1], None, Op.mult)
                p.v("tensor_copy", fcb[:, dc, hh * 1024:(hh + 1) * 1024], wld[:])
        for ic in range(32):
            p.dma(wld[:], pw_in[ic * 128:(ic + 1) * 128, :])
            p.v("tensor_copy", pwb[:, ic, :], wld[:])
        identf = p.sb(g + "identf_s", [128, 128], F32)
        p.dma(identf[:], idf_in)
        nt = rows // 128
        for tt in range(nt):
            r = slice(tt * 128, (tt + 1) * 128)
            for dc in range(8):
                p.dma(ogT[:, dc, :], og8[dc, r, :], q="scalar", transpose=True)
            p.dma(xt[:], xn2r[r, :])
            for dh in range(2):
                for dc in range(8):
                    p.t("matmul", pj[:], ogT[:, dc, :], owb[:, dc, dh * 512:(dh + 1) * 512],
                        start=(dc == 0), stop=(dc == 7))
                p.v("scalar_tensor_tensor", x1b[:, dh * 512:(dh + 1) * 512], pj[:], 0.5,
                    xt[:, dh * 512:(dh + 1) * 512], Op.mult, Op.add)
            p.dma(x1b_out[r, :], x1b[:])
            p.a("activation", sqscr[:], x1b[:], AF.Square, accum_out=st9[:, 0:1])
            p.v("tensor_scalar", st9[:, 1:2], st9[:, 0:1], 1.0 / D, 1e-6, Op.mult, Op.add)
            p.a("activation", st9[:, 2:3], st9[:, 1:2], AF.Sqrt)
            p.v("reciprocal", st9[:, 3:4], st9[:, 2:3])
            p.v("tensor_copy", x1bb[:], x1b[:])
            p.t("transpose", pc1[:], st9[:, 3:4], identf[:])
            p.v("tensor_copy", rrow[:], pc1[:])
            p.t("matmul", pB[:], ones1[:], rrow[:], start=True, stop=True)
            p.v("tensor_copy", rstBb[:], pB[:])
            for dc in range(8):
                p.t("transpose", ptb[:], x1bb[:, dc * 128:(dc + 1) * 128], identb[:])
                p.v("tensor_tensor", x1bs[:, dc, :], ptb[:], rstBb[:], Op.mult)
            for ic in range(32):
                for dc in range(8):
                    p.t("matmul", pB[:], fcb[:, dc, ic * 128:(ic + 1) * 128], x1bs[:, dc, :],
                        start=(dc == 0), stop=(dc == 7))
                p.a("activation", ub[:, ic, :], pB[:], AF.Silu)
            for dh in range(2):
                for ic in range(32):
                    p.t("matmul", pj[:], ub[:, ic, :], pwb[:, ic, dh * 512:(dh + 1) * 512],
                        start=(ic == 0), stop=(ic == 31))
                p.v("scalar_tensor_tensor", yt[:, dh * 512:(dh + 1) * 512], pj[:], 0.5,
                    x1b[:, dh * 512:(dh + 1) * 512], Op.mult, Op.add)
            p.dma(s1_out[r, :], yt[:])


def build_s9(rows):
    p = Prog()
    ogr = p.din("ogr", [8, rows, 128], BF16)
    xn2r = p.din("xn2", [rows, D], F32)
    ow_in = p.din("ow", [D, D], F32)
    ln2_in = p.din("ln2w", [D], F32)
    fcw_in = p.din("fcw", [D, 4 * D], F32)
    pw_in = p.din("pw", [4 * D, D], F32)
    idb_in = p.din("identb", [128, 128], BF16)
    idf_in = p.din("identf", [128, 128], F32)
    s1_out = p.dout("s1", [rows, D], F32)
    x1b_out = p.dout("x1b", [rows, D], F32)
    emit_s9(p, "s9_", ogr, xn2r, ow_in, ln2_in, fcw_in, pw_in, idb_in, idf_in, s1_out, x1b_out, rows)
    p.finish()
    return p


def stage9(og_full, xn2, ow, ln2w, fcw, pw):
    p = build_s9(TPC)
    idb = np.eye(128).astype(BF)
    idf = np.eye(128, dtype=np.float32)
    og4 = og_full.reshape(B, S, 8, 128).astype(BF)
    osh = [np.ascontiguousarray(og4[c // 4, (c % 4) * TPC:(c % 4 + 1) * TPC].transpose(1, 0, 2))
           for c in range(NCORES)]
    xsh = shard_rows(xn2.reshape(T, D))
    im = [{"ogr": osh[c], "xn2": xsh[c], "ow": ow, "ln2w": ln2w, "fcw": fcw, "pw": pw,
           "identb": idb, "identf": idf} for c in range(NCORES)]
    res = run(p, im)
    s1 = np.concatenate([r["s1"] for r in res], 0)
    x1b = np.concatenate([r["x1b"] for r in res], 0)
    return s1, x1b





def emit_s10(p, g, xnbr, xnlor, gw_in, w1_in, w2_in, w3_in, ws1_in, ws2_in, ws3_in, x1r, s1r, iot16_in, out, rows):
    with p.scope():
        gwraw = p.sb(g + "gwraw", [16, D], F32)
        gwbf = p.sb(g + "gwbf", [16, D], BF16)
        gwlos = p.sb(g + "gwlos", [16, D], F32)
        gwlo = p.sb(g + "gwlo", [16, D], BF16)
        gwT = p.sb(g + "gwT", [128, 8, 16], BF16)
        gwloT = p.sb(g + "gwloT", [128, 8, 16], BF16)
        xnTlo = p.sb(g + "xnTlo", [128, 8, 128], BF16)
        wld = p.sb(g + "wld", [128, 1024], F32)
        wld2 = p.sb(g + "wld2", [128, 1024], F32)
        wbuf1 = p.sb(g + "wbuf1", [128, 2, 8, D], BF16)
        wbuf3 = p.sb(g + "wbuf3", [128, 2, 8, D], BF16)
        wbuf2 = p.sb(g + "wbuf2", [128, 2, 8, D], BF16)
        abuf = p.sb(g + "abuf", [128, 1, 8, 512], BF16)
        w1b_ = [wbuf1[:, i].rearrange("p a (x d) -> p (a x) d", x=2) for i in range(2)]
        w3b_ = [wbuf3[:, i].rearrange("p a (x d) -> p (a x) d", x=2) for i in range(2)]
        w2b_ = [wbuf2[:, i] for i in range(2)]
        ab_ = [abuf[:, 0], abuf[:, 0]]
        gsl = p.sb(g + "gsl", [128, 512], F32)
        yacc = p.sb(g + "yacc", [128, 16, D], BF16)
        c_all = p.sb(g + "c_all", [128, 16, 16], F32)
        lg = p.sb(g + "lg", [128, 16], F32)
        sc = p.sb(g + "sc", [128, 16], F32)
        gsc = p.sb(g + "gsc", [128, 4], F32)
        gm = p.sb(g + "gm", [128, 16], F32)
        msk = p.sb(g + "msk", [128, 16], F32)
        st = p.sb(g + "st", [128, 8], F32)
        iot16 = p.sb(g + "iot16_s", [128, 16], F32)
        xt = p.sb(g + "xt", [128, D], F32)
        s1t = p.sb(g + "s1t", [128, D], F32)
        g1t = p.sb(g + "g1t", [128, D], F32)
        pl = p.ps(g + "pl", [128, 16])
        ph_ = [p.ps(g + f"ph{i}", [128, 512]) for i in range(2)]
        ph3_ = [p.ps(g + f"ph3{i}", [128, 512]) for i in range(2)]
        py_ = [p.ps(g + f"py{i}", [128, 512]) for i in range(2)]

        p.dma(iot16[:], iot16_in)
        p.dma(gwraw[:], gw_in)
        p.v("tensor_copy", gwbf[:], gwraw[:])
        p.v("tensor_tensor", gwlos[:], gwraw[:], gwbf[:], Op.subtract)
        p.v("tensor_copy", gwlo[:], gwlos[:])
        for dc in range(8):
            p.dma(gwT[:, dc, :], gwbf[:, dc * 128:(dc + 1) * 128], q="scalar", transpose=True)
            p.dma(gwloT[:, dc, :], gwlo[:, dc * 128:(dc + 1) * 128], q="scalar", transpose=True)
        nt = rows // 128
        NCH = rows // 512
        xnTall = p.sb(g + "xnTall", [128, 8, rows], BF16)
        for ch in range(NCH):
            for dc in range(8):
                p.dma(xnTall[:, dc, ch * 512:(ch + 1) * 512],
                      xnbr[ch * 512:(ch + 1) * 512, dc * 128:(dc + 1) * 128],
                      q=("scalar", "sync")[(ch + dc) % 2], transpose=True)
        for tt in range(nt):
            r = slice(tt * 128, (tt + 1) * 128)
            ts128 = slice(tt * 128, (tt + 1) * 128)
            for dc in range(8):
                p.dma(xnTlo[:, dc, :], xnlor[r, dc * 128:(dc + 1) * 128], q="scalar", transpose=True)
            for dc in range(8):
                p.t("matmul", pl[:], xnTall[:, dc, ts128], gwT[:, dc, :], start=(dc == 0), stop=False)
                p.t("matmul", pl[:], xnTall[:, dc, ts128], gwloT[:, dc, :], start=False, stop=False)
                p.t("matmul", pl[:], xnTlo[:, dc, :], gwT[:, dc, :], start=False, stop=(dc == 7))
            p.v("tensor_copy", lg[:], pl[:])
            p.v("tensor_reduce", st[:, 0:1], lg[:], AX, Op.max)
            p.v("tensor_scalar", st[:, 1:2], st[:, 0:1], -1.0, None, Op.mult)
            p.v("tensor_scalar", sc[:], lg[:], st[:, 1:2], None, Op.add)
            p.a("activation", sc[:], sc[:], AF.Exp)
            p.v("tensor_reduce", st[:, 2:3], sc[:], AX, Op.add)
            p.v("reciprocal", st[:, 3:4], st[:, 2:3])
            p.v("tensor_scalar", sc[:], sc[:], st[:, 3:4], None, Op.mult)
            for g4 in range(4):
                p.v("tensor_reduce", gsc[:, g4:g4 + 1], sc[:, g4 * 4:(g4 + 1) * 4], AX, Op.max)
            p.v("tensor_reduce", st[:, 4:5], gsc[:], AX, Op.max)
            p.v("tensor_scalar", gm[:, 0:4], gsc[:], st[:, 4:5], None, Op.is_equal)
            p.v("scalar_tensor_tensor", gsc[:], gm[:, 0:4], -1e30, gsc[:], Op.mult, Op.add)
            p.v("tensor_reduce", st[:, 5:6], gsc[:], AX, Op.max)
            for g4 in range(4):
                p.v("tensor_reduce", gsc[:, g4:g4 + 1], sc[:, g4 * 4:(g4 + 1) * 4], AX, Op.max)
            p.v("tensor_scalar", gm[:, 0:4], gsc[:], st[:, 5:6], None, Op.is_ge)
            p.v("tensor_copy", msk[:].rearrange("p (g k) -> p g k", k=4),
                gm[:, 0:4].unsqueeze(2).broadcast_to([128, 4, 4]))
            p.v("tensor_tensor", msk[:], msk[:], sc[:], Op.mult)
            p.v("tensor_copy", lg[:], msk[:])
            for k in range(3):
                p.v("tensor_reduce", st[:, 6:7], lg[:], AX, Op.max)
                p.v("tensor_scalar", gm[:], lg[:], st[:, 6:7], None, Op.is_equal)
                p.v("scalar_tensor_tensor", lg[:], gm[:], -1e30, lg[:], Op.mult, Op.add)
            p.v("tensor_reduce", st[:, 6:7], lg[:], AX, Op.max)
            p.v("tensor_scalar", gm[:], msk[:], st[:, 6:7], None, Op.is_ge)
            p.v("tensor_tensor", c_all[:, tt, :], gm[:], sc[:], Op.mult)
            p.op("vector", "memset", yacc[:, tt, :], 0.0)
        for e in range(16):
            eb = e % 2
            w1b, w3b, w2b, ab = w1b_[eb], w3b_[eb], w2b_[eb], ab_[eb]
            ph, ph3, py = ph_[eb], ph3_[eb], py_[eb]
            for dc in range(8):
                p.dma(wld[:, 0:512], w1_in[e, dc * 128:(dc + 1) * 128, :], q=("sync", "gpsimd")[dc % 2])
                p.v("tensor_copy", w1b[:, 2 * dc, :], wld[:, 0:512])
                p.dma(wld2[:, 0:512], w3_in[e, dc * 128:(dc + 1) * 128, :], q=("scalar", "gpsimd")[dc % 2])
                p.v("tensor_copy", w3b[:, 2 * dc, :], wld2[:, 0:512])
            for ic in range(4):
                p.dma(wld[:], w2_in[e, ic * 128:(ic + 1) * 128, :])
                p.v("tensor_copy", w2b[:, ic, :], wld[:])
            for ch in range(NCH):
                tsl = slice(ch * 512, (ch + 1) * 512)
                for ic in range(4):
                    for dc in range(8):
                        p.t("matmul", ph[:], w1b[:, 2 * dc, ic * 128:(ic + 1) * 128], xnTall[:, dc, tsl],
                            start=(dc == 0), stop=(dc == 7))
                    for dc in range(8):
                        p.t("matmul", ph3[:], w3b[:, 2 * dc, ic * 128:(ic + 1) * 128], xnTall[:, dc, tsl],
                            start=(dc == 0), stop=(dc == 7))
                    p.a("activation", gsl[:], ph[:], AF.Silu)
                    p.v("tensor_tensor", ab[:, ic, :], gsl[:], ph3[:], Op.mult)
                for sub in range(4):
                    tt = ch * 4 + sub
                    for dh in range(2):
                        for ic in range(4):
                            p.t("matmul", py[:], ab[:, ic, sub * 128:(sub + 1) * 128],
                                w2b[:, ic, dh * 512:(dh + 1) * 512], start=(ic == 0), stop=(ic == 3))
                        p.v("scalar_tensor_tensor", yacc[:, tt, dh * 512:(dh + 1) * 512], py[:],
                            c_all[:, tt, e:e + 1], yacc[:, tt, dh * 512:(dh + 1) * 512], Op.mult, Op.add)
        ws1b = wbuf1[:, 0]
        ws3b = wbuf3[:, 0]
        ws2b = wbuf2[:, 0]
        abs_ = abuf[:, 0]
        for dc in range(8):
            p.dma(wld[:], ws1_in[dc * 128:(dc + 1) * 128, :])
            p.v("tensor_copy", ws1b[:, dc, :], wld[:])
            p.dma(wld[:], ws3_in[dc * 128:(dc + 1) * 128, :])
            p.v("tensor_copy", ws3b[:, dc, :], wld[:])
            p.dma(wld[:], ws2_in[dc * 128:(dc + 1) * 128, :])
            p.v("tensor_copy", ws2b[:, dc, :], wld[:])
        for ch in range(NCH):
            tsl = slice(ch * 512, (ch + 1) * 512)
            for ic in range(8):
                ph, ph3 = ph_[ic % 2], ph3_[ic % 2]
                for dc in range(8):
                    p.t("matmul", ph[:], ws1b[:, dc, ic * 128:(ic + 1) * 128], xnTall[:, dc, tsl],
                        start=(dc == 0), stop=(dc == 7))
                for dc in range(8):
                    p.t("matmul", ph3[:], ws3b[:, dc, ic * 128:(ic + 1) * 128], xnTall[:, dc, tsl],
                        start=(dc == 0), stop=(dc == 7))
                p.a("activation", gsl[:], ph[:], AF.Silu)
                p.v("tensor_tensor", abs_[:, ic, :], gsl[:], ph3[:], Op.mult)
            for sub in range(4):
                tt = ch * 4 + sub
                for dh in range(2):
                    py = py_[dh]
                    for ic in range(8):
                        p.t("matmul", py[:], abs_[:, ic, sub * 128:(sub + 1) * 128],
                            ws2b[:, ic, dh * 512:(dh + 1) * 512], start=(ic == 0), stop=(ic == 7))
                    p.v("scalar_tensor_tensor", yacc[:, tt, dh * 512:(dh + 1) * 512], py[:], 1.0,
                        yacc[:, tt, dh * 512:(dh + 1) * 512], Op.mult, Op.add)
                r = slice(tt * 128, (tt + 1) * 128)
                p.a("activation", g1t[:], yacc[:, tt, :], AF.Silu)
                p.dma(xt[:], x1r[r, :])
                p.dma(s1t[:], s1r[r, :])
                p.v("tensor_tensor", g1t[:], g1t[:], s1t[:], Op.mult)
                p.v("tensor_tensor", g1t[:], g1t[:], xt[:], Op.add)
                p.dma(out[r, :], g1t[:])


def build_s10(rows):
    p = Prog()
    xnbr = p.din("xnbr", [rows, D], BF16)
    xnlor = p.din("xnlor", [rows, D], BF16)
    gw_in = p.din("gw1", [16, D], F32)
    w1_in = p.din("w1", [16, D, 512], F32)
    w2_in = p.din("w2", [16, 512, D], F32)
    w3_in = p.din("w3", [16, D, 512], F32)
    ws1_in = p.din("ws1", [D, D], F32)
    ws2_in = p.din("ws2", [D, D], F32)
    ws3_in = p.din("ws3", [D, D], F32)
    x1r = p.din("x1", [rows, D], F32)
    s1r = p.din("s1", [rows, D], F32)
    iot16_in = p.din("iot16", [128, 16], F32)
    out = p.dout("out", [rows, D], F32)
    emit_s10(p, "s10_", xnbr, xnlor, gw_in, w1_in, w2_in, w3_in, ws1_in, ws2_in, ws3_in, x1r, s1r, iot16_in, out, rows)
    p.finish()
    return p


def stage10(xn2b, xn2lo, gw1, w1, w2, w3, ws1, ws2, ws3, x1, s1):
    p = build_s10(TPC)
    iot16 = np.broadcast_to(np.arange(16, dtype=np.float32), (128, 16)).copy()
    bsh = shard_rows(xn2b.reshape(T, D)); lsh = shard_rows(xn2lo.reshape(T, D))
    xsh = shard_rows(x1.reshape(T, D)); ssh = shard_rows(s1.reshape(T, D))
    im = [{"xnbr": bsh[c], "xnlor": lsh[c], "gw1": gw1, "w1": w1, "w2": w2, "w3": w3,
           "ws1": ws1, "ws2": ws2, "ws3": ws3, "x1": xsh[c], "s1": ssh[c], "iot16": iot16}
          for c in range(NCORES)]
    res = run(p, im)
    return np.concatenate([r["out"] for r in res], 0)





def emit_s4_sharded(p, g, xnbr, xnlor, gw, iota_tok_in, idf_in, maskvec_in, cofff_in,
                    countsl, counts_all, table_loc, table_red, table_u32,
                    gbidx_out, wts_out, cc_counts_hook, cc_table_hook, rows):
    with p.scope():
        NTT = rows // 128
        gwraw = p.sb(g + "gwraw", [64, D], F32)
        gwbf = p.sb(g + "gwbf", [64, D], BF16)
        gwlosc = p.sb(g + "gwlosc", [64, D], F32)
        gwlo = p.sb(g + "gwlo", [64, D], BF16)
        gwT = p.sb(g + "gwT", [128, 8, 64], BF16)
        gwloT = p.sb(g + "gwloT", [128, 8, 64], BF16)
        xnT = p.sb(g + "xnT", [128, 8, 128], BF16)
        xnloT = p.sb(g + "xnloT", [128, 8, 128], BF16)
        lwork = p.sb(g + "lwork", [128, 64], F32)
        iot = p.sb(g + "iot", [128, 64], F32)
        oh = p.sb(g + "oh", [128, 64], F32)
        mx = p.sb(g + "mx", [128, 4], F32)
        idx_all = p.sb(g + "idx_all", [128, NTT, 4], F32)
        ew = p.sb(g + "ew", [128, 8], F32)
        wtt = p.sb(g + "wtt", [128, 4], F32)
        selT_all = p.sb(g + "selT_all", [64, rows], F32)
        zerT = p.sb(g + "zerT", [64, rows], F32)
        posT = p.sb(g + "posT", [64, rows], F32)
        sel = p.sb(g + "sel", [128, 64], F32)
        pos_tok = p.sb(g + "pos_tok", [128, 64], F32)
        poscol = p.sb(g + "poscol", [128, 4], F32)
        base = p.sb(g + "base", [128, 1], F32)
        km = p.sb(g + "km", [128, 1], F32)
        ge = p.sb(g + "ge", [128, 1], F32)
        t1 = p.sb(g + "t1", [128, 1], F32)
        gidxf = p.sb(g + "gidxf", [128, 1], F32)
        gidxu = p.sb(g + "gidxu", [128, 1], U32)
        gbu = p.sb(g + "gbu", [128, 4], U32)
        tidi = p.sb(g + "tidi", [128, 1], I32)
        tidf = p.sb(g + "tidf", [128, 1], F32)
        tids = p.sb(g + "tids", [128, 1], F32)
        identf = p.sb(g + "identf_s", [128, 128], F32)
        zl = p.sb(g + "zl", [128, 128], F32)
        cnt_sb = p.sb(g + "cnt_sb", [8, 64], F32)
        mvec = p.sb(g + "mvec", [8, 1], F32)
        brow = p.sb(g + "brow", [1, 64], F32)
        bcol = p.sb(g + "bcol", [64, 1], F32)
        coffc = p.sb(g + "coffc", [1, 1], F32)
        coffcol = p.sb(g + "coffcol", [128, 1], F32)
        ones1 = p.sb(g + "ones1", [1, 128], F32)
        tbl_cv = p.sb(g + "tbl_cv", [128, 128], F32)
        tbl_cu = p.sb(g + "tbl_cu", [128, 128], U32)
        pl = p.ps(g + "pl", [128, 64])
        pt2 = p.ps(g + "pt2", [64, 128])
        ptp = p.ps(g + "ptp", [128, 64])
        pb1 = p.ps(g + "pb1", [1, 64])
        pcc = p.ps(g + "pcc", [128, 1])

        p.dma(identf[:], idf_in)
        p.dma(iot[:], iota_tok_in)
        p.dma(mvec[:], maskvec_in)
        p.dma(coffc[:], cofff_in)
        p.op("vector", "memset", zerT[:], 0.0)
        p.op("vector", "memset", zl[:], 0.0)
        p.op("vector", "memset", ones1[:], 1.0)
        p.dma(table_loc.rearrange("(q a) x -> q (a x)", q=128), zl[:].bitcast(F32))
        p.dma(gwraw[:], gw)
        p.v("tensor_copy", gwbf[:], gwraw[:])
        p.v("tensor_tensor", gwlosc[:], gwraw[:], gwbf[:], Op.subtract)
        p.v("tensor_copy", gwlo[:], gwlosc[:])
        for dc in range(8):
            p.dma(gwT[:, dc, :], gwbf[:, dc * 128:(dc + 1) * 128], q="scalar", transpose=True)
            p.dma(gwloT[:, dc, :], gwlo[:, dc * 128:(dc + 1) * 128], q="scalar", transpose=True)
        breg = p.nc.gpsimd.alloc_register()
        p.raw("gpsimd", lambda e: e.reg_mov(breg, T - 1))
        p.g("iota", tidi[:], pattern=[[0, 1]], base=0, channel_multiplier=1)
        p.v("tensor_copy", tidf[:], tidi[:])
        p.t("matmul", pcc[:], ones1[:], coffc[:], start=True, stop=True)
        p.v("tensor_copy", coffcol[:], pcc[:])
        p.v("tensor_tensor", tidf[:], tidf[:], coffcol[:], Op.add)
        for tt in range(NTT):
            r = slice(tt * 128, (tt + 1) * 128)
            for dc in range(8):
                p.dma(xnT[:, dc, :], xnbr[r, dc * 128:(dc + 1) * 128], q="scalar", transpose=True)
                p.dma(xnloT[:, dc, :], xnlor[r, dc * 128:(dc + 1) * 128], q="scalar", transpose=True)
            for dc in range(8):
                p.t("matmul", pl[:], xnT[:, dc, :], gwT[:, dc, :], start=(dc == 0), stop=False)
                p.t("matmul", pl[:], xnT[:, dc, :], gwloT[:, dc, :], start=False, stop=False)
                p.t("matmul", pl[:], xnloT[:, dc, :], gwT[:, dc, :], start=False, stop=(dc == 7))
            p.v("tensor_copy", lwork[:], pl[:])
            p.op("vector", "memset", sel[:], 0.0)
            for k in range(4):
                p.v("tensor_reduce", mx[:, k:k + 1], lwork[:], AX, Op.max)
                p.v("tensor_scalar", oh[:], lwork[:], mx[:, k:k + 1], None, Op.is_equal)
                p.v("scalar_tensor_tensor", lwork[:], oh[:], -1e30, lwork[:], Op.mult, Op.add)
                p.v("tensor_tensor", sel[:], sel[:], oh[:], Op.add)
                p.v("tensor_tensor", oh[:], oh[:], iot[:], Op.mult)
                p.v("tensor_reduce", idx_all[:, tt, k:k + 1], oh[:], AX, Op.add)
            p.v("tensor_scalar", ew[:, 6:7], mx[:, 0:1], -1.0, None, Op.mult)
            p.v("tensor_scalar", ew[:, 0:4], mx[:], ew[:, 6:7], None, Op.add)
            p.a("activation", ew[:, 0:4], ew[:, 0:4], AF.Exp)
            p.v("tensor_reduce", ew[:, 4:5], ew[:, 0:4], AX, Op.add)
            p.v("reciprocal", ew[:, 5:6], ew[:, 4:5])
            p.v("tensor_scalar", wtt[:], ew[:, 0:4], ew[:, 5:6], None, Op.mult)
            p.dma(wts_out[r, :], wtt[:])
            p.t("transpose", pt2[:], sel[:, 0:64], identf[:])
            p.v("tensor_copy", selT_all[:, r], pt2[:])
        p.v("tensor_tensor_scan", posT[:], zerT[:], selT_all[:], 0.0, Op.add, Op.add)
        p.dma(countsl, posT[:, rows - 1:rows])
        cc_counts_hook(p)
        p.dma(cnt_sb[:], counts_all.rearrange("(a b) x -> a (b x)", a=8))
        p.t("matmul", pb1[:], mvec[:], cnt_sb[:], start=True, stop=True)
        p.v("tensor_copy", brow[:], pb1[:])
        p.t("transpose", pcc[0:64, :], brow[:], identf[0:1, 0:1])
        p.v("tensor_copy", bcol[:], pcc[0:64, :])
        p.v("tensor_scalar", posT[:], posT[:], bcol[:], None, Op.add)
        for tt in range(NTT):
            r = slice(tt * 128, (tt + 1) * 128)
            p.t("transpose", ptp[:], posT[:, r], identf[0:64, 0:64])
            p.v("tensor_copy", pos_tok[:, 0:64], ptp[:])
            p.v("tensor_scalar", tids[:], tidf[:], float(tt * 128), None, Op.add)
            for k in range(4):
                p.v("tensor_scalar", oh[:], iot[:], idx_all[:, tt, k:k + 1], None, Op.is_equal)
                p.v("tensor_tensor", oh[:], oh[:], pos_tok[:, 0:64], Op.mult)
                p.v("tensor_reduce", poscol[:, k:k + 1], oh[:], AX, Op.add)
            for k in range(4):
                p.v("tensor_scalar", poscol[:, k:k + 1], poscol[:, k:k + 1], -1.0, None, Op.add)
                p.v("tensor_scalar", base[:], idx_all[:, tt, k:k + 1], 256.0, None, Op.mult)
                p.v("tensor_tensor", base[:], base[:], poscol[:, k:k + 1], Op.add)
                p.v("tensor_scalar", ge[:], poscol[:, k:k + 1], 256.0, None, Op.is_ge)
                p.v("scalar_tensor_tensor", gidxf[:], ge[:], 100000.0, base[:], Op.mult, Op.add)
                p.v("tensor_copy", gidxu[:], gidxf[:])
                p.idma("gpsimd", out=table_loc, out_offset=IndirectOffsetOnAxis(ap=gidxu[:], axis=0),
                       in_=tids[:], in_offset=None, bounds_check=breg, oob_is_err=False)
                p.v("tensor_scalar", km[:], poscol[:, k:k + 1], 256.0, None, Op.is_lt)
                p.v("tensor_tensor", t1[:], base[:], km[:], Op.mult)
                p.v("scalar_tensor_tensor", gidxf[:], ge[:], float(T), t1[:], Op.mult, Op.add)
                p.v("tensor_copy", gbu[:, k:k + 1], gidxf[:])
            p.dma(gbidx_out[r, :], gbu[:])
        cc_table_hook(p)
        p.dma(tbl_cv[:], table_red.rearrange("(q a) x -> q (a x)", q=128))
        p.v("tensor_copy", tbl_cu[:], tbl_cv[:])
        p.dma(table_u32.rearrange("(q a) x -> q (a x)", q=128), tbl_cu[:])



def build_fused():
    from concourse.expressions import RuntimeValue
    p = Prog()
    xloc = p.din("xloc", [TPC, D], F32)
    Whead = p.din("Whead", [4, 3, 64, D], F32)
    bhead = p.din("bhead", [4, 3, 64], F32)
    bmask_in = p.din("bmask", [4, 128, 512], BF16)
    amask_in = p.din("amask", [4, 128, 512], BF16)
    identb_in = p.din("identb", [128, 128], BF16)
    identf_in = p.din("identf", [128, 128], F32)
    gw64 = p.din("gw64", [64, D], F32)
    iota_tok = p.din("iota_tok", [128, 64], F32)
    maskvec = p.din("maskvec", [8, 1], F32)
    cofff = p.din("cofff", [1, 1], F32)
    coffi = p.din("coffi", [1, 2], I32)
    gp = p.din("gp", [8, 768, 512], I32)
    gs = p.din("gs", [8, 768, 8], F32)
    up = p.din("up", [8, 768, 512], I32)
    us = p.din("us", [8, 768, 8], F32)
    dp = p.din("dp", [8, 1024, 384], I32)
    ds_ = p.din("ds", [8, 1024, 6], F32)
    sgp = p.din("sgp", [768, 512], I32)
    sgs = p.din("sgs", [768, 8], F32)
    sup = p.din("sup", [768, 512], I32)
    sus = p.din("sus", [768, 8], F32)
    sdp = p.din("sdp", [1024, 384], I32)
    sds = p.din("sds", [1024, 6], F32)
    Wh8 = p.din("Wh8", [2, 4, D, 128], F32)
    bac = p.din("bac", [2, D, 2], F32)
    adt = p.din("adt", [2, 2], F32)
    lnw = p.din("lnw", [D], F32)
    onw = p.din("onw", [1, 128], F32)
    msu = p.din("msu", [128, 128], F32)
    miu = p.din("miu", [128, 128], F32)
    ow_in = p.din("ow", [D, D], F32)
    ln2w = p.din("ln2w", [D], F32)
    fcw_in = p.din("fcw", [D, 4 * D], F32)
    pw_in = p.din("pw", [4 * D, D], F32)
    gw16 = p.din("gw1", [16, D], F32)
    w1_in = p.din("w1", [16, D, 512], F32)
    w2_in = p.din("w2", [16, 512, D], F32)
    w3_in = p.din("w3", [16, D, 512], F32)
    ws1_in = p.din("ws1", [D, D], F32)
    ws2_in = p.din("ws2", [D, D], F32)
    ws3_in = p.din("ws3", [D, D], F32)
    iot16 = p.din("iot16", [128, 16], F32)
    out = p.dout("out", [TPC, D], F32)

    xn_l = p.dint("xn_l", [TPC, D], F32)
    xnb_l = p.dint("xnb_l", [TPC, D], BF16)
    xnlo_l = p.dint("xnlo_l", [TPC, D], BF16)
    xnb_batch = p.dint("xnb_batch", [S, D], BF16)
    xnb_full = p.dint("xnb_full", [T, D], BF16)
    z4_l = p.dint("z4_l", [4, S, 64], F32)
    zAG = p.dint("zAG", [16, S, 64], F32)
    z_loc = p.dint("z_loc", [16, TPC, 64], F32)
    s0_l = p.dint("s0_l", [TPC, D], F32)
    countsl = p.dint("countsl", [64, 1], F32)
    counts_all = p.dint("counts_all", [512, 1], F32)
    table_loc = p.dint("table_loc", [T, 1], F32)
    table_red = p.dint("table_red", [T, 1], F32)
    table_u32 = p.dint("table_u32", [T, 1], U32)
    tbl_loc = p.dint("tbl_loc", [8, 256], U32)
    gbidx_l = p.dint("gbidx_l", [TPC, 4], U32)
    wts_l = p.dint("wts_l", [TPC, 4], F32)
    eo_l = p.dint("eo_l", [8, 256, D], BF16)
    eoall = p.dint("eoall", [T + 1, D], BF16)
    x1_l = p.dint("x1_l", [TPC, D], F32)
    xn2_l = p.dint("xn2_l", [TPC, D], F32)
    xn2b_l = p.dint("xn2b_l", [TPC, D], BF16)
    xn2lo_l = p.dint("xn2lo_l", [TPC, D], BF16)
    xn2b_batch = p.dint("xn2b_batch", [S, D], BF16)
    og_l = p.dint("og_l", [2, S, 128], BF16)
    ogAG = p.dint("ogAG", [8, S, 128], BF16)
    og_loc = p.dint("og_loc", [8, TPC, 128], BF16)
    s1_l = p.dint("s1_l", [TPC, D], F32)
    x1b_l = p.dint("x1b_l", [TPC, D], F32)

    GRP2 = [[0, 1, 2, 3], [4, 5, 6, 7]]
    GRP1 = [list(range(8))]

    with p.scope():
        coffs = p.sb("f_coffs", [1, 2], I32)
        p.dma(coffs[:], coffi)
        r_sync = p.nc.sync.alloc_register()
        r_sync2 = p.nc.sync.alloc_register()
        p.emit("sync", lambda e: p.nc.reg_load(r_sync, coffs[0:1, 1:2]), "comp",
               reads=[Prog._region(coffs[0:1, :])])
        p.emit("sync", lambda e: p.nc.reg_load(r_sync2, coffs[0:1, 0:1]), "comp",
               reads=[Prog._region(coffs[0:1, :])])

        emit_s1(p, "a_", xloc, xn_l, xnb_l, xnlo_l, TPC)
        p.emit("gpsimd", lambda e: e.collective_compute("AllGather", Op.bypass,
               replica_groups=GRP2, ins=[xnb_l.opt()], outs=[xnb_batch.opt()]), "cc",
               reads=[Prog._region(xnb_l.opt())], writes=[Prog._region(xnb_batch.opt())])
        p.emit("gpsimd", lambda e: e.collective_compute("AllGather", Op.bypass,
               replica_groups=GRP1, ins=[xnb_l.opt()], outs=[xnb_full.opt()]), "cc",
               reads=[Prog._region(xnb_l.opt())], writes=[Prog._region(xnb_full.opt())])
        emit_s2(p, "b_", xnb_batch, Whead, bhead, bmask_in, amask_in, identb_in, identf_in, z4_l)
        p.emit("gpsimd", lambda e: e.collective_compute("AllGather", Op.bypass,
               replica_groups=GRP2, ins=[z4_l.opt()], outs=[zAG.opt()]), "cc",
               reads=[Prog._region(z4_l.opt())], writes=[Prog._region(zAG.opt())])
        p.dma(z_loc.opt(), zAG[:, bass.ds(RuntimeValue(r_sync), TPC), :])
        emit_s4_sharded(p, "d_", xnb_l, xnlo_l, gw64, iota_tok, identf_in, maskvec, cofff,
                        countsl, counts_all, table_loc, table_red, table_u32,
                        gbidx_l, wts_l,
                        lambda pp: pp.emit("gpsimd", lambda e: e.collective_compute(
                            "AllGather", Op.bypass, replica_groups=GRP1,
                            ins=[countsl.opt()], outs=[counts_all.opt()]), "cc",
                            reads=[Prog._region(countsl)], writes=[Prog._region(counts_all)]),
                        lambda pp: pp.emit("gpsimd", lambda e: e.collective_compute(
                            "AllReduce", Op.add, replica_groups=GRP1,
                            ins=[table_loc.opt()], outs=[table_red.opt()]), "cc",
                            reads=[Prog._region(table_loc)], writes=[Prog._region(table_red)]),
                        TPC)
        emit_s3(p, "c_", xn_l, z_loc, p.din("gamma", [D], F32), p.din("beta", [D], F32), s0_l, TPC)
        p.dma(tbl_loc.rearrange("a b -> (a b)").unsqueeze(1), table_u32[bass.ds(RuntimeValue(r_sync2), TPC), :])
        emit_s5(p, "e_", xnb_full, tbl_loc, gp, gs, up, us, dp, ds_, identb_in, eo_l)
        p.emit("gpsimd", lambda e: e.collective_compute("AllGather", Op.bypass,
               replica_groups=GRP1, ins=[eo_l.opt()], outs=[eoall[0:T, :].opt()]), "cc",
               reads=[Prog._region(eo_l.opt())], writes=[Prog._region(eoall[0:T, :].opt())])
        with p.scope():
            zrow1 = p.sb("f_zrow1", [1, D], BF16)
            p.op("vector", "memset", zrow1[:], 0.0)
            p.dma(eoall[T:T + 1, :], zrow1[:])
        emit_s6(p, "f_", eoall, gbidx_l, wts_l, xloc, s0_l, xnb_l,
                sgp, sgs, sup, sus, sdp, sds, identb_in, x1_l, TPC)
        emit_s1(p, "g_", x1_l, xn2_l, xn2b_l, xn2lo_l, TPC)
        p.emit("gpsimd", lambda e: e.collective_compute("AllGather", Op.bypass,
               replica_groups=GRP2, ins=[xn2b_l.opt()], outs=[xn2b_batch.opt()]), "cc",
               reads=[Prog._region(xn2b_l.opt())], writes=[Prog._region(xn2b_batch.opt())])
        emit_s8(p, "h_", xn2b_batch, Wh8, bac, adt, lnw, onw, msu, miu, identf_in, identb_in, og_l)
        p.emit("gpsimd", lambda e: e.collective_compute("AllGather", Op.bypass,
               replica_groups=GRP2, ins=[og_l.opt()], outs=[ogAG.opt()]), "cc",
               reads=[Prog._region(og_l.opt())], writes=[Prog._region(ogAG.opt())])
        p.dma(og_loc.opt(), ogAG[:, bass.ds(RuntimeValue(r_sync), TPC), :])
        emit_s9(p, "i_", og_loc, xn2_l, ow_in, ln2w, fcw_in, pw_in, identb_in, identf_in,
                s1_l, x1b_l, TPC)
        emit_s10(p, "j_", xn2b_l, xn2lo_l, gw16, w1_in, w2_in, w3_in, ws1_in, ws2_in, ws3_in,
                 x1_l, s1_l, iot16, out, TPC)
    p.finish()
    return p


def kernel(**inputs):
    z = inputs
    x = np.ascontiguousarray(z["L0_attn_embedded"].astype(np.float32, copy=False)).reshape(T, D)
    bm, am = attn_masks()
    identb = np.eye(128).astype(BF)
    identf = np.eye(128, dtype=np.float32)
    iota64 = np.broadcast_to(np.arange(64, dtype=np.float32), (128, 64)).copy()
    iota16 = np.broadcast_to(np.arange(16, dtype=np.float32), (128, 16)).copy()
    msu = np.where(np.arange(128)[:, None] < np.arange(128)[None, :], 0.0, NEG).astype(np.float32)
    miu = np.where(np.arange(128)[:, None] <= np.arange(128)[None, :], 0.0, NEG).astype(np.float32)
    Wq, Wk, Wv = z["L0_attn_Wq"], z["L0_attn_Wk"], z["L0_attn_Wv"]
    bq, bk, bv = z["L0_attn_bq"], z["L0_attn_bk"], z["L0_attn_bv"]
    p = build_fused()
    im = []
    for c in range(NCORES):
        hs = [(c * 4 + j) % 16 for j in range(4)]
        Wh = np.stack([np.stack([W[h * 64:(h + 1) * 64] for W in (Wq, Wk, Wv)]) for h in hs])
        bh_ = np.stack([np.stack([bb[h * 64:(h + 1) * 64] for bb in (bq, bk, bv)]) for h in hs])
        sl8 = slice(c * 8, (c + 1) * 8)
        Whl, bacl, adtl = [], [], []
        for bhp in (2 * c, 2 * c + 1):
            h = bhp % 8
            cs = slice(h * 128, (h + 1) * 128)
            Whl.append(np.stack([z["L1_attn_qw"][:, cs], z["L1_attn_kw"][:, cs],
                                 z["L1_attn_vw"][:, cs], z["L1_attn_gw"][:, cs]]))
            bacl.append(np.stack([z["L1_attn_bw"][:, h], z["L1_attn_aw"][:, h]], 1))
            adtl.append(np.array([z["L1_attn_A_log"][h], z["L1_attn_dt_bias"][h]], np.float32))
        im.append({
            "xloc": np.ascontiguousarray(x[c * TPC:(c + 1) * TPC]),
            "Whead": np.ascontiguousarray(Wh), "bhead": np.ascontiguousarray(bh_),
            "bmask": bm, "amask": am, "identb": identb, "identf": identf,
            "gw64": z["L0_ffn_gate_w"], "iota_tok": iota64,
            "maskvec": (np.arange(8) < c).astype(np.float32).reshape(8, 1),
            "cofff": np.array([[c * TPC]], np.float32),
            "coffi": np.array([[c * TPC, (c % 4) * TPC]], np.int32),
            "gp": np.ascontiguousarray(z["L0_ffn_gate_packed"][sl8]),
            "gs": np.ascontiguousarray(z["L0_ffn_gate_scales"][sl8]),
            "up": np.ascontiguousarray(z["L0_ffn_up_packed"][sl8]),
            "us": np.ascontiguousarray(z["L0_ffn_up_scales"][sl8]),
            "dp": np.ascontiguousarray(z["L0_ffn_down_packed"][sl8]),
            "ds": np.ascontiguousarray(z["L0_ffn_down_scales"][sl8]),
            "sgp": z["L0_ffn_sh_gate_packed"], "sgs": z["L0_ffn_sh_gate_scales"],
            "sup": z["L0_ffn_sh_up_packed"], "sus": z["L0_ffn_sh_up_scales"],
            "sdp": z["L0_ffn_sh_down_packed"], "sds": z["L0_ffn_sh_down_scales"],
            "Wh8": np.ascontiguousarray(np.stack(Whl)), "bac": np.ascontiguousarray(np.stack(bacl)),
            "adt": np.stack(adtl), "lnw": z["L1_attn_ln1_w"],
            "onw": z["L1_attn_onorm_w"].reshape(1, 128), "msu": msu, "miu": miu,
            "ow": z["L1_attn_ow"], "ln2w": z["L1_attn_ln2_w"], "fcw": z["L1_attn_fcw"],
            "pw": z["L1_attn_pw"], "gw1": z["L1_ffn_gate_w"],
            "w1": z["L1_ffn_w1"], "w2": z["L1_ffn_w2"], "w3": z["L1_ffn_w3"],
            "ws1": z["L1_ffn_ws1"], "ws2": z["L1_ffn_ws2"], "ws3": z["L1_ffn_ws3"],
            "iot16": iota16,
            "gamma": z["L0_attn_gamma"], "beta": z["L0_attn_beta"],
        })
    res = run(p, im)
    outp = np.concatenate([r["out"] for r in res], 0)
    return outp.reshape(B, S, D).astype(np.float32, copy=False)






# version 14 (final)
# speedup vs baseline: 1.4557x; 1.1452x over previous
import numpy as np
import ml_dtypes
from contextlib import ExitStack
import concourse.bass as bass
import concourse.mybir as mybir
from concourse.alu_op_type import AluOpType as Op
from concourse.bass_utils import run_bass_kernel_spmd
from concourse.bass import IndirectOffsetOnAxis

F32 = mybir.dt.float32
BF16 = mybir.dt.bfloat16
I32 = mybir.dt.int32
U32 = mybir.dt.uint32
AF = mybir.ActivationFunctionType
AX = mybir.AxisListType.X
BF = ml_dtypes.bfloat16

NCORES = 8
B, S, D = 2, 8192, 1024
T = B * S
TPC = T // NCORES
NT = TPC // 128

ENGINES = ("sync", "vector", "scalar", "tensor", "gpsimd")


class Prog:

    SEMS = ("Cv", "Cs", "Ct", "Cg", "Cy", "Dy", "Ds", "Dg", "CC")
    STEP = {"Cv": 1, "Cs": 1, "Ct": 1, "Cg": 1, "Cy": 1, "Dy": 16, "Ds": 16, "Dg": 16, "CC": 1}

    def __init__(self):
        self.nc = bass.Bass("TRN2", target_bir_lowering=False)
        self.ops = []
        self._es = ExitStack()
        self._scopes = []

    def _sem_of(self, engine, kind):
        if kind == "cc":
            return "CC"
        if kind in ("dma", "sdma"):
            return {"sync": "Dy", "scalar": "Ds", "gpsimd": "Dg", "vector": "Dy", "tensor": "Dy"}[engine]
        return {"vector": "Cv", "scalar": "Cs", "tensor": "Ct", "gpsimd": "Cg", "sync": "Cy"}[engine]

    def din(self, name, shape, dtype):
        return self.nc.dram_tensor(name, list(shape), dtype, kind="ExternalInput").ap()

    def dout(self, name, shape, dtype):
        return self.nc.dram_tensor(name, list(shape), dtype, kind="ExternalOutput").ap()

    def dint(self, name, shape, dtype):
        return self.nc.dram_tensor(name, list(shape), dtype).ap()

    def sb(self, name, shape, dtype):
        es = self._scopes[-1] if self._scopes else self._es
        return es.enter_context(self.nc.sbuf_tensor(name, list(shape), dtype))

    def ps(self, name, shape, dtype=F32):
        es = self._scopes[-1] if self._scopes else self._es
        assert len(shape) == 2
        esz = {F32: 4, BF16: 2, I32: 4, U32: 4}[dtype]
        cols = max(shape[1], 2048 // esz)
        h = es.enter_context(self.nc.psum_tensor(name, [shape[0], cols], dtype))
        return h[:, 0:shape[1]]

    from contextlib import contextmanager as _cm

    @_cm
    def scope(self):
        es = ExitStack()
        self._scopes.append(es)
        try:
            with es:
                yield
        finally:
            self._scopes.pop()
            self.ops.append(dict(engine=None, kind="barrier", fn=None, reads=[], writes=[]))

    @staticmethod
    def _region(ap):
        if ap is None:
            return None
        t = getattr(ap, "tensor", None)
        if t is None:
            if hasattr(ap, "ap") and hasattr(ap, "axis"):
                return Prog._region(ap.ap)
            return None
        off = ap.offset
        if not isinstance(off, int):
            return (t.name, 0, 1 << 62)
        span = 1
        for st_, ct in ap.ap:
            span += (ct - 1) * abs(st_)
        return (t.name, off, off + span)

    @staticmethod
    def _collect(args, kwargs, first_is_write=True):
        reads, writes = [], []
        seen_first = [False]

        def add(x, force_read=False):
            r = Prog._region(x)
            if r is None:
                return
            if not seen_first[0] and first_is_write and not force_read:
                writes.append(r)
                seen_first[0] = True
            else:
                reads.append(r)

        for a in args:
            if hasattr(a, "tensor") or (hasattr(a, "ap") and hasattr(a, "axis")):
                add(a)
        for k, v in kwargs.items():
            if hasattr(v, "tensor") or (hasattr(v, "ap") and hasattr(v, "axis")):
                if k in ("accum_out", "out", "out_offset"):
                    writes.append(Prog._region(v))
                else:
                    reads.append(Prog._region(v))
        return reads, writes

    def emit(self, engine, fn, kind="comp", reads=None, writes=None):
        assert engine in ENGINES
        self.ops.append(dict(engine=engine, kind=kind, fn=fn,
                             reads=reads or [], writes=writes or []))

    def raw(self, engine, fn):
        self.emit(engine, fn, "raw")

    def dma(self, out, in_, q="sync", **kw):
        kind = "sdma" if q == "gpsimd" else "dma"
        r_out = self._region(out)
        r_in = self._region(in_)
        self.emit(q, (lambda e, out=out, in_=in_, kw=kw: e.dma_start(out=out, in_=in_, **kw)),
                  kind, reads=[r_in], writes=[r_out])

    def idma(self, q, **kw):
        reads, writes = self._collect((), kw, first_is_write=False)
        self.emit(q, (lambda e, kw=kw: e.indirect_dma_start(**kw)), "sdma",
                  reads=reads, writes=writes)

    def op(self, engine, meth, *a, **kw):
        reads, writes = self._collect(a, kw)
        if meth == "matmul":
            reads = reads + writes
        self.emit(engine, (lambda e, a=a, kw=kw: getattr(e, meth)(*a, **kw)), "comp",
                  reads=reads, writes=writes)

    def v(self, meth, *a, **kw):
        self.op("vector", meth, *a, **kw)

    def a(self, meth, *a, **kw):
        self.op("scalar", meth, *a, **kw)

    def t(self, meth, *a, **kw):
        self.op("tensor", meth, *a, **kw)

    def g(self, meth, *a, **kw):
        self.op("gpsimd", meth, *a, **kw)

    def cc(self, kind, op, ins, outs):
        self.emit("gpsimd", (lambda e, k=kind, o=op, i=ins, u=outs:
                             e.collective_compute(k, o, replica_groups=[list(range(NCORES))],
                                                  ins=i, outs=u)), "cc",
                  reads=[self._region(x) for x in ins],
                  writes=[self._region(x) for x in outs])

    NLANES = 8

    def finish(self):
        nc = self.nc
        L = self.NLANES
        def step_of(base):
            return 16 if base in ("Dy", "Ds", "Dg") else 1

        ordinal = {}
        counts = {}
        qcount = {}
        last_access = {}
        req = []
        gate = {}
        barrier_req = {}
        for i, o in enumerate(self.ops):
            if o["kind"] == "barrier":
                req.append(None)
                barrier_req = dict(counts)
                last_access = {}
                continue
            if o["kind"] == "raw":
                req.append(dict(barrier_req))
                continue
            base = self._sem_of(o["engine"], o["kind"])
            if o["kind"] in ("dma", "sdma"):
                qn = qcount.get(base, 0)
                qcount[base] = qn + 1
                key = (base, qn % L)
            elif o["kind"] == "cc":
                key = (base, 0)
            else:
                key = (base, 0)
            myreq = dict(barrier_req)
            my_engine = o["engine"]
            my_comp = o["kind"] == "comp"

            def need(j):
                oj = self.ops[j]
                if my_comp and oj["kind"] == "comp" and oj["engine"] == my_engine == "tensor":
                    return
                k2, o2 = ordinal[j]
                myreq[k2] = max(myreq.get(k2, 0), o2 + 1)

            for (nm, lo, hi) in o["reads"]:
                for (l2, h2, w2, j) in last_access.get(nm, ()):
                    if w2 and l2 < hi and lo < h2:
                        need(j)
            for (nm, lo, hi) in o["writes"]:
                for (l2, h2, w2, j) in last_access.get(nm, ()):
                    if l2 < hi and lo < h2:
                        need(j)
            k = counts.get(key, 0)
            ordinal[i] = (key, k)
            counts[key] = k + 1
            if o["kind"] in ("dma", "sdma", "cc"):
                gate[i] = (key, k)
            req.append(myreq)
            for (nm, lo, hi) in o["writes"]:
                last_access.setdefault(nm, []).append((lo, hi, True, i))
            for (nm, lo, hi) in o["reads"]:
                last_access.setdefault(nm, []).append((lo, hi, False, i))
        totals = dict(counts)

        with self._es:
            sems = {key: self._es.enter_context(nc.semaphore(f"S{key[0]}{key[1]}"))
                    for key in totals}
            blk = self._es.enter_context(nc.Block())

            def section(eng_name):
                def body(eng):
                    hw = {key: 0 for key in totals}
                    def wait(key, k):
                        if k > hw[key]:
                            eng.wait_ge(sems[key], step_of(key[0]) * k)
                            hw[key] = k
                    for i, o in enumerate(self.ops):
                        if o["kind"] == "barrier" or o["engine"] != eng_name:
                            continue
                        if o["kind"] == "raw":
                            for key, k in (req[i] or {}).items():
                                wait(key, k)
                            o["fn"](eng)
                            continue
                        for key, k in req[i].items():
                            wait(key, k)
                        if i in gate:
                            wait(gate[i][0], gate[i][1])
                        inst = o["fn"](eng)
                        key = ordinal[i][0]
                        inst.then_inc(sems[key], step_of(key[0]))
                    if eng_name == "sync":
                        for key, tot in totals.items():
                            wait(key, tot)
                return body

            for name in ENGINES:
                getattr(blk, name)(section(name))
        return nc


def shard_rows(x, n=NCORES):
    x = np.ascontiguousarray(x)
    return np.split(x, n, axis=0)


def run(prog, in_maps, cores=None):
    cores = cores if cores is not None else list(range(len(in_maps)))
    res = run_bass_kernel_spmd(prog.nc, in_maps, core_ids=cores)
    return res.results



def emit_s1(p, g, x, xn_out, xnb_out, xnlo_out, rows):
    with p.scope():
        xt = p.sb(g + "xt", [128, D], F32)
        sq = p.sb(g + "sq", [128, D], F32)
        xnt = p.sb(g + "xnt", [128, D], F32)
        xnbt = p.sb(g + "xnbt", [128, D], BF16)
        xnlot = p.sb(g + "xnlot", [128, D], BF16)
        st = p.sb(g + "st", [128, 8], F32)
        nb = p.sb(g + "nb", [128, 1], F32)
        nt = rows // 128
        for i in range(nt):
            r = slice(i * 128, (i + 1) * 128)
            p.dma(xt[:], x[r, :])
            p.a("activation", sq[:], xt[:], AF.Square, accum_out=st[:, 1:2])
            p.v("tensor_reduce", st[:, 0:1], xt[:], AX, Op.add)
            p.v("tensor_scalar", st[:, 2:3], st[:, 0:1], 1.0 / D, None, Op.mult)
            p.v("tensor_scalar", st[:, 3:4], st[:, 1:2], 1.0 / D, None, Op.mult)
            p.v("tensor_tensor", st[:, 4:5], st[:, 2:3], st[:, 2:3], Op.mult)
            p.v("tensor_tensor", st[:, 5:6], st[:, 3:4], st[:, 4:5], Op.subtract)
            p.v("tensor_scalar", st[:, 5:6], st[:, 5:6], 1e-5, None, Op.add)
            p.a("activation", st[:, 6:7], st[:, 5:6], AF.Sqrt)
            p.v("reciprocal", st[:, 7:8], st[:, 6:7])
            p.v("tensor_scalar", nb[:], st[:, 2:3], -1.0, None, Op.mult)
            p.v("tensor_scalar", xnt[:], xt[:], nb[:], st[:, 7:8], Op.add, Op.mult)
            p.v("tensor_copy", xnbt[:], xnt[:])
            p.v("tensor_tensor", xnlot[:], xnt[:], xnbt[:], Op.subtract)
            p.dma(xn_out[r, :], xnt[:])
            p.dma(xnb_out[r, :], xnbt[:])
            p.dma(xnlo_out[r, :], xnlot[:])


def build_s1(rows):
    p = Prog()
    x = p.din("x", [rows, D], F32)
    xn_out = p.dout("xn", [rows, D], F32)
    xnb_out = p.dout("xnb", [rows, D], BF16)
    xnlo_out = p.dout("xnlo", [rows, D], BF16)
    emit_s1(p, "s1_", x, xn_out, xnb_out, xnlo_out, rows)
    p.finish()
    return p


def stage1(x_full):
    p = build_s1(TPC)
    shards = shard_rows(x_full.reshape(T, D))
    res = run(p, [{"x": s} for s in shards])
    xn = np.concatenate([r["xn"] for r in res], 0)
    xnb = np.concatenate([r["xnb"] for r in res], 0)
    xnlo = np.concatenate([r["xnlo"] for r in res], 0)
    return xn, xnb, xnlo





SQS = 1.0 / float(np.sqrt(np.float32(S)))
NEG = -1.0e9

def attn_masks():
    i = np.arange(128)[:, None]
    j = np.arange(512)[None, :]
    bm = np.stack([np.where(128 * pat + i >= j, 0.0, NEG) for pat in range(4)])
    am = np.stack([np.where(j < 128 * r + i, 0.0, NEG) for r in range(4)])
    return bm.astype(BF), am.astype(BF)

def emit_s2(p, g, xnb, Whead, bhead, bmask_in, amask_in, identb_in, identf_in, zout):
    with p.scope():
        wraw = p.sb(g + "wraw", [64, 3, D], F32)
        wbf = p.sb(g + "wbf", [64, 3, D], BF16)
        wT = p.sb(g + "wT", [128, 8, 3, 128], BF16)
        bqt = p.sb(g + "bqt", [128, 2], F32)
        bvr = p.sb(g + "bvr", [1, 128], F32)
        ones1 = p.sb(g + "ones1", [1, 128], F32)
        bmask = p.sb(g + "bmask_s", [128, 4, 512], BF16)
        amask = p.sb(g + "amask_s", [128, 4, 512], BF16)
        identb = p.sb(g + "identb_s", [128, 128], BF16)
        identf = p.sb(g + "identf_s", [128, 128], F32)
        xnT = p.sb(g + "xnT", [128, 2, 8, 512], BF16)
        qTs = p.sb(g + "qTs", [128, S], BF16)
        kTs = p.sb(g + "kTs", [128, S], BF16)
        vtok = p.sb(g + "vtok", [128, 2, 64, 65], BF16)
        zs = p.sb(g + "zs", [65, 512], F32)
        ztok = p.sb(g + "ztok", [128, 66], F32)
        escr = p.sb(g + "escr", [128, 2, 2, 512], BF16)
        pbuf = p.sb(g + "pbuf", [128, 2, 4, 512], BF16)
        dcols = p.sb(g + "dcols", [128, 2, 17], F32)
        dlow = p.sb(g + "dlow", [128, 2, 64], F32)
        rec = p.sb(g + "rec", [128, 1], F32)

        pq = [p.ps(g + f"pq{i}", [128, 512]) for i in range(4)]
        pz = [p.ps(g + f"pz{i}", [65, 512]) for i in range(2)]
        pj = p.ps(g + "pj", [128, 512])
        pvp = pq[3][:, 0:128]
        ptr = pj[:, 0:65]

        for aa in range(4):
            p.dma(bmask[:, aa, :], bmask_in[aa])
            p.dma(amask[:, aa, :], amask_in[aa])
        p.dma(identb[:], identb_in)
        p.dma(identf[:], identf_in)
        p.op("vector", "memset", ones1[:], 1.0)
        p.op("vector", "memset", vtok[:], 0.0)

        for jp in range(2):
            jA, jB = 2 * jp, 2 * jp + 1
            for jj, pos in ((jA, 0), (jB, 1)):
                for m in range(3):
                    p.dma(wraw[:, m, :], Whead[jj, m])
                p.v("tensor_copy", wbf[:], wraw[:])
                for m in range(3):
                    for dc in range(8):
                        p.dma(wT[:, dc, m, pos * 64:(pos + 1) * 64],
                              wbf[:, m, dc * 128:(dc + 1) * 128], q="scalar", transpose=True)
                p.dma(bqt[pos * 64:(pos + 1) * 64, 0:1], bhead[jj, 0].unsqueeze(1))
                p.dma(bqt[pos * 64:(pos + 1) * 64, 1:2], bhead[jj, 1].unsqueeze(1))
                p.dma(bvr[:, pos * 64:(pos + 1) * 64], bhead[jj, 2].unsqueeze(0))
            p.op("vector", "memset", vtok[:, :, :, 64:65], 1.0)
            for ch in range(16):
                trows = slice(ch * 512, (ch + 1) * 512)
                par = ch % 2
                for dc in range(8):
                    p.dma(xnT[:, par, dc, :], xnb[trows, dc * 128:(dc + 1) * 128],
                          q="scalar", transpose=True)
                for m, dst in ((0, qTs), (1, kTs)):
                    for dc in range(8):
                        p.t("matmul", pj[:], wT[:, dc, m, :], xnT[:, par, dc, :],
                            start=(dc == 0), stop=(dc == 7))
                    p.v("tensor_scalar", dst[:, ch * 512:(ch + 1) * 512], pj[:],
                        bqt[:, m:m + 1], None, Op.add)
                for sub in range(4):
                    kt = ch * 4 + sub
                    for dc in range(8):
                        p.t("matmul", pvp, xnT[:, par, dc, sub * 128:(sub + 1) * 128],
                            wT[:, dc, 2, :], start=(dc == 0), stop=False)
                    p.t("matmul", pvp, ones1[:], bvr[:], start=False, stop=True)
                    p.v("tensor_copy", vtok[:, 0, kt, 0:64], pvp[:, 0:64])
                    p.v("tensor_copy", vtok[:, 1, kt, 0:64], pvp[:, 64:128])
            aw = []
            for qt in range(64):
                for ks in range(qt // 4 + 1):
                    aw.append((qt, ks))

            def emit_qk_A(qt, ks):
                nsup = qt // 4
                par = ks % 2
                if ks == nsup:
                    rr = qt % 4
                    p.t("matmul", pq[2 * par][:], identb[:], amask[:, rr, :], start=True, stop=False)
                    p.t("matmul", pq[2 * par + 1][:], identb[:], amask[:, rr, :], start=True, stop=False)
                    st0 = False
                else:
                    st0 = True
                p.t("matmul", pq[2 * par][:], qTs[0:64, qt * 128:(qt + 1) * 128],
                    kTs[0:64, ks * 512:(ks + 1) * 512], start=st0, stop=True,
                    tile_position=(0, 0))
                p.t("matmul", pq[2 * par + 1][:], qTs[64:128, qt * 128:(qt + 1) * 128],
                    kTs[64:128, ks * 512:(ks + 1) * 512], start=st0, stop=True,
                    tile_position=(64, 0))

            emit_qk_A(*aw[0])
            for i, (qt, ks) in enumerate(aw):
                if i + 1 < len(aw):
                    emit_qk_A(*aw[i + 1])
                par = ks % 2
                p.a("activation", escr[:, 0, ks % 2, :], pq[2 * par][:], AF.Exp, scale=SQS,
                    accum_out=dcols[:, 0, ks:ks + 1])
                p.a("activation", escr[:, 1, ks % 2, :], pq[2 * par + 1][:], AF.Exp, scale=SQS,
                    accum_out=dcols[:, 1, ks:ks + 1])
                if ks == qt // 4:
                    for hh in range(2):
                        p.v("tensor_reduce", dlow[:, hh, qt:qt + 1], dcols[:, hh, 0:qt // 4 + 1], AX, Op.add)
            for qs in range(16):
                kts = list(range(4 * qs, 64))

                def emit_kq_B(kt, qs=qs):
                    pat = kt - 4 * qs
                    par = kt % 2
                    if pat <= 3:
                        p.t("matmul", pq[2 * par][:], identb[:], bmask[:, pat, :], start=True, stop=False)
                        p.t("matmul", pq[2 * par + 1][:], identb[:], bmask[:, pat, :], start=True, stop=False)
                    p.t("matmul", pq[2 * par][:], kTs[0:64, kt * 128:(kt + 1) * 128],
                        qTs[0:64, qs * 512:(qs + 1) * 512], start=(pat > 3), stop=True,
                        tile_position=(0, 0))
                    p.t("matmul", pq[2 * par + 1][:], kTs[64:128, kt * 128:(kt + 1) * 128],
                        qTs[64:128, qs * 512:(qs + 1) * 512], start=(pat > 3), stop=True,
                        tile_position=(64, 0))

                emit_kq_B(kts[0])
                for i, kt in enumerate(kts):
                    if i + 1 < len(kts):
                        emit_kq_B(kts[i + 1])
                    par = kt % 2
                    par4 = kt % 4
                    p.a("activation", pbuf[:, 0, par4, :], pq[2 * par][:], AF.Exp, scale=SQS)
                    p.a("activation", pbuf[:, 1, par4, :], pq[2 * par + 1][:], AF.Exp, scale=SQS)
                    p.t("matmul", pz[0][:], vtok[:, 0, kt, :], pbuf[:, 0, par4, :],
                        start=(kt == 4 * qs), stop=(kt == 63))
                    p.t("matmul", pz[1][:], vtok[:, 1, kt, :], pbuf[:, 1, par4, :],
                        start=(kt == 4 * qs), stop=(kt == 63))
                for hh, jj in ((0, jA), (1, jB)):
                    p.v("tensor_copy", zs[:], pz[hh][:])
                    for q4 in range(4):
                        qt = qs * 4 + q4
                        p.t("transpose", ptr, zs[:, q4 * 128:(q4 + 1) * 128], identf[0:65, 0:65])
                        p.v("tensor_copy", ztok[:, 0:65], ptr)
                        p.v("tensor_tensor", ztok[:, 65:66], ztok[:, 64:65], dlow[:, hh, qt:qt + 1], Op.add)
                        p.v("reciprocal", rec[:], ztok[:, 65:66])
                        p.v("tensor_scalar", ztok[:, 0:64], ztok[:, 0:64], rec[:], None, Op.mult)
                        p.dma(zout[jj, qt * 128:(qt + 1) * 128, :], ztok[:, 0:64])

def build_s2():
    p = Prog()
    xnb = p.din("xnb", [S, D], BF16)
    Whead = p.din("Whead", [4, 3, 64, D], F32)
    bhead = p.din("bhead", [4, 3, 64], F32)
    bmask_in = p.din("bmask", [4, 128, 512], BF16)
    amask_in = p.din("amask", [4, 128, 512], BF16)
    identb_in = p.din("identb", [128, 128], BF16)
    identf_in = p.din("identf", [128, 128], F32)
    zout = p.dout("z4", [4, S, 64], F32)
    emit_s2(p, "s2_", xnb, Whead, bhead, bmask_in, amask_in, identb_in, identf_in, zout)
    p.finish()
    return p

def stage2(xnb_full, Wq, bq, Wk, bk, Wv, bv):
    bm, am = attn_masks()
    identb = np.eye(128).astype(BF)
    identf = np.eye(128, dtype=np.float32)
    xnb3 = xnb_full.reshape(B, S, D)
    in_maps = []
    for c in range(NCORES):
        hs = [(c * 4 + j) % 16 for j in range(4)]
        b = (c * 4) // 16
        Wh = np.stack([np.stack([W[h * 64:(h + 1) * 64] for W in (Wq, Wk, Wv)]) for h in hs])
        bh_ = np.stack([np.stack([bb[h * 64:(h + 1) * 64] for bb in (bq, bk, bv)]) for h in hs])
        in_maps.append({"xnb": np.ascontiguousarray(xnb3[b]), "Whead": np.ascontiguousarray(Wh),
                        "bhead": np.ascontiguousarray(bh_), "bmask": bm, "amask": am,
                        "identb": identb, "identf": identf})
    p = build_s2()
    res = run(p, in_maps)
    z = np.zeros((B, S, 16, 64), np.float32)
    for c in range(NCORES):
        b = (c * 4) // 16
        for j in range(4):
            h = (c * 4 + j) % 16
            z[b, :, h, :] = res[c]["z4"][j]
    return z.reshape(B, S, D)




def emit_s3(p, g, xn, z16, gam, bet, s_out, rows):
    with p.scope():
        xt = p.sb(g + "xt", [128, D], F32)
        zt = p.sb(g + "zt", [128, D], F32)
        sq = p.sb(g + "sq", [128, D], F32)
        ot = p.sb(g + "ot", [128, D], F32)
        st = p.sb(g + "st", [128, 8], F32)
        nb = p.sb(g + "nb", [128, 1], F32)
        grow = p.sb(g + "grow", [1, 2, D], F32)
        gbc = p.sb(g + "gbc", [128, 2, D], F32)
        pb = p.ps(g + "pb", [128, 512])
        ones1 = p.sb(g + "ones1", [1, 128], F32)
        p.op("vector", "memset", ones1[:], 1.0)
        p.dma(grow[:, 0, :], gam.unsqueeze(0))
        p.dma(grow[:, 1, :], bet.unsqueeze(0))
        for m in range(2):
            for hf in range(2):
                p.t("matmul", pb[:], ones1[:], grow[:, m, hf * 512:(hf + 1) * 512], start=True, stop=True)
                p.v("tensor_copy", gbc[:, m, hf * 512:(hf + 1) * 512], pb[:])
        nt = rows // 128
        for i in range(nt):
            r = slice(i * 128, (i + 1) * 128)
            p.dma(xt[:], xn[r, :])
            for h16 in range(16):
                p.dma(zt[:, h16 * 64:(h16 + 1) * 64], z16[h16, r, :])
            p.v("tensor_tensor", xt[:], xt[:], zt[:], Op.add)
            p.a("activation", sq[:], xt[:], AF.Square, accum_out=st[:, 1:2])
            p.v("tensor_reduce", st[:, 0:1], xt[:], AX, Op.add)
            p.v("tensor_scalar", st[:, 2:3], st[:, 0:1], 1.0 / D, None, Op.mult)
            p.v("tensor_scalar", st[:, 3:4], st[:, 1:2], 1.0 / D, None, Op.mult)
            p.v("tensor_tensor", st[:, 4:5], st[:, 2:3], st[:, 2:3], Op.mult)
            p.v("tensor_tensor", st[:, 5:6], st[:, 3:4], st[:, 4:5], Op.subtract)
            p.v("tensor_scalar", st[:, 5:6], st[:, 5:6], 1e-5, None, Op.add)
            p.a("activation", st[:, 6:7], st[:, 5:6], AF.Sqrt)
            p.v("reciprocal", st[:, 7:8], st[:, 6:7])
            p.v("tensor_scalar", nb[:], st[:, 2:3], -1.0, None, Op.mult)
            p.v("tensor_scalar", ot[:], xt[:], nb[:], st[:, 7:8], Op.add, Op.mult)
            p.v("tensor_tensor", ot[:], ot[:], gbc[:, 0, :], Op.mult)
            p.v("tensor_tensor", ot[:], ot[:], gbc[:, 1, :], Op.add)
            p.dma(s_out[r, :], ot[:])


def build_s3(rows):
    p = Prog()
    xn = p.din("xn", [rows, D], F32)
    zi = p.din("z", [16, rows, 64], F32)
    gam = p.din("gamma", [D], F32)
    bet = p.din("beta", [D], F32)
    s_out = p.dout("s0", [rows, D], F32)
    emit_s3(p, "s3_", xn, zi, gam, bet, s_out, rows)
    p.finish()
    return p


def stage3(xn, z, gamma, beta):
    p = build_s3(TPC)
    xs = shard_rows(xn.reshape(T, D))
    z4 = z.reshape(B, S, 16, 64)
    zsh = [np.ascontiguousarray(z4[c // 4, (c % 4) * TPC:(c % 4 + 1) * TPC].transpose(1, 0, 2))
           for c in range(NCORES)]
    res = run(p, [{"xn": a, "z": b, "gamma": gamma, "beta": beta} for a, b in zip(xs, zsh)])
    return np.concatenate([r["s0"] for r in res], 0)



def build_s4(NTT=T // 128):
    p = Prog()
    xnb = p.din("xnb", [T, D], BF16)
    xnlo = p.din("xnlo", [T, D], BF16)
    gw = p.din("gw", [64, D], F32)
    iota_tok = p.din("iota_tok", [128, 64], F32)
    idin = p.din("identf", [128, 128], F32)
    table = p.dout("table", [T, 1], U32)
    gbidx = p.dout("gbidx", [T, 4], U32)
    wts_o = p.dout("wts", [T, 4], F32)

    gwraw = p.sb("gwraw", [64, D], F32)
    gwbf = p.sb("gwbf", [64, D], BF16)
    gwlosc = p.sb("gwlosc", [64, D], F32)
    gwlo = p.sb("gwlo", [64, D], BF16)
    gwT = p.sb("gwT", [128, 8, 64], BF16)
    gwloT = p.sb("gwloT", [128, 8, 64], BF16)
    xnT = p.sb("xnT", [128, 8, 128], BF16)
    xnloT = p.sb("xnloT", [128, 8, 128], BF16)
    lwork = p.sb("lwork", [128, 64], F32)
    iot = p.sb("iot", [128, 64], F32)
    oh = p.sb("oh", [128, 64], F32)
    mx = p.sb("mx", [128, 4], F32)
    idx_all = p.sb("idx_all", [128, T // 128, 4], F32)
    ew = p.sb("ew", [128, 8], F32)
    wtt = p.sb("wtt", [128, 4], F32)
    selT_all = p.sb("selT_all", [64, T], F32)
    zerT = p.sb("zerT", [64, 2048], F32)
    posT = p.sb("posT", [64, T], F32)
    carry = p.sb("carry", [64, 1], F32)
    sel = p.sb("sel", [128, 64], F32)
    pos_tok = p.sb("pos_tok", [128, 64], F32)
    poscol = p.sb("poscol", [128, 4], F32)
    base = p.sb("base", [128, 1], F32)
    km = p.sb("km", [128, 1], F32)
    ge = p.sb("ge", [128, 1], F32)
    t1 = p.sb("t1", [128, 1], F32)
    gidxf = p.sb("gidxf", [128, 1], F32)
    gidxu = p.sb("gidxu", [128, 1], U32)
    gbu = p.sb("gbu", [128, 4], U32)
    tid = p.sb("tid", [128, 1], U32)
    identf = p.sb("identf2", [128, 128], F32)
    zl = p.sb("zl", [128, 128], U32)
    pl = p.ps("pl", [128, 64])
    pt2 = p.ps("pt2", [64, 128])
    ptp = p.ps("ptp", [128, 64])

    p.dma(identf[:], idin)
    p.dma(iot[:], iota_tok)
    p.op("vector", "memset", zerT[:], 0.0)
    p.op("vector", "memset", zl[:], 0)
    p.dma(table.rearrange("(q a) x -> q (a x)", q=128), zl[:])
    p.dma(gwraw[:], gw)
    p.v("tensor_copy", gwbf[:], gwraw[:])
    p.v("tensor_tensor", gwlosc[:], gwraw[:], gwbf[:], Op.subtract)
    p.v("tensor_copy", gwlo[:], gwlosc[:])
    for dc in range(8):
        p.dma(gwT[:, dc, :], gwbf[:, dc * 128:(dc + 1) * 128], q="scalar", transpose=True)
        p.dma(gwloT[:, dc, :], gwlo[:, dc * 128:(dc + 1) * 128], q="scalar", transpose=True)

    breg = p.nc.gpsimd.alloc_register()
    p.raw("gpsimd", lambda e: e.reg_mov(breg, T - 1))
    for tt in range(NTT):
        r = slice(tt * 128, (tt + 1) * 128)
        for dc in range(8):
            p.dma(xnT[:, dc, :], xnb[r, dc * 128:(dc + 1) * 128], q="scalar", transpose=True)
            p.dma(xnloT[:, dc, :], xnlo[r, dc * 128:(dc + 1) * 128], q="scalar", transpose=True)
        for dc in range(8):
            p.t("matmul", pl[:], xnT[:, dc, :], gwT[:, dc, :], start=(dc == 0), stop=False)
            p.t("matmul", pl[:], xnT[:, dc, :], gwloT[:, dc, :], start=False, stop=False)
            p.t("matmul", pl[:], xnloT[:, dc, :], gwT[:, dc, :], start=False, stop=(dc == 7))
        p.v("tensor_copy", lwork[:], pl[:])
        p.op("vector", "memset", sel[:], 0.0)
        for k in range(4):
            p.v("tensor_reduce", mx[:, k:k + 1], lwork[:], AX, Op.max)
            p.v("tensor_scalar", oh[:], lwork[:], mx[:, k:k + 1], None, Op.is_equal)
            p.v("scalar_tensor_tensor", lwork[:], oh[:], -1e30, lwork[:], Op.mult, Op.add)
            p.v("tensor_tensor", sel[:], sel[:], oh[:], Op.add)
            p.v("tensor_tensor", oh[:], oh[:], iot[:], Op.mult)
            p.v("tensor_reduce", idx_all[:, tt, k:k + 1], oh[:], AX, Op.add)
        p.v("tensor_scalar", ew[:, 6:7], mx[:, 0:1], -1.0, None, Op.mult)
        p.v("tensor_scalar", ew[:, 0:4], mx[:], ew[:, 6:7], None, Op.add)
        p.a("activation", ew[:, 0:4], ew[:, 0:4], AF.Exp)
        p.v("tensor_reduce", ew[:, 4:5], ew[:, 0:4], AX, Op.add)
        p.v("reciprocal", ew[:, 5:6], ew[:, 4:5])
        p.v("tensor_scalar", wtt[:], ew[:, 0:4], ew[:, 5:6], None, Op.mult)
        p.dma(wts_o[r, :], wtt[:])
        p.t("transpose", pt2[:], sel[:, 0:64], identf[:])
        p.v("tensor_copy", selT_all[:, r], pt2[:])
    p.op("vector", "memset", carry[:], 0.0)
    for ch in range(T // 2048):
        r = slice(ch * 2048, (ch + 1) * 2048)
        p.v("tensor_tensor_scan", posT[:, r], zerT[:], selT_all[:, r], carry[:], Op.add, Op.add)
        p.v("tensor_copy", carry[:], posT[:, ch * 2048 + 2047:ch * 2048 + 2048])
    for tt in range(NTT):
        r = slice(tt * 128, (tt + 1) * 128)
        p.t("transpose", ptp[:], posT[:, r], identf[0:64, 0:64])
        p.v("tensor_copy", pos_tok[:, 0:64], ptp[:])
        p.g("iota", tid[:], pattern=[[0, 1]], base=tt * 128, channel_multiplier=1)
        for k in range(4):
            p.v("tensor_scalar", oh[:], iot[:], idx_all[:, tt, k:k + 1], None, Op.is_equal)
            p.v("tensor_tensor", oh[:], oh[:], pos_tok[:, 0:64], Op.mult)
            p.v("tensor_reduce", poscol[:, k:k + 1], oh[:], AX, Op.add)
        for k in range(4):
            p.v("tensor_scalar", poscol[:, k:k + 1], poscol[:, k:k + 1], -1.0, None, Op.add)
            p.v("tensor_scalar", base[:], idx_all[:, tt, k:k + 1], 256.0, None, Op.mult)
            p.v("tensor_tensor", base[:], base[:], poscol[:, k:k + 1], Op.add)
            p.v("tensor_scalar", ge[:], poscol[:, k:k + 1], 256.0, None, Op.is_ge)
            p.v("scalar_tensor_tensor", gidxf[:], ge[:], 100000.0, base[:], Op.mult, Op.add)
            p.v("tensor_copy", gidxu[:], gidxf[:])
            p.idma("gpsimd", out=table, out_offset=IndirectOffsetOnAxis(ap=gidxu[:], axis=0),
                   in_=tid[:], in_offset=None, bounds_check=breg, oob_is_err=False)
            p.v("tensor_scalar", km[:], poscol[:, k:k + 1], 256.0, None, Op.is_lt)
            p.v("tensor_tensor", t1[:], base[:], km[:], Op.mult)
            p.v("scalar_tensor_tensor", gidxf[:], ge[:], float(T), t1[:], Op.mult, Op.add)
            p.v("tensor_copy", gbu[:, k:k + 1], gidxf[:])
        p.dma(gbidx[r, :], gbu[:])
    p.finish()
    return p

def stage4(xnb_full, xnlo_full, gate_w, ntt=T // 128):
    p = build_s4(ntt)
    iota_tok = np.broadcast_to(np.arange(64, dtype=np.float32), (128, 64)).copy()
    identf = np.eye(128, dtype=np.float32)
    im = [{"xnb": xnb_full, "xnlo": xnlo_full, "gw": gate_w, "iota_tok": iota_tok, "identf": identf}
          for _ in range(NCORES)]
    res = run(p, im)
    r = res[0]
    return r["table"][:, 0], r["gbidx"], r["wts"]





def emit_dequant(p, c_t, scexp, t0, t1, t2, dec, W):
    for nib in (0, 1):
        if nib == 0:
            p.v("tensor_scalar", t1[:, 0:W], c_t[:], 7, 22, Op.bitwise_and, Op.logical_shift_left)
            p.v("tensor_scalar", t0[:, 0:W], c_t[:], 8, 28, Op.bitwise_and, Op.logical_shift_left)
        else:
            p.v("tensor_scalar", t1[:, 0:W], c_t[:], 4, 7, Op.logical_shift_right, Op.bitwise_and)
            p.v("tensor_scalar", t1[:, 0:W], t1[:, 0:W], 22, None, Op.logical_shift_left)
            p.v("tensor_scalar", t0[:, 0:W], c_t[:], 128, 24, Op.bitwise_and, Op.logical_shift_left)
        p.v("tensor_scalar", t1[:, 0:W], t1[:, 0:W], 0x3F000000, None, Op.add)
        p.v("tensor_tensor", t0[:, 0:W], t0[:, 0:W], scexp[:, 0:W].bitcast(I32), Op.bitwise_or)
        f0 = t1[:, 0:W].bitcast(F32)
        p.v("tensor_scalar", t2[:, 0:W], f0, 1.0, -1.0, Op.min, Op.add)
        p.v("tensor_tensor", t2[:, 0:W], t2[:, 0:W], f0, Op.add)
        p.v("tensor_tensor", dec[:, nib:2 * W:2], t2[:, 0:W], t0[:, 0:W].bitcast(F32), Op.mult)

def emit_scale_expand(p, scl, scexp, G8):
    p.v("tensor_copy", scexp[:, 0:G8 * 64].rearrange("p (g k) -> p g k", k=64),
        scl[:, 0:G8].unsqueeze(2).broadcast_to([128, G8, 64]))

def emit_s5(p, g, xnb, tbl, gp, gs, up, us, dp, ds, idb_in, eo_out):
    with p.scope():
        ixt = p.sb(g + "ixt", [128, 2], U32)
        buftok = p.sb(g + "buftok", [128, 2, D], BF16)
        bufT = p.sb(g + "bufT", [128, 8, 256], BF16)
        identb = p.sb(g + "identb_s", [128, 128], BF16)
        pk = p.sb(g + "pk", [128, 512], I32)
        scl = p.sb(g + "scl", [128, 8], F32)
        scexp = p.sb(g + "scexp", [128, 512], F32)
        t0 = p.sb(g + "t0", [128, 512], I32)
        t1 = p.sb(g + "t1", [128, 512], I32)
        t2 = p.sb(g + "t2", [128, 512], F32)
        dec = p.sb(g + "dec", [128, 1024], BF16)
        wgT = p.sb(g + "wgT", [128, 8, 768], BF16)
        wuT = p.sb(g + "wuT", [128, 8, 768], BF16)
        wdT = p.sb(g + "wdT", [128, 6, 1024], BF16)
        gsl = p.sb(g + "gsl", [128, 256], F32)
        hsb = p.sb(g + "hsb", [128, 6, 256], BF16)
        eos = p.sb(g + "eos", [128, 1024], BF16)
        ptb = p.ps(g + "ptb", [128, 128], BF16)
        pg = p.ps(g + "pg", [128, 256])
        pu = p.ps(g + "pu", [128, 256])
        pdn = p.ps(g + "pdn", [128, 512])

        p.dma(identb[:], idb_in)
        for ee in range(8):
            for h in range(2):
                p.dma(ixt[:, h:h + 1], tbl[ee, h * 128:(h + 1) * 128].unsqueeze(1))
            for h in range(2):
                p.idma("gpsimd", out=buftok[:, h, :], out_offset=None, in_=xnb,
                       in_offset=IndirectOffsetOnAxis(ap=ixt[:, h:h + 1], axis=0))
            for h in range(2):
                for dc in range(8):
                    p.t("transpose", ptb[:], buftok[:, h, dc * 128:(dc + 1) * 128], identb[:])
                    p.v("tensor_copy", bufT[:, dc, h * 128:(h + 1) * 128], ptb[:])
            for (pkd, sdd, wdst) in ((gp, gs, wgT), (up, us, wuT)):
                for ic in range(6):
                    p.dma(pk[:], pkd[ee, ic * 128:(ic + 1) * 128, :])
                    p.dma(scl[:], sdd[ee, ic * 128:(ic + 1) * 128, :])
                    emit_scale_expand(p, scl, scexp, 8)
                    emit_dequant(p, pk[:], scexp, t0, t1, t2, dec, 512)
                    for dc in range(8):
                        p.t("transpose", ptb[:], dec[:, dc * 128:(dc + 1) * 128], identb[:])
                        p.v("tensor_copy", wdst[:, dc, ic * 128:(ic + 1) * 128], ptb[:])
            for dc in range(8):
                p.dma(pk[:, 0:384], dp[ee, dc * 128:(dc + 1) * 128, :])
                p.dma(scl[:, 0:6], ds[ee, dc * 128:(dc + 1) * 128, :])
                emit_scale_expand(p, scl, scexp, 6)
                emit_dequant(p, pk[:, 0:384], scexp, t0, t1, t2, dec, 384)
                for ic in range(6):
                    p.t("transpose", ptb[:], dec[:, ic * 128:(ic + 1) * 128], identb[:])
                    p.v("tensor_copy", wdT[:, ic, dc * 128:(dc + 1) * 128], ptb[:])
            for ic in range(6):
                for dc in range(8):
                    p.t("matmul", pg[:], wgT[:, dc, ic * 128:(ic + 1) * 128], bufT[:, dc, :],
                        start=(dc == 0), stop=(dc == 7))
                for dc in range(8):
                    p.t("matmul", pu[:], wuT[:, dc, ic * 128:(ic + 1) * 128], bufT[:, dc, :],
                        start=(dc == 0), stop=(dc == 7))
                p.a("activation", gsl[:], pg[:], AF.Silu)
                p.v("tensor_tensor", hsb[:, ic, :], gsl[:], pu[:], Op.mult)
            for c2 in range(2):
                for dh in range(2):
                    for ic in range(6):
                        p.t("matmul", pdn[:], hsb[:, ic, c2 * 128:(c2 + 1) * 128],
                            wdT[:, ic, dh * 512:(dh + 1) * 512], start=(ic == 0), stop=(ic == 5))
                    p.v("tensor_copy", eos[:, dh * 512:(dh + 1) * 512], pdn[:])
                p.dma(eo_out[ee, c2 * 128:(c2 + 1) * 128, :], eos[:])


def build_s5():
    p = Prog()
    xnb = p.din("xnb", [T, D], BF16)
    tbl = p.din("tbl", [8, 256], U32)
    gp = p.din("gp", [8, 768, 512], I32)
    gs = p.din("gs", [8, 768, 8], F32)
    up = p.din("up", [8, 768, 512], I32)
    us = p.din("us", [8, 768, 8], F32)
    dp = p.din("dp", [8, 1024, 384], I32)
    ds = p.din("ds", [8, 1024, 6], F32)
    idb_in = p.din("identb", [128, 128], BF16)
    eo_out = p.dout("eo", [8, 256, D], BF16)
    emit_s5(p, "s5_", xnb, tbl, gp, gs, up, us, dp, ds, idb_in, eo_out)
    p.finish()
    return p


def stage5(xnb_full, table, packs):
    gp, gs_, up, us, dp, ds = packs
    p = build_s5()
    idb = np.eye(128).astype(BF)
    t64 = table.reshape(64, 256).astype(np.uint32)
    im = []
    for c in range(NCORES):
        sl = slice(c * 8, (c + 1) * 8)
        im.append({"xnb": xnb_full, "tbl": np.ascontiguousarray(t64[sl]),
                   "gp": np.ascontiguousarray(gp[sl]), "gs": np.ascontiguousarray(gs_[sl]),
                   "up": np.ascontiguousarray(up[sl]), "us": np.ascontiguousarray(us[sl]),
                   "dp": np.ascontiguousarray(dp[sl]), "ds": np.ascontiguousarray(ds[sl]),
                   "identb": idb})
    res = run(p, im)
    eo = np.concatenate([r["eo"] for r in res], 0)
    return eo





def emit_s6(p, g, eoall, gbix, wtsi, xi, s0i, xnbr, sgp, sgs, sup, sus, sdp, sds, idb_in, x1_out, rows):
    with p.scope():
        identb = p.sb(g + "identb_s", [128, 128], BF16)
        pk = p.sb(g + "pk", [128, 512], I32)
        scl = p.sb(g + "scl", [128, 8], F32)
        scexp = p.sb(g + "scexp", [128, 512], F32)
        t0 = p.sb(g + "t0", [128, 512], I32)
        t1 = p.sb(g + "t1", [128, 512], I32)
        t2 = p.sb(g + "t2", [128, 512], F32)
        dec = p.sb(g + "dec", [128, 1024], BF16)
        wsgT = p.sb(g + "wsgT", [128, 8, 768], BF16)
        wsuT = p.sb(g + "wsuT", [128, 8, 768], BF16)
        wsdT = p.sb(g + "wsdT", [128, 6, 1024], BF16)
        xnT = p.sb(g + "xnT", [128, 8, 512], BF16)
        gsl = p.sb(g + "gsl", [128, 512], F32)
        hshT = p.sb(g + "hshT", [128, 6, 512], BF16)
        shseg = p.sb(g + "shseg", [128, 512], BF16)
        sh_tok = p.sb(g + "sh_tok", [128, 4, D], BF16)
        acc = p.sb(g + "acc", [128, D], F32)
        gbuf = p.sb(g + "gbuf", [128, D], BF16)
        ixc = p.sb(g + "ixc", [128, 4], U32)
        wtc = p.sb(g + "wtc", [128, 4], F32)
        g0t = p.sb(g + "g0t", [128, D], F32)
        xt = p.sb(g + "xt", [128, D], F32)
        s0t = p.sb(g + "s0t", [128, D], F32)
        ptb = p.ps(g + "ptb", [128, 128], BF16)
        pg = p.ps(g + "pg", [128, 512])
        pu = p.ps(g + "pu", [128, 512])
        pdn = p.ps(g + "pdn", [128, 512])

        p.dma(identb[:], idb_in)
        for (pkd, sdd, wdst) in ((sgp, sgs, wsgT), (sup, sus, wsuT)):
            for ic in range(6):
                p.dma(pk[:], pkd[ic * 128:(ic + 1) * 128, :])
                p.dma(scl[:], sdd[ic * 128:(ic + 1) * 128, :])
                emit_scale_expand(p, scl, scexp, 8)
                emit_dequant(p, pk[:], scexp, t0, t1, t2, dec, 512)
                for dc in range(8):
                    p.t("transpose", ptb[:], dec[:, dc * 128:(dc + 1) * 128], identb[:])
                    p.v("tensor_copy", wdst[:, dc, ic * 128:(ic + 1) * 128], ptb[:])
        for dc in range(8):
            p.dma(pk[:, 0:384], sdp[dc * 128:(dc + 1) * 128, :])
            p.dma(scl[:, 0:6], sds[dc * 128:(dc + 1) * 128, :])
            emit_scale_expand(p, scl, scexp, 6)
            emit_dequant(p, pk[:, 0:384], scexp, t0, t1, t2, dec, 384)
            for ic in range(6):
                p.t("transpose", ptb[:], dec[:, ic * 128:(ic + 1) * 128], identb[:])
                p.v("tensor_copy", wsdT[:, ic, dc * 128:(dc + 1) * 128], ptb[:])
        nch = rows // 512
        for ch in range(nch):
            trows = slice(ch * 512, (ch + 1) * 512)
            for dc in range(8):
                p.dma(xnT[:, dc, :], xnbr[trows, dc * 128:(dc + 1) * 128], q="scalar", transpose=True)
            for ic in range(6):
                for dc in range(8):
                    p.t("matmul", pg[:], wsgT[:, dc, ic * 128:(ic + 1) * 128], xnT[:, dc, :],
                        start=(dc == 0), stop=(dc == 7))
                for dc in range(8):
                    p.t("matmul", pu[:], wsuT[:, dc, ic * 128:(ic + 1) * 128], xnT[:, dc, :],
                        start=(dc == 0), stop=(dc == 7))
                p.a("activation", gsl[:], pg[:], AF.Silu)
                p.v("tensor_tensor", hshT[:, ic, :], gsl[:], pu[:], Op.mult)
            for dc8 in range(8):
                for ic in range(6):
                    p.t("matmul", pdn[:], wsdT[:, ic, dc8 * 128:(dc8 + 1) * 128], hshT[:, ic, :],
                        start=(ic == 0), stop=(ic == 5))
                p.v("tensor_copy", shseg[:], pdn[:])
                for sub in range(4):
                    p.t("transpose", ptb[:], shseg[:, sub * 128:(sub + 1) * 128], identb[:])
                    p.v("tensor_copy", sh_tok[:, sub, dc8 * 128:(dc8 + 1) * 128], ptb[:])
            for sub in range(4):
                tt = ch * 4 + sub
                r = slice(tt * 128, (tt + 1) * 128)
                p.v("tensor_copy", acc[:], sh_tok[:, sub, :])
                p.dma(ixc[:], gbix[r, :])
                p.dma(wtc[:], wtsi[r, :])
                for k in range(4):
                    p.idma("gpsimd", out=gbuf[:], out_offset=None, in_=eoall,
                           in_offset=IndirectOffsetOnAxis(ap=ixc[:, k:k + 1], axis=0))
                    p.v("scalar_tensor_tensor", acc[:], gbuf[:], wtc[:, k:k + 1], acc[:], Op.mult, Op.add)
                p.a("activation", g0t[:], acc[:], AF.Silu)
                p.dma(xt[:], xi[r, :])
                p.dma(s0t[:], s0i[r, :])
                p.v("tensor_tensor", g0t[:], g0t[:], s0t[:], Op.mult)
                p.v("tensor_tensor", g0t[:], g0t[:], xt[:], Op.add)
                p.dma(x1_out[r, :], g0t[:])


def build_s6(rows):
    p = Prog()
    eoall = p.din("eoall", [T + 1, D], BF16)
    gbix = p.din("gbix", [rows, 4], U32)
    wtsi = p.din("wtsi", [rows, 4], F32)
    xi = p.din("x", [rows, D], F32)
    s0i = p.din("s0", [rows, D], F32)
    xnbr = p.din("xnbr", [rows, D], BF16)
    sgp = p.din("sgp", [768, 512], I32)
    sgs = p.din("sgs", [768, 8], F32)
    sup = p.din("sup", [768, 512], I32)
    sus = p.din("sus", [768, 8], F32)
    sdp = p.din("sdp", [1024, 384], I32)
    sds = p.din("sds", [1024, 6], F32)
    idb_in = p.din("identb", [128, 128], BF16)
    x1_out = p.dout("x1", [rows, D], F32)
    emit_s6(p, "s6_", eoall, gbix, wtsi, xi, s0i, xnbr, sgp, sgs, sup, sus, sdp, sds, idb_in, x1_out, rows)
    p.finish()
    return p


def stage6(eo, gbidx, wts, x, s0, xnb, shpacks):
    sgp, sgs, sup, sus, sdp, sds = shpacks
    eoall = np.concatenate([eo.reshape(T, D), np.zeros((1, D), eo.dtype)], 0)
    p = build_s6(TPC)
    idb = np.eye(128).astype(BF)
    gsh = shard_rows(gbidx); wsh = shard_rows(wts)
    xsh = shard_rows(x.reshape(T, D)); ssh = shard_rows(s0.reshape(T, D)); bsh = shard_rows(xnb.reshape(T, D))
    im = [{"eoall": eoall, "gbix": gsh[c], "wtsi": wsh[c], "x": xsh[c], "s0": ssh[c],
           "xnbr": bsh[c], "sgp": sgp, "sgs": sgs, "sup": sup, "sus": sus,
           "sdp": sdp, "sds": sds, "identb": idb} for c in range(NCORES)]
    res = run(p, im)
    return np.concatenate([r["x1"] for r in res], 0)





def emit_s8(p, g, xb, Wh, bac, adt, lnw_in, onw_in, msu_in, miu_in, idf_in, idb_in, og_out):
    with p.scope():
        NC = S // 128
        identf = p.sb(g + "identf_s", [128, 128], F32)
        identb = p.sb(g + "identb_s", [128, 128], BF16)
        msu = p.sb(g + "msu_s", [128, 128], F32)
        miu = p.sb(g + "miu_s", [128, 128], F32)
        ones1 = p.sb(g + "ones1", [1, 128], F32)
        onwB = p.sb(g + "onwB", [128, 128], F32)
        lnwt = p.sb(g + "lnwt", [128, 8], F32)
        pmm2 = [[p.ps(g + f"pmm{j}_{cp}", [128, 128]) for cp in range(2)] for j in range(2)]
        pmm = [pmm2[0][0], pmm2[1][0]]
        pA = [p.ps(g + f"pA{j}", [128, 128]) for j in range(2)]
        pB = pA
        pxs = [p.ps(g + f"px{j}", [128, 256]) for j in range(2)]
        ppool = pxs
        pr1 = pmm[0][0:1, 0:128]

        p.dma(identf[:], idf_in)
        p.dma(identb[:], idb_in)
        p.dma(msu[:], msu_in)
        p.dma(miu[:], miu_in)
        p.op("vector", "memset", ones1[:], 1.0)
        for c8 in range(8):
            p.dma(lnwt[:, c8:c8 + 1], lnw_in[c8 * 128:(c8 + 1) * 128].unsqueeze(1))
        p.dma(onwB[0:1, :], onw_in)
        p.t("matmul", pmm[0][:], ones1[:], onwB[0:1, :], start=True, stop=True)
        p.v("tensor_copy", onwB[:], pmm[0][:])

        J = []
        for j in range(2):
            d = {}
            sb = lambda nm, shp, dt: p.sb(g + nm + str(j), shp, dt)
            d["nAcol"] = sb("nAcol", [128, 1], F32)
            d["dtbcol"] = sb("dtbcol", [128, 1], F32)
            d["rawq"] = sb("rawq", [128, NC, 128], BF16)
            d["rawk"] = sb("rawk", [128, NC, 128], BF16)
            d["rawv"] = sb("rawv", [128, NC, 128], BF16)
            d["rawg"] = sb("rawg", [128, NC, 128], BF16)
            d["q_tok"] = d["rawq"]
            d["k_tok"] = d["rawk"]
            d["v_tok"] = d["rawv"]
            d["g_tok"] = d["rawg"]
            d["sg"] = sb("sg", [128, 128], F32)
            d["bt_all"] = sb("bt_all", [128, NC], F32)
            d["st8"] = sb("st8", [128, 8], F32)
            d["sx"] = sb("sx", [128, 1], F32)
            d["G_tok"] = sb("G_tok", [128, NC], F32)
            d["carry"] = sb("carry", [1, NC + 1], F32)
            d["growt"] = sb("growt", [1, 128], F32)
            for nm, shp in (("Uf", [128, 128]), ("S0f", [128, 128]), ("S0s", [128, 128])):
                d[nm] = sb(nm, shp, F32)
            d["ogt2"] = sb("ogt2", [128, 2, 128], BF16)
            for nm, shp in (("grel", [1, 128]), ("gpn", [1, 1]), ("gcoln", [128, 1]),
                            ("gamr", [1, 128]), ("gamc", [128, 1]), ("decr", [1, 128]),
                            ("decc", [128, 1]), ("gCc", [128, 1])):
                d[nm] = sb(nm, shp, F32)
            d["px"] = pxs[j]
            d["otokR"] = sb("otokR", [128, 2, 128], F32)
            d["tb1"] = sb("tb1", [128, 2, 256], BF16)
            d["qf32"] = sb("qf32", [128, 128], F32)
            J.append(d)

        def emit_sectionA():
            for j, d in enumerate(J):
                for m in range(4):
                    for dc in range(8):
                        p.dma(d["sg"][:], Wh[j, m, dc * 128:(dc + 1) * 128, :], q=("sync", "scalar")[dc % 2])
                        p.v("tensor_scalar", d["sg"][:], d["sg"][:], lnwt[:, dc:dc + 1], None, Op.mult)
                        p.v("tensor_copy", d["wcb"][:, dc, m, :], d["sg"][:])
                for dc in range(8):
                    p.dma(d["sg"][:, 0:2], bac[j, dc * 128:(dc + 1) * 128, :])
                    p.v("tensor_scalar", d["sg"][:, 0:2], d["sg"][:, 0:2], lnwt[:, dc:dc + 1], None, Op.mult)
                    p.v("tensor_copy", d["bab"][:, dc, :], d["sg"][:, 0:2])
                p.dma(d["adts"][:], adt[j].unsqueeze(0))
                p.a("activation", d["asc"][:, 0:1], d["adts"][:, 0:1], AF.Exp)
                p.v("tensor_scalar", d["asc"][:, 1:2], d["asc"][:, 0:1], -1.0, None, Op.mult)
                p.t("matmul", pmm[j][0:128, 0:1], ones1[:], d["asc"][:, 1:2], start=True, stop=True)
                p.v("tensor_copy", d["nAcol"][:], pmm[j][0:128, 0:1])
                p.t("matmul", pmm[j][0:128, 0:1], ones1[:], d["adts"][:, 1:2], start=True, stop=True)
                p.v("tensor_copy", d["dtbcol"][:], pmm[j][0:128, 0:1])

            d0 = J[0]
            def a1_dma(tt):
                p.dma(xrowS[:, tt % 2, :], xb[tt * 128:(tt + 1) * 128, :],
                      q=("sync", "scalar")[tt % 2])
            def a1_trans(tt):
                par = tt % 2
                p.a("activation", sqshared_s8[:], xrowS[:, par, :], AF.Square, accum_out=d0["st8"][:, 0:1])
                p.v("tensor_scalar", d0["st8"][:, 1:2], d0["st8"][:, 0:1], 1.0 / D, 1e-6, Op.mult, Op.add)
                p.a("activation", d0["st8"][:, 2:3], d0["st8"][:, 1:2], AF.Ln)
                p.a("activation", rstdS[:, tt:tt + 1], d0["st8"][:, 2:3], AF.Exp, scale=-0.5)
                for dc in range(8):
                    pb = pA[dc % 2]
                    p.t("matmul", pb[:], xrowS[:, par, dc * 128:(dc + 1) * 128], identb[:],
                        start=True, stop=True)
                    if dc % 4 < 2:
                        p.v("tensor_copy", xnTs[:, par, dc, :], pb[:])
                    else:
                        p.a("activation", xnTs[:, par, dc, :], pb[:], AF.Copy)
            a1_dma(0)
            a1_trans(0)
            for tt in range(NC):
                par = tt % 2
                if tt + 1 < NC:
                    a1_dma(tt + 1)
                for j, d in enumerate(J):
                    for m, dst in ((0, "rawq"), (1, "rawk"), (2, "rawv"), (3, "rawg")):
                        for dc in range(8):
                            p.t("matmul", ppool[j][:, 0:128], xnTs[:, par, dc, :], d["wcb"][:, dc, m, :],
                                start=(dc == 0), stop=(dc == 7))
                        p.v("tensor_scalar", d[dst][:, tt, :], ppool[j][:, 0:128],
                            rstdS[:, tt:tt + 1], None, Op.mult)
                    for dc in range(8):
                        p.t("matmul", ppool[j][:, 128:130], xnTs[:, par, dc, :], d["bab"][:, dc, :],
                            start=(dc == 0), stop=(dc == 7))
                    p.v("tensor_scalar", d["bt_all"][:, tt:tt + 1], ppool[j][:, 128:129],
                        rstdS[:, tt:tt + 1], None, Op.mult)
                    p.v("tensor_scalar", d["arw"][:, tt:tt + 1], ppool[j][:, 129:130],
                        rstdS[:, tt:tt + 1], None, Op.mult)
                if tt + 1 < NC:
                    a1_trans(tt + 1)
            for tt in range(NC):
                for j, d in enumerate(J):
                    for src_ in ("rawq", "rawk", "rawv", "rawg"):
                        p.a("activation", d["sg"][:], d[src_][:, tt, :], AF.Sigmoid)
                        p.v("tensor_tensor", d[src_][:, tt, :], d[src_][:, tt, :], d["sg"][:], Op.mult)
                    p.a("activation", d["bt_all"][:, tt:tt + 1], d["bt_all"][:, tt:tt + 1], AF.Sigmoid)
            for tt in range(NC):
                for j, d in enumerate(J):
                    for nm in ("q_tok", "k_tok"):
                        p.a("activation", d["sg"][:], d[nm][:, tt, :], AF.Square, accum_out=d["st8"][:, 4:5])
                        p.v("tensor_scalar", d["st8"][:, 5:6], d["st8"][:, 4:5], 1e-6, None, Op.add)
                        p.a("activation", d["st8"][:, 6:7], d["st8"][:, 5:6], AF.Ln)
                        p.a("activation", d["st8"][:, 7:8], d["st8"][:, 6:7], AF.Exp, scale=-0.5)
                        p.v("tensor_scalar", d[nm][:, tt, :], d[nm][:, tt, :], d["st8"][:, 7:8], None, Op.mult)
                    p.v("tensor_scalar", d["sx"][:], d["arw"][:, tt:tt + 1], d["dtbcol"][:], None, Op.add)
                    p.a("activation", d["st8"][:, 4:5], d["sx"][:], AF.Abs)
                    p.a("activation", d["st8"][:, 5:6], d["st8"][:, 4:5], AF.Exp, scale=-1.0)
                    p.a("activation", d["st8"][:, 6:7], d["st8"][:, 5:6], AF.Ln, bias=1.0)
                    p.a("activation", d["st8"][:, 7:8], d["sx"][:], AF.Relu)
                    p.v("tensor_tensor", d["sx"][:], d["st8"][:, 6:7], d["st8"][:, 7:8], Op.add)
                    p.v("tensor_scalar", d["sx"][:], d["sx"][:], d["nAcol"][:], None, Op.mult)
                    p.t("transpose", pr1, d["sx"][:], identf[:])
                    if tt == 0:
                        p.op("vector", "memset", d["carry"][:, 0:1], 0.0)
                    p.v("tensor_tensor_scan", d["growt"][:], d["carry"][:, 0:1].broadcast_to([1, 128]) if False else d["growt"][:],
                        pr1, d["carry"][:, tt:tt + 1], Op.add, Op.add) if False else None
                    p.op("vector", "memset", d["sg"][0:1, 0:128].bitcast(F32), 0.0)
                    p.v("tensor_tensor_scan", d["growt"][:], d["sg"][0:1, 0:128].bitcast(F32), pr1,
                        d["carry"][:, tt:tt + 1], Op.add, Op.add)
                    p.v("tensor_copy", d["carry"][:, tt + 1:tt + 2], d["growt"][:, 127:128])
                    p.t("transpose", pmm[1][0:128, 0:1], d["growt"][:], identf[0:1, 0:1])
                    p.v("tensor_copy", d["G_tok"][:, tt:tt + 1], pmm[1][0:128, 0:1])
            for j, d in enumerate(J):
                p.op("vector", "memset", d["S0f"][:], 0.0)

        def emit_sectionB():
            def gen_B1(c, j):
                d = J[j]
                s8r = c % 4
                cp = c % 2
                PM = pmm2[j][cp]
                KT, QTb = d["KTb"][:, cp, :], d["QTb"][:, cp, :]
                QT = d["QT"][:, cp, :]
                KbT, ktf, vtf, kbf = d["KbTb"][:, cp, :], d["ktf"][:, cp, :], d["vtf"][:, cp, :], d["kbfb"][:, cp, :]
                Gdiff, expSU, expIU = d["Gdiff"][:, cp, :], d["expSU"][:, cp, :], d["expIU"][:, cp, :]
                LtN, Lcur, LcurT = d["LtNb"][:, cp, :], d["Lcurb"][:, cp, :], d["LcurTb"][:, cp, :]
                rhs0, Xf = d["rhs0"][:, cp, :], d["Xfb"][:, cp, :]
                tb1 = d["tb1"][:, cp, :]
                grel, gamr, decr = d["grel2"][0:1, cp, :], d["gamr2"][0:1, cp, :], d["decr2"][0:1, cp, :]
                gpn = d["gpn2"][0:1, cp, :]
                gcoln, gamc, decc = d["gcoln2"][:, cp, :], d["gamc2"][:, cp, :], d["decc2"][:, cp, :]
                gamB = d["gamB"][:, cp, :]
                PMc = PM[0:128, 0:1]
                PMr = PM[0:1, 0:128]
                p.v("tensor_copy", ktf, d["k_tok"][:, c, :])
                p.v("tensor_copy", vtf, d["v_tok"][:, c, :])
                p.t("matmul", PM[:], d["k_tok"][:, c, :], identb[:], start=True, stop=True)
                p.v("tensor_copy", KT, PM[:])
                yield
                p.t("matmul", PM[:], d["q_tok"][:, c, :], identb[:], start=True, stop=True)
                p.v("tensor_copy", QT, PM[:])
                p.v("tensor_copy", QTb, PM[:])
                yield
                p.v("tensor_scalar", gpn, d["carry"][:, c:c + 1], -1.0, None, Op.mult)
                p.t("matmul", PMc, ones1[:], gpn, start=True, stop=True)
                p.v("tensor_copy", gamB, PMc)
                p.v("tensor_scalar", gcoln, d["G_tok"][:, c:c + 1], gamB, None, Op.add)
                p.a("activation", gamc, gcoln, AF.Exp)
                yield
                p.t("transpose", PMr, gcoln, identf[:])
                p.v("tensor_copy", grel, PMr)
                p.a("activation", gamr, grel, AF.Exp)
                yield
                p.v("tensor_scalar", gcoln, gcoln, -1.0, None, Op.mult)
                p.t("matmul", PM[:], ones1[:], grel, start=True, stop=True)
                p.v("tensor_scalar", Gdiff, PM[:], gcoln, None, Op.add)
                p.v("tensor_tensor", expSU, Gdiff, msu[:], Op.add)
                p.a("activation", expSU, expSU, AF.Exp)
                yield
                p.v("tensor_tensor", expIU, Gdiff, miu[:], Op.add)
                p.a("activation", expIU, expIU, AF.Exp)
                yield
                p.v("tensor_scalar", kbf, ktf, d["bt_all"][:, c:c + 1], None, Op.mult)
                p.t("matmul", PM[:], kbf, identb[:], start=True, stop=True)
                p.v("tensor_copy", KbT, PM[:])
                yield
                p.t("matmul", PM[:], KT, KbT, start=True, stop=True)
                p.v("tensor_tensor", Lcur, expSU, PM[:], Op.mult)
                p.v("tensor_scalar", LtN, Lcur, -1.0, None, Op.mult)
                yield
                p.t("matmul", PM[:], KT, QTb, start=True, stop=True)
                p.v("tensor_tensor", d["PqtR"][:, s8r, :], expIU, PM[:], Op.mult)
                yield
                p.v("tensor_scalar", rhs0[:, 0:128], ktf, d["bt_all"][:, c:c + 1],
                    gamc, Op.mult, Op.mult)
                p.v("tensor_scalar", rhs0[:, 128:256], vtf, d["bt_all"][:, c:c + 1], None, Op.mult)
                p.v("tensor_copy", Xf, rhs0)
                yield
                p.t("matmul", d["px"][:], LtN, Xf, start=True, stop=True)
                p.a("activation", tb1, d["px"][:], AF.Copy)
                p.v("tensor_tensor", Xf, Xf, tb1, Op.add)
                yield
                p.t("matmul", d["px"][:, 0:128], Lcur, identb[:], start=True, stop=True)
                p.v("tensor_copy", LcurT, d["px"][:, 0:128])
                yield
                for lev in range(6):
                    p.t("matmul", PM[:], LcurT, Lcur, start=True, stop=True)
                    p.t("matmul", d["px"][:, 0:128], Lcur, LcurT, start=True, stop=True)
                    p.v("tensor_copy", Lcur, PM[:])
                    p.v("tensor_copy", LcurT, d["px"][:, 0:128])
                    yield
                    p.t("matmul", d["px"][:], LcurT, Xf, start=True, stop=True)
                    p.a("activation", tb1, d["px"][:], AF.Copy)
                    p.v("tensor_tensor", Xf, Xf, tb1, Op.add)
                    yield
                p.v("tensor_copy", d["ZR"][:, s8r, :], Xf[:, 128:256])
                p.t("matmul", d["px"][:, 0:128], Xf[:, 0:128], identb[:], start=True, stop=True)
                p.a("activation", d["WtnR"][:, s8r, :], d["px"][:, 0:128], AF.Copy, scale=-1.0)
                yield
                p.t("matmul", PM[:], ones1[:], gamr, start=True, stop=True)
                p.v("tensor_tensor", d["QTgR"][:, s8r, :], QT, PM[:], Op.mult)
                yield
                p.v("tensor_scalar", decr, grel, -1.0, grel[0:1, 127:128], Op.mult, Op.add)
                p.a("activation", decr, decr, AF.Exp)
                p.t("transpose", PMc, decr, identf[0:1, 0:1])
                p.v("tensor_copy", decc, PMc)
                yield
                p.t("matmul", PMc, ones1[:], gamr[0:1, 127:128], start=True, stop=True)
                p.v("tensor_copy", d["gCcR"][:, s8r:s8r + 1], PMc)
                p.v("tensor_scalar", d["kdecR"][:, s8r, :], ktf, decc, None, Op.mult)

            def gen_B2(c, j):
                d = J[j]
                s8r = c % 4
                opar = c % 2
                r = slice(c * 128, (c + 1) * 128)
                p.t("matmul", pA[j][:], d["WtnR"][:, s8r, :], d["S0f"][:], start=True, stop=False)
                p.t("matmul", pA[j][:], identf[:], d["ZR"][:, s8r, :], start=False, stop=True)
                p.a("activation", d["Uf"][:], pA[j][:], AF.Copy)
                yield
                p.t("matmul", pA[j][:], d["QTgR"][:, s8r, :], d["S0f"][:], start=True, stop=False)
                p.t("matmul", pA[j][:], d["PqtR"][:, s8r, :], d["Uf"][:], start=False, stop=True)
                p.a("activation", d["otokR"][:, opar, :], pA[j][:], AF.Copy)
                yield
                p.a("activation", d["S0s"][:], d["S0f"][:], AF.Copy, scale=d["gCcR"][:, s8r:s8r + 1])
                yield
                p.t("matmul", pA[j][:], d["kdecR"][:, s8r, :], d["Uf"][:], start=True, stop=False)
                p.t("matmul", pA[j][:], identf[:], d["S0s"][:], start=False, stop=True)
                p.a("activation", d["S0f"][:], pA[j][:], AF.Copy)
                yield
                p.a("activation", d["sg"][:], d["otokR"][:, opar, :], AF.Square, accum_out=d["st8"][:, 0:1])
                p.g("tensor_scalar", d["st8"][:, 1:2], d["st8"][:, 0:1], 1.0 / 128.0, 1e-6, Op.mult, Op.add)
                yield
                p.a("activation", d["st8"][:, 2:3], d["st8"][:, 1:2], AF.Ln)
                p.a("activation", d["st8"][:, 3:4], d["st8"][:, 2:3], AF.Exp, scale=-0.5)
                p.g("tensor_scalar", d["otokR"][:, opar, :], d["otokR"][:, opar, :], d["st8"][:, 3:4], None, Op.mult)
                p.g("tensor_tensor", d["otokR"][:, opar, :], d["otokR"][:, opar, :], onwB[:], Op.mult)
                yield
                p.g("tensor_tensor", d["ogt2"][:, opar, :], d["otokR"][:, opar, :], d["g_tok"][:, c, :], Op.mult)
                p.dma(og_out[j, r, :], d["ogt2"][:, opar, :], q=("sync", "scalar")[j])

            LOOKAHEAD = 3
            for c in range(NC + LOOKAHEAD):
                gens = []
                if c < NC:
                    gens += [gen_B1(c, 0), gen_B1(c, 1)]
                if c >= LOOKAHEAD:
                    gens += [gen_B2(c - LOOKAHEAD, 0), gen_B2(c - LOOKAHEAD, 1)]
                _end = object()
                while gens:
                    gens = [gg for gg in gens if next(gg, _end) is not _end]


        with p.scope():
            for j, d in enumerate(J):
                d["wcb"] = p.sb(g + "wcb" + str(j), [128, 8, 4, 128], BF16)
                d["bab"] = p.sb(g + "bab" + str(j), [128, 8, 2], BF16)
                d["adts"] = p.sb(g + "adts" + str(j), [1, 2], F32)
                d["asc"] = p.sb(g + "asc" + str(j), [1, 4], F32)
                d["arw"] = p.sb(g + "arw" + str(j), [128, NC], F32)
            xnTs = p.sb(g + "xnTs", [128, 2, 8, 128], BF16)
            xrowS = p.sb(g + "xrowS", [128, 2, D], BF16)
            rstdS = p.sb(g + "rstdS", [128, NC], F32)
            global sqshared_s8
            sqshared_s8 = p.sb(g + "sqshared", [128, D], BF16)
            emit_sectionA()
        for j, d in enumerate(J):
            for nm, shp in (("QT", [128, 2, 128]),
                            ("ktf", [128, 2, 128]), ("vtf", [128, 2, 128]),
                            ("Gdiff", [128, 2, 128]), ("expSU", [128, 2, 128]), ("expIU", [128, 2, 128]),
                            ("rhs0", [128, 2, 256]), ("gamB", [128, 2, 1])):
                d[nm] = p.sb(g + nm + str(j), shp, F32)
            for nm, shp in (("KTb", [128, 2, 128]), ("QTb", [128, 2, 128]), ("KbTb", [128, 2, 128]),
                            ("kbfb", [128, 2, 128]), ("LtNb", [128, 2, 128]), ("Lcurb", [128, 2, 128]),
                            ("LcurTb", [128, 2, 128]), ("Xfb", [128, 2, 256])):
                d[nm] = p.sb(g + nm + str(j), shp, BF16)
            for nm in ("grel", "gamr", "decr"):
                d[nm + "2"] = p.sb(g + nm + "2" + str(j), [1, 2, 128], F32)
            d["gpn2"] = p.sb(g + "gpn2" + str(j), [1, 2, 1], F32)
            for nm in ("gcoln2", "gamc2", "decc2"):
                d[nm] = p.sb(g + nm + str(j), [128, 2, 1], F32)
            d["WtnR"] = p.sb(g + "WtnR" + str(j), [128, 4, 128], F32)
            d["ZR"] = p.sb(g + "ZR" + str(j), [128, 4, 128], F32)
            d["PqtR"] = p.sb(g + "PqtR" + str(j), [128, 4, 128], F32)
            d["QTgR"] = p.sb(g + "QTgR" + str(j), [128, 4, 128], F32)
            d["kdecR"] = p.sb(g + "kdecR" + str(j), [128, 4, 128], F32)
            d["gCcR"] = p.sb(g + "gCcR" + str(j), [128, 4], F32)
        emit_sectionB()

def build_s8():
    p = Prog()
    xb = p.din("xb", [S, D], BF16)
    Wh = p.din("Wh", [2, 4, D, 128], F32)
    bac = p.din("bac", [2, D, 2], F32)
    adt = p.din("adt", [2, 2], F32)
    lnw_in = p.din("lnw", [D], F32)
    onw_in = p.din("onw", [1, 128], F32)
    msu_in = p.din("msu", [128, 128], F32)
    miu_in = p.din("miu", [128, 128], F32)
    idf_in = p.din("identf", [128, 128], F32)
    idb_in = p.din("identb", [128, 128], BF16)
    og_out = p.dout("og", [2, S, 128], BF16)
    emit_s8(p, "s8_", xb, Wh, bac, adt, lnw_in, onw_in, msu_in, miu_in, idf_in, idb_in, og_out)
    p.finish()
    return p


def stage8(xn2b_full, Wq1, Wk1, Wv1, Wg1, bw, aw, A_log, dt_bias, lnw, onw):
    x3 = xn2b_full.reshape(B, S, D)
    msu = np.where(np.arange(128)[:, None] < np.arange(128)[None, :], 0.0, NEG).astype(np.float32)
    miu = np.where(np.arange(128)[:, None] <= np.arange(128)[None, :], 0.0, NEG).astype(np.float32)
    identf = np.eye(128, dtype=np.float32)
    idb = np.eye(128).astype(BF)
    p = build_s8()
    im = []
    for c in range(NCORES):
        bhs = [2 * c, 2 * c + 1]
        b = bhs[0] // 8
        Whl, bacl, adtl = [], [], []
        for bh in bhs:
            h = bh % 8
            cs = slice(h * 128, (h + 1) * 128)
            Whl.append(np.stack([Wq1[:, cs], Wk1[:, cs], Wv1[:, cs], Wg1[:, cs]]))
            bacl.append(np.stack([bw[:, h], aw[:, h]], 1))
            adtl.append(np.array([A_log[h], dt_bias[h]], np.float32))
        im.append({"xb": np.ascontiguousarray(x3[b]), "Wh": np.ascontiguousarray(np.stack(Whl)),
                   "bac": np.ascontiguousarray(np.stack(bacl)), "adt": np.stack(adtl),
                   "lnw": lnw, "onw": onw.reshape(1, 128), "msu": msu, "miu": miu,
                   "identf": identf, "identb": idb})
    res = run(p, im)
    og = np.zeros((B, S, 8, 128), np.float32)
    for c in range(NCORES):
        for jj in range(2):
            bh = 2 * c + jj
            og[bh // 8, :, bh % 8, :] = res[c]["og"][jj].astype(np.float32)
    return og.reshape(B, S, D)





def emit_s9(p, g, og8, xn2r, ow_in, ln2_in, fcw_in, pw_in, idb_in, idf_in, s1_out, x1b_out, rows):
    with p.scope():
        identb = p.sb(g + "identb_s", [128, 128], BF16)
        ones1 = p.sb(g + "ones1", [1, 128], F32)
        wld = p.sb(g + "wld", [128, 1024], F32)
        lnwt = p.sb(g + "lnwt", [128, 8], F32)
        owb = p.sb(g + "owb", [128, 8, D], BF16)
        fcb = p.sb(g + "fcb", [128, 8, 4 * D], BF16)
        pwb = p.sb(g + "pwb", [128, 32, D], BF16)
        ogT = p.sb(g + "ogT", [128, 8, 128], BF16)
        xt = p.sb(g + "xt", [128, D], F32)
        x1b = p.sb(g + "x1bt", [128, D], F32)
        x1bb = p.sb(g + "x1bb", [128, D], BF16)
        x1bs = p.sb(g + "x1bs", [128, 8, 128], BF16)
        sqscr = p.sb(g + "sqscr", [128, D], F32)
        st9 = p.sb(g + "st9", [128, 8], F32)
        rrow = p.sb(g + "rrow", [1, 128], F32)
        rstB = p.sb(g + "rstB", [128, 128], F32)
        rstBb = p.sb(g + "rstBb", [128, 128], BF16)
        ub = p.sb(g + "ub", [128, 32, 128], BF16)
        yt = p.sb(g + "yt", [128, D], F32)
        pj = p.ps(g + "pj", [128, 512])
        ptb = p.ps(g + "ptb", [128, 128], BF16)
        pc1 = p.ps(g + "pc1", [1, 128])
        pB = p.ps(g + "pB", [128, 128])

        p.dma(identb[:], idb_in)
        p.op("vector", "memset", ones1[:], 1.0)
        for c8 in range(8):
            p.dma(lnwt[:, c8:c8 + 1], ln2_in[c8 * 128:(c8 + 1) * 128].unsqueeze(1))
        for dc in range(8):
            p.dma(wld[:], ow_in[dc * 128:(dc + 1) * 128, :])
            p.v("tensor_copy", owb[:, dc, :], wld[:])
        for dc in range(8):
            p.dma(wld[:], fcw_in[dc * 128:(dc + 1) * 128, 0:1024])
            p.v("tensor_scalar", wld[:], wld[:], lnwt[:, dc:dc + 1], None, Op.mult)
            p.v("tensor_copy", fcb[:, dc, 0:1024], wld[:])
            for hh in range(1, 4):
                p.dma(wld[:], fcw_in[dc * 128:(dc + 1) * 128, hh * 1024:(hh + 1) * 1024])
                p.v("tensor_scalar", wld[:], wld[:], lnwt[:, dc:dc + 1], None, Op.mult)
                p.v("tensor_copy", fcb[:, dc, hh * 1024:(hh + 1) * 1024], wld[:])
        for ic in range(32):
            p.dma(wld[:], pw_in[ic * 128:(ic + 1) * 128, :])
            p.v("tensor_copy", pwb[:, ic, :], wld[:])
        identf = p.sb(g + "identf_s", [128, 128], F32)
        p.dma(identf[:], idf_in)
        nt = rows // 128
        for tt in range(nt):
            r = slice(tt * 128, (tt + 1) * 128)
            for dc in range(8):
                p.dma(ogT[:, dc, :], og8[dc, r, :], q="scalar", transpose=True)
            p.dma(xt[:], xn2r[r, :])
            for dh in range(2):
                for dc in range(8):
                    p.t("matmul", pj[:], ogT[:, dc, :], owb[:, dc, dh * 512:(dh + 1) * 512],
                        start=(dc == 0), stop=(dc == 7))
                p.v("scalar_tensor_tensor", x1b[:, dh * 512:(dh + 1) * 512], pj[:], 0.5,
                    xt[:, dh * 512:(dh + 1) * 512], Op.mult, Op.add)
            p.dma(x1b_out[r, :], x1b[:])
            p.a("activation", sqscr[:], x1b[:], AF.Square, accum_out=st9[:, 0:1])
            p.v("tensor_scalar", st9[:, 1:2], st9[:, 0:1], 1.0 / D, 1e-6, Op.mult, Op.add)
            p.a("activation", st9[:, 2:3], st9[:, 1:2], AF.Sqrt)
            p.v("reciprocal", st9[:, 3:4], st9[:, 2:3])
            p.v("tensor_copy", x1bb[:], x1b[:])
            p.t("transpose", pc1[:], st9[:, 3:4], identf[:])
            p.v("tensor_copy", rrow[:], pc1[:])
            p.t("matmul", pB[:], ones1[:], rrow[:], start=True, stop=True)
            p.v("tensor_copy", rstBb[:], pB[:])
            for dc in range(8):
                p.t("transpose", ptb[:], x1bb[:, dc * 128:(dc + 1) * 128], identb[:])
                p.v("tensor_tensor", x1bs[:, dc, :], ptb[:], rstBb[:], Op.mult)
            for ic in range(32):
                for dc in range(8):
                    p.t("matmul", pB[:], fcb[:, dc, ic * 128:(ic + 1) * 128], x1bs[:, dc, :],
                        start=(dc == 0), stop=(dc == 7))
                p.a("activation", ub[:, ic, :], pB[:], AF.Silu)
            for dh in range(2):
                for ic in range(32):
                    p.t("matmul", pj[:], ub[:, ic, :], pwb[:, ic, dh * 512:(dh + 1) * 512],
                        start=(ic == 0), stop=(ic == 31))
                p.v("scalar_tensor_tensor", yt[:, dh * 512:(dh + 1) * 512], pj[:], 0.5,
                    x1b[:, dh * 512:(dh + 1) * 512], Op.mult, Op.add)
            p.dma(s1_out[r, :], yt[:])


def build_s9(rows):
    p = Prog()
    ogr = p.din("ogr", [8, rows, 128], BF16)
    xn2r = p.din("xn2", [rows, D], F32)
    ow_in = p.din("ow", [D, D], F32)
    ln2_in = p.din("ln2w", [D], F32)
    fcw_in = p.din("fcw", [D, 4 * D], F32)
    pw_in = p.din("pw", [4 * D, D], F32)
    idb_in = p.din("identb", [128, 128], BF16)
    idf_in = p.din("identf", [128, 128], F32)
    s1_out = p.dout("s1", [rows, D], F32)
    x1b_out = p.dout("x1b", [rows, D], F32)
    emit_s9(p, "s9_", ogr, xn2r, ow_in, ln2_in, fcw_in, pw_in, idb_in, idf_in, s1_out, x1b_out, rows)
    p.finish()
    return p


def stage9(og_full, xn2, ow, ln2w, fcw, pw):
    p = build_s9(TPC)
    idb = np.eye(128).astype(BF)
    idf = np.eye(128, dtype=np.float32)
    og4 = og_full.reshape(B, S, 8, 128).astype(BF)
    osh = [np.ascontiguousarray(og4[c // 4, (c % 4) * TPC:(c % 4 + 1) * TPC].transpose(1, 0, 2))
           for c in range(NCORES)]
    xsh = shard_rows(xn2.reshape(T, D))
    im = [{"ogr": osh[c], "xn2": xsh[c], "ow": ow, "ln2w": ln2w, "fcw": fcw, "pw": pw,
           "identb": idb, "identf": idf} for c in range(NCORES)]
    res = run(p, im)
    s1 = np.concatenate([r["s1"] for r in res], 0)
    x1b = np.concatenate([r["x1b"] for r in res], 0)
    return s1, x1b





def emit_s10(p, g, xnbr, xnlor, gw_in, w1_in, w2_in, w3_in, ws1_in, ws2_in, ws3_in, x1r, s1r, iot16_in, out, rows):
    with p.scope():
        gwraw = p.sb(g + "gwraw", [16, D], F32)
        gwbf = p.sb(g + "gwbf", [16, D], BF16)
        gwlos = p.sb(g + "gwlos", [16, D], F32)
        gwlo = p.sb(g + "gwlo", [16, D], BF16)
        gwT = p.sb(g + "gwT", [128, 8, 16], BF16)
        gwloT = p.sb(g + "gwloT", [128, 8, 16], BF16)
        xnTlo = p.sb(g + "xnTlo", [128, 8, 128], BF16)
        wld = p.sb(g + "wld", [128, 1024], F32)
        wld2 = p.sb(g + "wld2", [128, 1024], F32)
        wbuf1 = p.sb(g + "wbuf1", [128, 2, 8, D], BF16)
        wbuf3 = p.sb(g + "wbuf3", [128, 2, 8, D], BF16)
        wbuf2 = p.sb(g + "wbuf2", [128, 2, 8, D], BF16)
        abuf = p.sb(g + "abuf", [128, 1, 8, 512], BF16)
        w1b_ = [wbuf1[:, i].rearrange("p a (x d) -> p (a x) d", x=2) for i in range(2)]
        w3b_ = [wbuf3[:, i].rearrange("p a (x d) -> p (a x) d", x=2) for i in range(2)]
        w2b_ = [wbuf2[:, i] for i in range(2)]
        ab_ = [abuf[:, 0], abuf[:, 0]]
        gsl = p.sb(g + "gsl", [128, 512], F32)
        yacc = p.sb(g + "yacc", [128, 16, D], BF16)
        c_all = p.sb(g + "c_all", [128, 16, 16], F32)
        lg = p.sb(g + "lg", [128, 16], F32)
        sc = p.sb(g + "sc", [128, 16], F32)
        gsc = p.sb(g + "gsc", [128, 4], F32)
        gm = p.sb(g + "gm", [128, 16], F32)
        msk = p.sb(g + "msk", [128, 16], F32)
        st = p.sb(g + "st", [128, 8], F32)
        iot16 = p.sb(g + "iot16_s", [128, 16], F32)
        xt = p.sb(g + "xt", [128, D], F32)
        s1t = p.sb(g + "s1t", [128, D], F32)
        g1t = p.sb(g + "g1t", [128, D], F32)
        pl = p.ps(g + "pl", [128, 16])
        ph_ = [p.ps(g + f"ph{i}", [128, 512]) for i in range(2)]
        ph3_ = [p.ps(g + f"ph3{i}", [128, 512]) for i in range(2)]
        py_ = [p.ps(g + f"py{i}", [128, 512]) for i in range(2)]

        p.dma(iot16[:], iot16_in)
        p.dma(gwraw[:], gw_in)
        p.v("tensor_copy", gwbf[:], gwraw[:])
        p.v("tensor_tensor", gwlos[:], gwraw[:], gwbf[:], Op.subtract)
        p.v("tensor_copy", gwlo[:], gwlos[:])
        for dc in range(8):
            p.dma(gwT[:, dc, :], gwbf[:, dc * 128:(dc + 1) * 128], q="scalar", transpose=True)
            p.dma(gwloT[:, dc, :], gwlo[:, dc * 128:(dc + 1) * 128], q="scalar", transpose=True)
        nt = rows // 128
        NCH = rows // 512
        xnTall = p.sb(g + "xnTall", [128, 8, rows], BF16)
        for ch in range(NCH):
            for dc in range(8):
                p.dma(xnTall[:, dc, ch * 512:(ch + 1) * 512],
                      xnbr[ch * 512:(ch + 1) * 512, dc * 128:(dc + 1) * 128],
                      q=("scalar", "sync")[(ch + dc) % 2], transpose=True)
        for tt in range(nt):
            r = slice(tt * 128, (tt + 1) * 128)
            ts128 = slice(tt * 128, (tt + 1) * 128)
            for dc in range(8):
                p.dma(xnTlo[:, dc, :], xnlor[r, dc * 128:(dc + 1) * 128], q="scalar", transpose=True)
            for dc in range(8):
                p.t("matmul", pl[:], xnTall[:, dc, ts128], gwT[:, dc, :], start=(dc == 0), stop=False)
                p.t("matmul", pl[:], xnTall[:, dc, ts128], gwloT[:, dc, :], start=False, stop=False)
                p.t("matmul", pl[:], xnTlo[:, dc, :], gwT[:, dc, :], start=False, stop=(dc == 7))
            p.v("tensor_copy", lg[:], pl[:])
            p.v("tensor_reduce", st[:, 0:1], lg[:], AX, Op.max)
            p.v("tensor_scalar", st[:, 1:2], st[:, 0:1], -1.0, None, Op.mult)
            p.v("tensor_scalar", sc[:], lg[:], st[:, 1:2], None, Op.add)
            p.a("activation", sc[:], sc[:], AF.Exp)
            p.v("tensor_reduce", st[:, 2:3], sc[:], AX, Op.add)
            p.v("reciprocal", st[:, 3:4], st[:, 2:3])
            p.v("tensor_scalar", sc[:], sc[:], st[:, 3:4], None, Op.mult)
            for g4 in range(4):
                p.v("tensor_reduce", gsc[:, g4:g4 + 1], sc[:, g4 * 4:(g4 + 1) * 4], AX, Op.max)
            p.v("tensor_reduce", st[:, 4:5], gsc[:], AX, Op.max)
            p.v("tensor_scalar", gm[:, 0:4], gsc[:], st[:, 4:5], None, Op.is_equal)
            p.v("scalar_tensor_tensor", gsc[:], gm[:, 0:4], -1e30, gsc[:], Op.mult, Op.add)
            p.v("tensor_reduce", st[:, 5:6], gsc[:], AX, Op.max)
            for g4 in range(4):
                p.v("tensor_reduce", gsc[:, g4:g4 + 1], sc[:, g4 * 4:(g4 + 1) * 4], AX, Op.max)
            p.v("tensor_scalar", gm[:, 0:4], gsc[:], st[:, 5:6], None, Op.is_ge)
            p.v("tensor_copy", msk[:].rearrange("p (g k) -> p g k", k=4),
                gm[:, 0:4].unsqueeze(2).broadcast_to([128, 4, 4]))
            p.v("tensor_tensor", msk[:], msk[:], sc[:], Op.mult)
            p.v("tensor_copy", lg[:], msk[:])
            for k in range(3):
                p.v("tensor_reduce", st[:, 6:7], lg[:], AX, Op.max)
                p.v("tensor_scalar", gm[:], lg[:], st[:, 6:7], None, Op.is_equal)
                p.v("scalar_tensor_tensor", lg[:], gm[:], -1e30, lg[:], Op.mult, Op.add)
            p.v("tensor_reduce", st[:, 6:7], lg[:], AX, Op.max)
            p.v("tensor_scalar", gm[:], msk[:], st[:, 6:7], None, Op.is_ge)
            p.v("tensor_tensor", c_all[:, tt, :], gm[:], sc[:], Op.mult)
            p.op("vector", "memset", yacc[:, tt, :], 0.0)
        for e in range(16):
            eb = e % 2
            w1b, w3b, w2b, ab = w1b_[eb], w3b_[eb], w2b_[eb], ab_[eb]
            ph, ph3, py = ph_[eb], ph3_[eb], py_[eb]
            for dc in range(8):
                p.dma(wld[:, 0:512], w1_in[e, dc * 128:(dc + 1) * 128, :], q=("sync", "gpsimd")[dc % 2])
                p.v("tensor_copy", w1b[:, 2 * dc, :], wld[:, 0:512])
                p.dma(wld2[:, 0:512], w3_in[e, dc * 128:(dc + 1) * 128, :], q=("scalar", "gpsimd")[dc % 2])
                p.v("tensor_copy", w3b[:, 2 * dc, :], wld2[:, 0:512])
            for ic in range(4):
                p.dma(wld[:], w2_in[e, ic * 128:(ic + 1) * 128, :])
                p.v("tensor_copy", w2b[:, ic, :], wld[:])
            for ch in range(NCH):
                tsl = slice(ch * 512, (ch + 1) * 512)
                for ic in range(4):
                    for dc in range(8):
                        p.t("matmul", ph[:], w1b[:, 2 * dc, ic * 128:(ic + 1) * 128], xnTall[:, dc, tsl],
                            start=(dc == 0), stop=(dc == 7))
                    for dc in range(8):
                        p.t("matmul", ph3[:], w3b[:, 2 * dc, ic * 128:(ic + 1) * 128], xnTall[:, dc, tsl],
                            start=(dc == 0), stop=(dc == 7))
                    p.a("activation", gsl[:], ph[:], AF.Silu)
                    p.v("tensor_tensor", ab[:, ic, :], gsl[:], ph3[:], Op.mult)
                for sub in range(4):
                    tt = ch * 4 + sub
                    for dh in range(2):
                        for ic in range(4):
                            p.t("matmul", py[:], ab[:, ic, sub * 128:(sub + 1) * 128],
                                w2b[:, ic, dh * 512:(dh + 1) * 512], start=(ic == 0), stop=(ic == 3))
                        p.v("scalar_tensor_tensor", yacc[:, tt, dh * 512:(dh + 1) * 512], py[:],
                            c_all[:, tt, e:e + 1], yacc[:, tt, dh * 512:(dh + 1) * 512], Op.mult, Op.add)
        ws1b = wbuf1[:, 0]
        ws3b = wbuf3[:, 0]
        ws2b = wbuf2[:, 0]
        abs_ = abuf[:, 0]
        for dc in range(8):
            p.dma(wld[:], ws1_in[dc * 128:(dc + 1) * 128, :])
            p.v("tensor_copy", ws1b[:, dc, :], wld[:])
            p.dma(wld[:], ws3_in[dc * 128:(dc + 1) * 128, :])
            p.v("tensor_copy", ws3b[:, dc, :], wld[:])
            p.dma(wld[:], ws2_in[dc * 128:(dc + 1) * 128, :])
            p.v("tensor_copy", ws2b[:, dc, :], wld[:])
        for ch in range(NCH):
            tsl = slice(ch * 512, (ch + 1) * 512)
            for ic in range(8):
                ph, ph3 = ph_[ic % 2], ph3_[ic % 2]
                for dc in range(8):
                    p.t("matmul", ph[:], ws1b[:, dc, ic * 128:(ic + 1) * 128], xnTall[:, dc, tsl],
                        start=(dc == 0), stop=(dc == 7))
                for dc in range(8):
                    p.t("matmul", ph3[:], ws3b[:, dc, ic * 128:(ic + 1) * 128], xnTall[:, dc, tsl],
                        start=(dc == 0), stop=(dc == 7))
                p.a("activation", gsl[:], ph[:], AF.Silu)
                p.v("tensor_tensor", abs_[:, ic, :], gsl[:], ph3[:], Op.mult)
            for sub in range(4):
                tt = ch * 4 + sub
                for dh in range(2):
                    py = py_[dh]
                    for ic in range(8):
                        p.t("matmul", py[:], abs_[:, ic, sub * 128:(sub + 1) * 128],
                            ws2b[:, ic, dh * 512:(dh + 1) * 512], start=(ic == 0), stop=(ic == 7))
                    p.v("scalar_tensor_tensor", yacc[:, tt, dh * 512:(dh + 1) * 512], py[:], 1.0,
                        yacc[:, tt, dh * 512:(dh + 1) * 512], Op.mult, Op.add)
                r = slice(tt * 128, (tt + 1) * 128)
                p.a("activation", g1t[:], yacc[:, tt, :], AF.Silu)
                p.dma(xt[:], x1r[r, :])
                p.dma(s1t[:], s1r[r, :])
                p.v("tensor_tensor", g1t[:], g1t[:], s1t[:], Op.mult)
                p.v("tensor_tensor", g1t[:], g1t[:], xt[:], Op.add)
                p.dma(out[r, :], g1t[:])


def build_s10(rows):
    p = Prog()
    xnbr = p.din("xnbr", [rows, D], BF16)
    xnlor = p.din("xnlor", [rows, D], BF16)
    gw_in = p.din("gw1", [16, D], F32)
    w1_in = p.din("w1", [16, D, 512], F32)
    w2_in = p.din("w2", [16, 512, D], F32)
    w3_in = p.din("w3", [16, D, 512], F32)
    ws1_in = p.din("ws1", [D, D], F32)
    ws2_in = p.din("ws2", [D, D], F32)
    ws3_in = p.din("ws3", [D, D], F32)
    x1r = p.din("x1", [rows, D], F32)
    s1r = p.din("s1", [rows, D], F32)
    iot16_in = p.din("iot16", [128, 16], F32)
    out = p.dout("out", [rows, D], F32)
    emit_s10(p, "s10_", xnbr, xnlor, gw_in, w1_in, w2_in, w3_in, ws1_in, ws2_in, ws3_in, x1r, s1r, iot16_in, out, rows)
    p.finish()
    return p


def stage10(xn2b, xn2lo, gw1, w1, w2, w3, ws1, ws2, ws3, x1, s1):
    p = build_s10(TPC)
    iot16 = np.broadcast_to(np.arange(16, dtype=np.float32), (128, 16)).copy()
    bsh = shard_rows(xn2b.reshape(T, D)); lsh = shard_rows(xn2lo.reshape(T, D))
    xsh = shard_rows(x1.reshape(T, D)); ssh = shard_rows(s1.reshape(T, D))
    im = [{"xnbr": bsh[c], "xnlor": lsh[c], "gw1": gw1, "w1": w1, "w2": w2, "w3": w3,
           "ws1": ws1, "ws2": ws2, "ws3": ws3, "x1": xsh[c], "s1": ssh[c], "iot16": iot16}
          for c in range(NCORES)]
    res = run(p, im)
    return np.concatenate([r["out"] for r in res], 0)





def emit_s4_sharded(p, g, xnbr, xnlor, gw, iota_tok_in, idf_in, maskvec_in, cofff_in,
                    countsl, counts_all, table_loc, table_red, table_u32,
                    gbidx_out, wts_out, cc_counts_hook, cc_table_hook, rows):
    with p.scope():
        NTT = rows // 128
        gwraw = p.sb(g + "gwraw", [64, D], F32)
        gwbf = p.sb(g + "gwbf", [64, D], BF16)
        gwlosc = p.sb(g + "gwlosc", [64, D], F32)
        gwlo = p.sb(g + "gwlo", [64, D], BF16)
        gwT = p.sb(g + "gwT", [128, 8, 64], BF16)
        gwloT = p.sb(g + "gwloT", [128, 8, 64], BF16)
        xnT = p.sb(g + "xnT", [128, 8, 128], BF16)
        xnloT = p.sb(g + "xnloT", [128, 8, 128], BF16)
        lwork = p.sb(g + "lwork", [128, 64], F32)
        iot = p.sb(g + "iot", [128, 64], F32)
        oh = p.sb(g + "oh", [128, 64], F32)
        mx = p.sb(g + "mx", [128, 4], F32)
        idx_all = p.sb(g + "idx_all", [128, NTT, 4], F32)
        ew = p.sb(g + "ew", [128, 8], F32)
        wtt = p.sb(g + "wtt", [128, 4], F32)
        selT_all = p.sb(g + "selT_all", [64, rows], F32)
        zerT = p.sb(g + "zerT", [64, rows], F32)
        posT = p.sb(g + "posT", [64, rows], F32)
        sel = p.sb(g + "sel", [128, 64], F32)
        pos_tok = p.sb(g + "pos_tok", [128, 64], F32)
        poscol = p.sb(g + "poscol", [128, 4], F32)
        base = p.sb(g + "base", [128, 1], F32)
        km = p.sb(g + "km", [128, 1], F32)
        ge = p.sb(g + "ge", [128, 1], F32)
        t1 = p.sb(g + "t1", [128, 1], F32)
        gidxf = p.sb(g + "gidxf", [128, 1], F32)
        gidxu = p.sb(g + "gidxu", [128, 1], U32)
        gbu = p.sb(g + "gbu", [128, 4], U32)
        tidi = p.sb(g + "tidi", [128, 1], I32)
        tidf = p.sb(g + "tidf", [128, 1], F32)
        tids = p.sb(g + "tids", [128, 1], F32)
        identf = p.sb(g + "identf_s", [128, 128], F32)
        zl = p.sb(g + "zl", [128, 128], F32)
        cnt_sb = p.sb(g + "cnt_sb", [8, 64], F32)
        mvec = p.sb(g + "mvec", [8, 1], F32)
        brow = p.sb(g + "brow", [1, 64], F32)
        bcol = p.sb(g + "bcol", [64, 1], F32)
        coffc = p.sb(g + "coffc", [1, 1], F32)
        coffcol = p.sb(g + "coffcol", [128, 1], F32)
        ones1 = p.sb(g + "ones1", [1, 128], F32)
        tbl_cv = p.sb(g + "tbl_cv", [128, 128], F32)
        tbl_cu = p.sb(g + "tbl_cu", [128, 128], U32)
        pl = p.ps(g + "pl", [128, 64])
        pt2 = p.ps(g + "pt2", [64, 128])
        ptp = p.ps(g + "ptp", [128, 64])
        pb1 = p.ps(g + "pb1", [1, 64])
        pcc = p.ps(g + "pcc", [128, 1])

        p.dma(identf[:], idf_in)
        p.dma(iot[:], iota_tok_in)
        p.dma(mvec[:], maskvec_in)
        p.dma(coffc[:], cofff_in)
        p.op("vector", "memset", zerT[:], 0.0)
        p.op("vector", "memset", zl[:], 0.0)
        p.op("vector", "memset", ones1[:], 1.0)
        p.dma(table_loc.rearrange("(q a) x -> q (a x)", q=128), zl[:].bitcast(F32))
        p.dma(gwraw[:], gw)
        p.v("tensor_copy", gwbf[:], gwraw[:])
        p.v("tensor_tensor", gwlosc[:], gwraw[:], gwbf[:], Op.subtract)
        p.v("tensor_copy", gwlo[:], gwlosc[:])
        for dc in range(8):
            p.dma(gwT[:, dc, :], gwbf[:, dc * 128:(dc + 1) * 128], q="scalar", transpose=True)
            p.dma(gwloT[:, dc, :], gwlo[:, dc * 128:(dc + 1) * 128], q="scalar", transpose=True)
        breg = p.nc.gpsimd.alloc_register()
        p.raw("gpsimd", lambda e: e.reg_mov(breg, T - 1))
        p.g("iota", tidi[:], pattern=[[0, 1]], base=0, channel_multiplier=1)
        p.v("tensor_copy", tidf[:], tidi[:])
        p.t("matmul", pcc[:], ones1[:], coffc[:], start=True, stop=True)
        p.v("tensor_copy", coffcol[:], pcc[:])
        p.v("tensor_tensor", tidf[:], tidf[:], coffcol[:], Op.add)
        for tt in range(NTT):
            r = slice(tt * 128, (tt + 1) * 128)
            for dc in range(8):
                p.dma(xnT[:, dc, :], xnbr[r, dc * 128:(dc + 1) * 128], q="scalar", transpose=True)
                p.dma(xnloT[:, dc, :], xnlor[r, dc * 128:(dc + 1) * 128], q="scalar", transpose=True)
            for dc in range(8):
                p.t("matmul", pl[:], xnT[:, dc, :], gwT[:, dc, :], start=(dc == 0), stop=False)
                p.t("matmul", pl[:], xnT[:, dc, :], gwloT[:, dc, :], start=False, stop=False)
                p.t("matmul", pl[:], xnloT[:, dc, :], gwT[:, dc, :], start=False, stop=(dc == 7))
            p.v("tensor_copy", lwork[:], pl[:])
            p.op("vector", "memset", sel[:], 0.0)
            for k in range(4):
                p.v("tensor_reduce", mx[:, k:k + 1], lwork[:], AX, Op.max)
                p.v("tensor_scalar", oh[:], lwork[:], mx[:, k:k + 1], None, Op.is_equal)
                p.v("scalar_tensor_tensor", lwork[:], oh[:], -1e30, lwork[:], Op.mult, Op.add)
                p.v("tensor_tensor", sel[:], sel[:], oh[:], Op.add)
                p.v("tensor_tensor", oh[:], oh[:], iot[:], Op.mult)
                p.v("tensor_reduce", idx_all[:, tt, k:k + 1], oh[:], AX, Op.add)
            p.v("tensor_scalar", ew[:, 6:7], mx[:, 0:1], -1.0, None, Op.mult)
            p.v("tensor_scalar", ew[:, 0:4], mx[:], ew[:, 6:7], None, Op.add)
            p.a("activation", ew[:, 0:4], ew[:, 0:4], AF.Exp)
            p.v("tensor_reduce", ew[:, 4:5], ew[:, 0:4], AX, Op.add)
            p.v("reciprocal", ew[:, 5:6], ew[:, 4:5])
            p.v("tensor_scalar", wtt[:], ew[:, 0:4], ew[:, 5:6], None, Op.mult)
            p.dma(wts_out[r, :], wtt[:])
            p.t("transpose", pt2[:], sel[:, 0:64], identf[:])
            p.v("tensor_copy", selT_all[:, r], pt2[:])
        p.v("tensor_tensor_scan", posT[:], zerT[:], selT_all[:], 0.0, Op.add, Op.add)
        p.dma(countsl, posT[:, rows - 1:rows])
        cc_counts_hook(p)
        p.dma(cnt_sb[:], counts_all.rearrange("(a b) x -> a (b x)", a=8))
        p.t("matmul", pb1[:], mvec[:], cnt_sb[:], start=True, stop=True)
        p.v("tensor_copy", brow[:], pb1[:])
        p.t("transpose", pcc[0:64, :], brow[:], identf[0:1, 0:1])
        p.v("tensor_copy", bcol[:], pcc[0:64, :])
        p.v("tensor_scalar", posT[:], posT[:], bcol[:], None, Op.add)
        for tt in range(NTT):
            r = slice(tt * 128, (tt + 1) * 128)
            p.t("transpose", ptp[:], posT[:, r], identf[0:64, 0:64])
            p.v("tensor_copy", pos_tok[:, 0:64], ptp[:])
            p.v("tensor_scalar", tids[:], tidf[:], float(tt * 128), None, Op.add)
            for k in range(4):
                p.v("tensor_scalar", oh[:], iot[:], idx_all[:, tt, k:k + 1], None, Op.is_equal)
                p.v("tensor_tensor", oh[:], oh[:], pos_tok[:, 0:64], Op.mult)
                p.v("tensor_reduce", poscol[:, k:k + 1], oh[:], AX, Op.add)
            for k in range(4):
                p.v("tensor_scalar", poscol[:, k:k + 1], poscol[:, k:k + 1], -1.0, None, Op.add)
                p.v("tensor_scalar", base[:], idx_all[:, tt, k:k + 1], 256.0, None, Op.mult)
                p.v("tensor_tensor", base[:], base[:], poscol[:, k:k + 1], Op.add)
                p.v("tensor_scalar", ge[:], poscol[:, k:k + 1], 256.0, None, Op.is_ge)
                p.v("scalar_tensor_tensor", gidxf[:], ge[:], 100000.0, base[:], Op.mult, Op.add)
                p.v("tensor_copy", gidxu[:], gidxf[:])
                p.idma("gpsimd", out=table_loc, out_offset=IndirectOffsetOnAxis(ap=gidxu[:], axis=0),
                       in_=tids[:], in_offset=None, bounds_check=breg, oob_is_err=False)
                p.v("tensor_scalar", km[:], poscol[:, k:k + 1], 256.0, None, Op.is_lt)
                p.v("tensor_tensor", t1[:], base[:], km[:], Op.mult)
                p.v("scalar_tensor_tensor", gidxf[:], ge[:], float(T), t1[:], Op.mult, Op.add)
                p.v("tensor_copy", gbu[:, k:k + 1], gidxf[:])
            p.dma(gbidx_out[r, :], gbu[:])
        cc_table_hook(p)
        p.dma(tbl_cv[:], table_red.rearrange("(q a) x -> q (a x)", q=128))
        p.v("tensor_copy", tbl_cu[:], tbl_cv[:])
        p.dma(table_u32.rearrange("(q a) x -> q (a x)", q=128), tbl_cu[:])



def build_fused():
    from concourse.expressions import RuntimeValue
    p = Prog()
    xloc = p.din("xloc", [TPC, D], F32)
    Whead = p.din("Whead", [4, 3, 64, D], F32)
    bhead = p.din("bhead", [4, 3, 64], F32)
    bmask_in = p.din("bmask", [4, 128, 512], BF16)
    amask_in = p.din("amask", [4, 128, 512], BF16)
    identb_in = p.din("identb", [128, 128], BF16)
    identf_in = p.din("identf", [128, 128], F32)
    gw64 = p.din("gw64", [64, D], F32)
    iota_tok = p.din("iota_tok", [128, 64], F32)
    maskvec = p.din("maskvec", [8, 1], F32)
    cofff = p.din("cofff", [1, 1], F32)
    coffi = p.din("coffi", [1, 2], I32)
    gp = p.din("gp", [8, 768, 512], I32)
    gs = p.din("gs", [8, 768, 8], F32)
    up = p.din("up", [8, 768, 512], I32)
    us = p.din("us", [8, 768, 8], F32)
    dp = p.din("dp", [8, 1024, 384], I32)
    ds_ = p.din("ds", [8, 1024, 6], F32)
    sgp = p.din("sgp", [768, 512], I32)
    sgs = p.din("sgs", [768, 8], F32)
    sup = p.din("sup", [768, 512], I32)
    sus = p.din("sus", [768, 8], F32)
    sdp = p.din("sdp", [1024, 384], I32)
    sds = p.din("sds", [1024, 6], F32)
    Wh8 = p.din("Wh8", [2, 4, D, 128], F32)
    bac = p.din("bac", [2, D, 2], F32)
    adt = p.din("adt", [2, 2], F32)
    lnw = p.din("lnw", [D], F32)
    onw = p.din("onw", [1, 128], F32)
    msu = p.din("msu", [128, 128], F32)
    miu = p.din("miu", [128, 128], F32)
    ow_in = p.din("ow", [D, D], F32)
    ln2w = p.din("ln2w", [D], F32)
    fcw_in = p.din("fcw", [D, 4 * D], F32)
    pw_in = p.din("pw", [4 * D, D], F32)
    gw16 = p.din("gw1", [16, D], F32)
    w1_in = p.din("w1", [16, D, 512], F32)
    w2_in = p.din("w2", [16, 512, D], F32)
    w3_in = p.din("w3", [16, D, 512], F32)
    ws1_in = p.din("ws1", [D, D], F32)
    ws2_in = p.din("ws2", [D, D], F32)
    ws3_in = p.din("ws3", [D, D], F32)
    iot16 = p.din("iot16", [128, 16], F32)
    out = p.dout("out", [TPC, D], F32)

    xn_l = p.dint("xn_l", [TPC, D], F32)
    xnb_l = p.dint("xnb_l", [TPC, D], BF16)
    xnlo_l = p.dint("xnlo_l", [TPC, D], BF16)
    xnb_batch = p.dint("xnb_batch", [S, D], BF16)
    xnb_full = p.dint("xnb_full", [T, D], BF16)
    z4_l = p.dint("z4_l", [4, S, 64], F32)
    zAG = p.dint("zAG", [16, S, 64], F32)
    z_loc = p.dint("z_loc", [16, TPC, 64], F32)
    s0_l = p.dint("s0_l", [TPC, D], F32)
    countsl = p.dint("countsl", [64, 1], F32)
    counts_all = p.dint("counts_all", [512, 1], F32)
    table_loc = p.dint("table_loc", [T, 1], F32)
    table_red = p.dint("table_red", [T, 1], F32)
    table_u32 = p.dint("table_u32", [T, 1], U32)
    tbl_loc = p.dint("tbl_loc", [8, 256], U32)
    gbidx_l = p.dint("gbidx_l", [TPC, 4], U32)
    wts_l = p.dint("wts_l", [TPC, 4], F32)
    eo_l = p.dint("eo_l", [8, 256, D], BF16)
    eoall = p.dint("eoall", [T + 1, D], BF16)
    x1_l = p.dint("x1_l", [TPC, D], F32)
    xn2_l = p.dint("xn2_l", [TPC, D], F32)
    xn2b_l = p.dint("xn2b_l", [TPC, D], BF16)
    xn2lo_l = p.dint("xn2lo_l", [TPC, D], BF16)
    xn2b_batch = p.dint("xn2b_batch", [S, D], BF16)
    og_l = p.dint("og_l", [2, S, 128], BF16)
    ogAG = p.dint("ogAG", [8, S, 128], BF16)
    og_loc = p.dint("og_loc", [8, TPC, 128], BF16)
    s1_l = p.dint("s1_l", [TPC, D], F32)
    x1b_l = p.dint("x1b_l", [TPC, D], F32)

    GRP2 = [[0, 1, 2, 3], [4, 5, 6, 7]]
    GRP1 = [list(range(8))]

    with p.scope():
        coffs = p.sb("f_coffs", [1, 2], I32)
        p.dma(coffs[:], coffi)
        r_sync = p.nc.sync.alloc_register()
        r_sync2 = p.nc.sync.alloc_register()
        p.emit("sync", lambda e: p.nc.reg_load(r_sync, coffs[0:1, 1:2]), "comp",
               reads=[Prog._region(coffs[0:1, :])])
        p.emit("sync", lambda e: p.nc.reg_load(r_sync2, coffs[0:1, 0:1]), "comp",
               reads=[Prog._region(coffs[0:1, :])])

        emit_s1(p, "a_", xloc, xn_l, xnb_l, xnlo_l, TPC)
        p.emit("gpsimd", lambda e: e.collective_compute("AllGather", Op.bypass,
               replica_groups=GRP2, ins=[xnb_l.opt()], outs=[xnb_batch.opt()]), "cc",
               reads=[Prog._region(xnb_l.opt())], writes=[Prog._region(xnb_batch.opt())])
        p.emit("gpsimd", lambda e: e.collective_compute("AllGather", Op.bypass,
               replica_groups=GRP1, ins=[xnb_l.opt()], outs=[xnb_full.opt()]), "cc",
               reads=[Prog._region(xnb_l.opt())], writes=[Prog._region(xnb_full.opt())])
        emit_s2(p, "b_", xnb_batch, Whead, bhead, bmask_in, amask_in, identb_in, identf_in, z4_l)
        p.emit("gpsimd", lambda e: e.collective_compute("AllGather", Op.bypass,
               replica_groups=GRP2, ins=[z4_l.opt()], outs=[zAG.opt()]), "cc",
               reads=[Prog._region(z4_l.opt())], writes=[Prog._region(zAG.opt())])
        p.dma(z_loc.opt(), zAG[:, bass.ds(RuntimeValue(r_sync), TPC), :])
        emit_s4_sharded(p, "d_", xnb_l, xnlo_l, gw64, iota_tok, identf_in, maskvec, cofff,
                        countsl, counts_all, table_loc, table_red, table_u32,
                        gbidx_l, wts_l,
                        lambda pp: pp.emit("gpsimd", lambda e: e.collective_compute(
                            "AllGather", Op.bypass, replica_groups=GRP1,
                            ins=[countsl.opt()], outs=[counts_all.opt()]), "cc",
                            reads=[Prog._region(countsl)], writes=[Prog._region(counts_all)]),
                        lambda pp: pp.emit("gpsimd", lambda e: e.collective_compute(
                            "AllReduce", Op.add, replica_groups=GRP1,
                            ins=[table_loc.opt()], outs=[table_red.opt()]), "cc",
                            reads=[Prog._region(table_loc)], writes=[Prog._region(table_red)]),
                        TPC)
        emit_s3(p, "c_", xn_l, z_loc, p.din("gamma", [D], F32), p.din("beta", [D], F32), s0_l, TPC)
        p.dma(tbl_loc.rearrange("a b -> (a b)").unsqueeze(1), table_u32[bass.ds(RuntimeValue(r_sync2), TPC), :])
        emit_s5(p, "e_", xnb_full, tbl_loc, gp, gs, up, us, dp, ds_, identb_in, eo_l)
        p.emit("gpsimd", lambda e: e.collective_compute("AllGather", Op.bypass,
               replica_groups=GRP1, ins=[eo_l.opt()], outs=[eoall[0:T, :].opt()]), "cc",
               reads=[Prog._region(eo_l.opt())], writes=[Prog._region(eoall[0:T, :].opt())])
        with p.scope():
            zrow1 = p.sb("f_zrow1", [1, D], BF16)
            p.op("vector", "memset", zrow1[:], 0.0)
            p.dma(eoall[T:T + 1, :], zrow1[:])
        emit_s6(p, "f_", eoall, gbidx_l, wts_l, xloc, s0_l, xnb_l,
                sgp, sgs, sup, sus, sdp, sds, identb_in, x1_l, TPC)
        emit_s1(p, "g_", x1_l, xn2_l, xn2b_l, xn2lo_l, TPC)
        p.emit("gpsimd", lambda e: e.collective_compute("AllGather", Op.bypass,
               replica_groups=GRP2, ins=[xn2b_l.opt()], outs=[xn2b_batch.opt()]), "cc",
               reads=[Prog._region(xn2b_l.opt())], writes=[Prog._region(xn2b_batch.opt())])
        emit_s8(p, "h_", xn2b_batch, Wh8, bac, adt, lnw, onw, msu, miu, identf_in, identb_in, og_l)
        p.emit("gpsimd", lambda e: e.collective_compute("AllGather", Op.bypass,
               replica_groups=GRP2, ins=[og_l.opt()], outs=[ogAG.opt()]), "cc",
               reads=[Prog._region(og_l.opt())], writes=[Prog._region(ogAG.opt())])
        p.dma(og_loc.opt(), ogAG[:, bass.ds(RuntimeValue(r_sync), TPC), :])
        emit_s9(p, "i_", og_loc, xn2_l, ow_in, ln2w, fcw_in, pw_in, identb_in, identf_in,
                s1_l, x1b_l, TPC)
        emit_s10(p, "j_", xn2b_l, xn2lo_l, gw16, w1_in, w2_in, w3_in, ws1_in, ws2_in, ws3_in,
                 x1_l, s1_l, iot16, out, TPC)
    p.finish()
    return p


def kernel(**inputs):
    z = inputs
    x = np.ascontiguousarray(z["L0_attn_embedded"].astype(np.float32, copy=False)).reshape(T, D)
    bm, am = attn_masks()
    identb = np.eye(128).astype(BF)
    identf = np.eye(128, dtype=np.float32)
    iota64 = np.broadcast_to(np.arange(64, dtype=np.float32), (128, 64)).copy()
    iota16 = np.broadcast_to(np.arange(16, dtype=np.float32), (128, 16)).copy()
    msu = np.where(np.arange(128)[:, None] < np.arange(128)[None, :], 0.0, NEG).astype(np.float32)
    miu = np.where(np.arange(128)[:, None] <= np.arange(128)[None, :], 0.0, NEG).astype(np.float32)
    Wq, Wk, Wv = z["L0_attn_Wq"], z["L0_attn_Wk"], z["L0_attn_Wv"]
    bq, bk, bv = z["L0_attn_bq"], z["L0_attn_bk"], z["L0_attn_bv"]
    p = build_fused()
    im = []
    for c in range(NCORES):
        hs = [(c * 4 + j) % 16 for j in range(4)]
        Wh = np.stack([np.stack([W[h * 64:(h + 1) * 64] for W in (Wq, Wk, Wv)]) for h in hs])
        bh_ = np.stack([np.stack([bb[h * 64:(h + 1) * 64] for bb in (bq, bk, bv)]) for h in hs])
        sl8 = slice(c * 8, (c + 1) * 8)
        Whl, bacl, adtl = [], [], []
        for bhp in (2 * c, 2 * c + 1):
            h = bhp % 8
            cs = slice(h * 128, (h + 1) * 128)
            Whl.append(np.stack([z["L1_attn_qw"][:, cs], z["L1_attn_kw"][:, cs],
                                 z["L1_attn_vw"][:, cs], z["L1_attn_gw"][:, cs]]))
            bacl.append(np.stack([z["L1_attn_bw"][:, h], z["L1_attn_aw"][:, h]], 1))
            adtl.append(np.array([z["L1_attn_A_log"][h], z["L1_attn_dt_bias"][h]], np.float32))
        im.append({
            "xloc": np.ascontiguousarray(x[c * TPC:(c + 1) * TPC]),
            "Whead": np.ascontiguousarray(Wh), "bhead": np.ascontiguousarray(bh_),
            "bmask": bm, "amask": am, "identb": identb, "identf": identf,
            "gw64": z["L0_ffn_gate_w"], "iota_tok": iota64,
            "maskvec": (np.arange(8) < c).astype(np.float32).reshape(8, 1),
            "cofff": np.array([[c * TPC]], np.float32),
            "coffi": np.array([[c * TPC, (c % 4) * TPC]], np.int32),
            "gp": np.ascontiguousarray(z["L0_ffn_gate_packed"][sl8]),
            "gs": np.ascontiguousarray(z["L0_ffn_gate_scales"][sl8]),
            "up": np.ascontiguousarray(z["L0_ffn_up_packed"][sl8]),
            "us": np.ascontiguousarray(z["L0_ffn_up_scales"][sl8]),
            "dp": np.ascontiguousarray(z["L0_ffn_down_packed"][sl8]),
            "ds": np.ascontiguousarray(z["L0_ffn_down_scales"][sl8]),
            "sgp": z["L0_ffn_sh_gate_packed"], "sgs": z["L0_ffn_sh_gate_scales"],
            "sup": z["L0_ffn_sh_up_packed"], "sus": z["L0_ffn_sh_up_scales"],
            "sdp": z["L0_ffn_sh_down_packed"], "sds": z["L0_ffn_sh_down_scales"],
            "Wh8": np.ascontiguousarray(np.stack(Whl)), "bac": np.ascontiguousarray(np.stack(bacl)),
            "adt": np.stack(adtl), "lnw": z["L1_attn_ln1_w"],
            "onw": z["L1_attn_onorm_w"].reshape(1, 128), "msu": msu, "miu": miu,
            "ow": z["L1_attn_ow"], "ln2w": z["L1_attn_ln2_w"], "fcw": z["L1_attn_fcw"],
            "pw": z["L1_attn_pw"], "gw1": z["L1_ffn_gate_w"],
            "w1": z["L1_ffn_w1"], "w2": z["L1_ffn_w2"], "w3": z["L1_ffn_w3"],
            "ws1": z["L1_ffn_ws1"], "ws2": z["L1_ffn_ws2"], "ws3": z["L1_ffn_ws3"],
            "iot16": iota16,
            "gamma": z["L0_attn_gamma"], "beta": z["L0_attn_beta"],
        })
    res = run(p, im)
    outp = np.concatenate([r["out"] for r in res], 0)
    return outp.reshape(B, S, D).astype(np.float32, copy=False)






# version 15
# speedup vs baseline: 1.4591x; 1.0024x over previous
import numpy as np
import ml_dtypes
from contextlib import ExitStack
import concourse.bass as bass
import concourse.mybir as mybir
from concourse.alu_op_type import AluOpType as Op
from concourse.bass_utils import run_bass_kernel_spmd
from concourse.bass import IndirectOffsetOnAxis

F32 = mybir.dt.float32
BF16 = mybir.dt.bfloat16
I32 = mybir.dt.int32
U32 = mybir.dt.uint32
AF = mybir.ActivationFunctionType
AX = mybir.AxisListType.X
BF = ml_dtypes.bfloat16

NCORES = 8
B, S, D = 2, 8192, 1024
T = B * S
TPC = T // NCORES
NT = TPC // 128

ENGINES = ("sync", "vector", "scalar", "tensor", "gpsimd")


class Prog:

    SEMS = ("Cv", "Cs", "Ct", "Cg", "Cy", "Dy", "Ds", "Dg", "CC")
    STEP = {"Cv": 1, "Cs": 1, "Ct": 1, "Cg": 1, "Cy": 1, "Dy": 16, "Ds": 16, "Dg": 16, "CC": 1}

    def __init__(self):
        self.nc = bass.Bass("TRN2", target_bir_lowering=False)
        self.ops = []
        self._es = ExitStack()
        self._scopes = []

    def _sem_of(self, engine, kind):
        if kind == "cc":
            return "CC"
        if kind in ("dma", "sdma"):
            return {"sync": "Dy", "scalar": "Ds", "gpsimd": "Dg", "vector": "Dy", "tensor": "Dy"}[engine]
        return {"vector": "Cv", "scalar": "Cs", "tensor": "Ct", "gpsimd": "Cg", "sync": "Cy"}[engine]

    def din(self, name, shape, dtype):
        return self.nc.dram_tensor(name, list(shape), dtype, kind="ExternalInput").ap()

    def dout(self, name, shape, dtype):
        return self.nc.dram_tensor(name, list(shape), dtype, kind="ExternalOutput").ap()

    def dint(self, name, shape, dtype):
        return self.nc.dram_tensor(name, list(shape), dtype).ap()

    def sb(self, name, shape, dtype):
        es = self._scopes[-1] if self._scopes else self._es
        return es.enter_context(self.nc.sbuf_tensor(name, list(shape), dtype))

    def ps(self, name, shape, dtype=F32):
        es = self._scopes[-1] if self._scopes else self._es
        assert len(shape) == 2
        esz = {F32: 4, BF16: 2, I32: 4, U32: 4}[dtype]
        cols = max(shape[1], 2048 // esz)
        h = es.enter_context(self.nc.psum_tensor(name, [shape[0], cols], dtype))
        return h[:, 0:shape[1]]

    from contextlib import contextmanager as _cm

    @_cm
    def scope(self):
        es = ExitStack()
        self._scopes.append(es)
        try:
            with es:
                yield
        finally:
            self._scopes.pop()
            self.ops.append(dict(engine=None, kind="barrier", fn=None, reads=[], writes=[]))

    @staticmethod
    def _region(ap):
        if ap is None:
            return None
        t = getattr(ap, "tensor", None)
        if t is None:
            if hasattr(ap, "ap") and hasattr(ap, "axis"):
                return Prog._region(ap.ap)
            return None
        off = ap.offset
        if not isinstance(off, int):
            return (t.name, 0, 1 << 62)
        span = 1
        for st_, ct in ap.ap:
            span += (ct - 1) * abs(st_)
        return (t.name, off, off + span)

    @staticmethod
    def _collect(args, kwargs, first_is_write=True):
        reads, writes = [], []
        seen_first = [False]

        def add(x, force_read=False):
            r = Prog._region(x)
            if r is None:
                return
            if not seen_first[0] and first_is_write and not force_read:
                writes.append(r)
                seen_first[0] = True
            else:
                reads.append(r)

        for a in args:
            if hasattr(a, "tensor") or (hasattr(a, "ap") and hasattr(a, "axis")):
                add(a)
        for k, v in kwargs.items():
            if hasattr(v, "tensor") or (hasattr(v, "ap") and hasattr(v, "axis")):
                if k in ("accum_out", "out", "out_offset"):
                    writes.append(Prog._region(v))
                else:
                    reads.append(Prog._region(v))
        return reads, writes

    def emit(self, engine, fn, kind="comp", reads=None, writes=None):
        assert engine in ENGINES
        self.ops.append(dict(engine=engine, kind=kind, fn=fn,
                             reads=reads or [], writes=writes or []))

    def raw(self, engine, fn):
        self.emit(engine, fn, "raw")

    def dma(self, out, in_, q="sync", **kw):
        kind = "sdma" if q == "gpsimd" else "dma"
        r_out = self._region(out)
        r_in = self._region(in_)
        self.emit(q, (lambda e, out=out, in_=in_, kw=kw: e.dma_start(out=out, in_=in_, **kw)),
                  kind, reads=[r_in], writes=[r_out])

    def idma(self, q, **kw):
        reads, writes = self._collect((), kw, first_is_write=False)
        self.emit(q, (lambda e, kw=kw: e.indirect_dma_start(**kw)), "sdma",
                  reads=reads, writes=writes)

    def op(self, engine, meth, *a, **kw):
        reads, writes = self._collect(a, kw)
        if meth == "matmul":
            reads = reads + writes
        self.emit(engine, (lambda e, a=a, kw=kw: getattr(e, meth)(*a, **kw)), "comp",
                  reads=reads, writes=writes)

    def v(self, meth, *a, **kw):
        self.op("vector", meth, *a, **kw)

    def a(self, meth, *a, **kw):
        self.op("scalar", meth, *a, **kw)

    def t(self, meth, *a, **kw):
        self.op("tensor", meth, *a, **kw)

    def g(self, meth, *a, **kw):
        self.op("gpsimd", meth, *a, **kw)

    def cc(self, kind, op, ins, outs):
        self.emit("gpsimd", (lambda e, k=kind, o=op, i=ins, u=outs:
                             e.collective_compute(k, o, replica_groups=[list(range(NCORES))],
                                                  ins=i, outs=u)), "cc",
                  reads=[self._region(x) for x in ins],
                  writes=[self._region(x) for x in outs])

    NLANES = 8

    def finish(self):
        nc = self.nc
        L = self.NLANES
        def step_of(base):
            return 16 if base in ("Dy", "Ds", "Dg") else 1

        ordinal = {}
        counts = {}
        qcount = {}
        last_access = {}
        req = []
        gate = {}
        barrier_req = {}
        for i, o in enumerate(self.ops):
            if o["kind"] == "barrier":
                req.append(None)
                barrier_req = dict(counts)
                last_access = {}
                continue
            if o["kind"] == "raw":
                req.append(dict(barrier_req))
                continue
            base = self._sem_of(o["engine"], o["kind"])
            if o["kind"] in ("dma", "sdma"):
                qn = qcount.get(base, 0)
                qcount[base] = qn + 1
                key = (base, qn % L)
            elif o["kind"] == "cc":
                key = (base, 0)
            else:
                key = (base, 0)
            myreq = dict(barrier_req)
            my_engine = o["engine"]
            my_comp = o["kind"] == "comp"

            def need(j):
                oj = self.ops[j]
                if my_comp and oj["kind"] == "comp" and oj["engine"] == my_engine == "tensor":
                    return
                k2, o2 = ordinal[j]
                myreq[k2] = max(myreq.get(k2, 0), o2 + 1)

            for (nm, lo, hi) in o["reads"]:
                for (l2, h2, w2, j) in last_access.get(nm, ()):
                    if w2 and l2 < hi and lo < h2:
                        need(j)
            for (nm, lo, hi) in o["writes"]:
                for (l2, h2, w2, j) in last_access.get(nm, ()):
                    if l2 < hi and lo < h2:
                        need(j)
            k = counts.get(key, 0)
            ordinal[i] = (key, k)
            counts[key] = k + 1
            if o["kind"] in ("dma", "sdma", "cc"):
                gate[i] = (key, k)
            req.append(myreq)
            for (nm, lo, hi) in o["writes"]:
                last_access.setdefault(nm, []).append((lo, hi, True, i))
            for (nm, lo, hi) in o["reads"]:
                last_access.setdefault(nm, []).append((lo, hi, False, i))
        totals = dict(counts)

        with self._es:
            sems = {key: self._es.enter_context(nc.semaphore(f"S{key[0]}{key[1]}"))
                    for key in totals}
            blk = self._es.enter_context(nc.Block())

            def section(eng_name):
                def body(eng):
                    hw = {key: 0 for key in totals}
                    def wait(key, k):
                        if k > hw[key]:
                            eng.wait_ge(sems[key], step_of(key[0]) * k)
                            hw[key] = k
                    for i, o in enumerate(self.ops):
                        if o["kind"] == "barrier" or o["engine"] != eng_name:
                            continue
                        if o["kind"] == "raw":
                            for key, k in (req[i] or {}).items():
                                wait(key, k)
                            o["fn"](eng)
                            continue
                        for key, k in req[i].items():
                            wait(key, k)
                        if i in gate:
                            wait(gate[i][0], gate[i][1])
                        inst = o["fn"](eng)
                        key = ordinal[i][0]
                        inst.then_inc(sems[key], step_of(key[0]))
                    if eng_name == "sync":
                        for key, tot in totals.items():
                            wait(key, tot)
                return body

            for name in ENGINES:
                getattr(blk, name)(section(name))
        return nc


def shard_rows(x, n=NCORES):
    x = np.ascontiguousarray(x)
    return np.split(x, n, axis=0)


def run(prog, in_maps, cores=None):
    cores = cores if cores is not None else list(range(len(in_maps)))
    res = run_bass_kernel_spmd(prog.nc, in_maps, core_ids=cores)
    return res.results



def emit_s1(p, g, x, xn_out, xnb_out, xnlo_out, rows):
    with p.scope():
        xt = p.sb(g + "xt", [128, D], F32)
        sq = p.sb(g + "sq", [128, D], F32)
        xnt = p.sb(g + "xnt", [128, D], F32)
        xnbt = p.sb(g + "xnbt", [128, D], BF16)
        xnlot = p.sb(g + "xnlot", [128, D], BF16)
        st = p.sb(g + "st", [128, 8], F32)
        nb = p.sb(g + "nb", [128, 1], F32)
        nt = rows // 128
        for i in range(nt):
            r = slice(i * 128, (i + 1) * 128)
            p.dma(xt[:], x[r, :])
            p.a("activation", sq[:], xt[:], AF.Square, accum_out=st[:, 1:2])
            p.v("tensor_reduce", st[:, 0:1], xt[:], AX, Op.add)
            p.v("tensor_scalar", st[:, 2:3], st[:, 0:1], 1.0 / D, None, Op.mult)
            p.v("tensor_scalar", st[:, 3:4], st[:, 1:2], 1.0 / D, None, Op.mult)
            p.v("tensor_tensor", st[:, 4:5], st[:, 2:3], st[:, 2:3], Op.mult)
            p.v("tensor_tensor", st[:, 5:6], st[:, 3:4], st[:, 4:5], Op.subtract)
            p.v("tensor_scalar", st[:, 5:6], st[:, 5:6], 1e-5, None, Op.add)
            p.a("activation", st[:, 6:7], st[:, 5:6], AF.Sqrt)
            p.v("reciprocal", st[:, 7:8], st[:, 6:7])
            p.v("tensor_scalar", nb[:], st[:, 2:3], -1.0, None, Op.mult)
            p.v("tensor_scalar", xnt[:], xt[:], nb[:], st[:, 7:8], Op.add, Op.mult)
            p.v("tensor_copy", xnbt[:], xnt[:])
            p.v("tensor_tensor", xnlot[:], xnt[:], xnbt[:], Op.subtract)
            p.dma(xn_out[r, :], xnt[:])
            p.dma(xnb_out[r, :], xnbt[:])
            p.dma(xnlo_out[r, :], xnlot[:])


def build_s1(rows):
    p = Prog()
    x = p.din("x", [rows, D], F32)
    xn_out = p.dout("xn", [rows, D], F32)
    xnb_out = p.dout("xnb", [rows, D], BF16)
    xnlo_out = p.dout("xnlo", [rows, D], BF16)
    emit_s1(p, "s1_", x, xn_out, xnb_out, xnlo_out, rows)
    p.finish()
    return p


def stage1(x_full):
    p = build_s1(TPC)
    shards = shard_rows(x_full.reshape(T, D))
    res = run(p, [{"x": s} for s in shards])
    xn = np.concatenate([r["xn"] for r in res], 0)
    xnb = np.concatenate([r["xnb"] for r in res], 0)
    xnlo = np.concatenate([r["xnlo"] for r in res], 0)
    return xn, xnb, xnlo





SQS = 1.0 / float(np.sqrt(np.float32(S)))
NEG = -1.0e9

def attn_masks():
    i = np.arange(128)[:, None]
    j = np.arange(512)[None, :]
    bm = np.stack([np.where(128 * pat + i >= j, 0.0, NEG) for pat in range(4)])
    am = np.stack([np.where(j < 128 * r + i, 0.0, NEG) for r in range(4)])
    return bm.astype(BF), am.astype(BF)

def emit_s2(p, g, xnb, Whead, bhead, bmask_in, amask_in, identb_in, identf_in, zout):
    with p.scope():
        wraw = p.sb(g + "wraw", [64, 3, D], F32)
        wbf = p.sb(g + "wbf", [64, 3, D], BF16)
        wT = p.sb(g + "wT", [128, 8, 3, 128], BF16)
        bqt = p.sb(g + "bqt", [128, 2], F32)
        bvr = p.sb(g + "bvr", [1, 128], F32)
        ones1 = p.sb(g + "ones1", [1, 128], F32)
        bmask = p.sb(g + "bmask_s", [128, 4, 512], BF16)
        amask = p.sb(g + "amask_s", [128, 4, 512], BF16)
        identb = p.sb(g + "identb_s", [128, 128], BF16)
        identf = p.sb(g + "identf_s", [128, 128], F32)
        xnT = p.sb(g + "xnT", [128, 2, 8, 512], BF16)
        qTs = p.sb(g + "qTs", [128, S], BF16)
        kTs = p.sb(g + "kTs", [128, S], BF16)
        vtok = p.sb(g + "vtok", [128, 2, 64, 65], BF16)
        zs = p.sb(g + "zs", [65, 512], F32)
        ztok = p.sb(g + "ztok", [128, 66], F32)
        escr = p.sb(g + "escr", [128, 2, 2, 512], BF16)
        pbuf = p.sb(g + "pbuf", [128, 2, 4, 512], BF16)
        dcols = p.sb(g + "dcols", [128, 2, 17], F32)
        dlow = p.sb(g + "dlow", [128, 2, 64], F32)
        rec = p.sb(g + "rec", [128, 1], F32)

        pq = [p.ps(g + f"pq{i}", [128, 512]) for i in range(4)]
        pzf = [p.ps(g + f"pz{i}", [128, 512]) for i in range(2)]
        pz = [pzf[i][0:65, :] for i in range(2)]
        pj = p.ps(g + "pj", [128, 512])
        pk = p.ps(g + "pk", [128, 512])
        pvp = pq[3][:, 0:128]
        ptr = pj[:, 0:65]
        bankA = [[pq[0], pq[2], pzf[0]], [pq[1], pq[3], pzf[1]]]
        bankB = [[pq[0], pq[2], pj], [pq[1], pq[3], pk]]

        for aa in range(4):
            p.dma(bmask[:, aa, :], bmask_in[aa])
            p.dma(amask[:, aa, :], amask_in[aa])
        p.dma(identb[:], identb_in)
        p.dma(identf[:], identf_in)
        p.op("vector", "memset", ones1[:], 1.0)
        p.op("vector", "memset", vtok[:], 0.0)

        for jp in range(2):
            jA, jB = 2 * jp, 2 * jp + 1
            for jj, pos in ((jA, 0), (jB, 1)):
                for m in range(3):
                    p.dma(wraw[:, m, :], Whead[jj, m])
                p.v("tensor_copy", wbf[:], wraw[:])
                for m in range(3):
                    for dc in range(8):
                        p.dma(wT[:, dc, m, pos * 64:(pos + 1) * 64],
                              wbf[:, m, dc * 128:(dc + 1) * 128], q="scalar", transpose=True)
                p.dma(bqt[pos * 64:(pos + 1) * 64, 0:1], bhead[jj, 0].unsqueeze(1))
                p.dma(bqt[pos * 64:(pos + 1) * 64, 1:2], bhead[jj, 1].unsqueeze(1))
                p.dma(bvr[:, pos * 64:(pos + 1) * 64], bhead[jj, 2].unsqueeze(0))
            p.op("vector", "memset", vtok[:, :, :, 64:65], 1.0)
            for ch in range(16):
                trows = slice(ch * 512, (ch + 1) * 512)
                par = ch % 2
                for dc in range(8):
                    p.dma(xnT[:, par, dc, :], xnb[trows, dc * 128:(dc + 1) * 128],
                          q="scalar", transpose=True)
                for m, dst in ((0, qTs), (1, kTs)):
                    for dc in range(8):
                        p.t("matmul", pj[:], wT[:, dc, m, :], xnT[:, par, dc, :],
                            start=(dc == 0), stop=(dc == 7))
                    p.v("tensor_scalar", dst[:, ch * 512:(ch + 1) * 512], pj[:],
                        bqt[:, m:m + 1], None, Op.add)
                for sub in range(4):
                    kt = ch * 4 + sub
                    for dc in range(8):
                        p.t("matmul", pvp, xnT[:, par, dc, sub * 128:(sub + 1) * 128],
                            wT[:, dc, 2, :], start=(dc == 0), stop=False)
                    p.t("matmul", pvp, ones1[:], bvr[:], start=False, stop=True)
                    p.v("tensor_copy", vtok[:, 0, kt, 0:64], pvp[:, 0:64])
                    p.v("tensor_copy", vtok[:, 1, kt, 0:64], pvp[:, 64:128])
            aw = []
            for qt in range(64):
                for ks in range(qt // 4 + 1):
                    aw.append((qt, ks))

            def emit_qk_A(i):
                qt, ks = aw[i]
                nsup = qt // 4
                b0, b1 = bankA[0][i % 3], bankA[1][i % 3]
                if ks == nsup:
                    rr = qt % 4
                    p.t("matmul", b0[:], identb[:], amask[:, rr, :], start=True, stop=False)
                    p.t("matmul", b1[:], identb[:], amask[:, rr, :], start=True, stop=False)
                    st0 = False
                else:
                    st0 = True
                p.t("matmul", b0[:], qTs[0:64, qt * 128:(qt + 1) * 128],
                    kTs[0:64, ks * 512:(ks + 1) * 512], start=st0, stop=True,
                    tile_position=(0, 0))
                p.t("matmul", b1[:], qTs[64:128, qt * 128:(qt + 1) * 128],
                    kTs[64:128, ks * 512:(ks + 1) * 512], start=st0, stop=True,
                    tile_position=(64, 0))

            emit_qk_A(0)
            if len(aw) > 1:
                emit_qk_A(1)
            for i, (qt, ks) in enumerate(aw):
                if i + 2 < len(aw):
                    emit_qk_A(i + 2)
                p.a("activation", escr[:, 0, ks % 2, :], bankA[0][i % 3][:], AF.Exp, scale=SQS,
                    accum_out=dcols[:, 0, ks:ks + 1])
                p.a("activation", escr[:, 1, ks % 2, :], bankA[1][i % 3][:], AF.Exp, scale=SQS,
                    accum_out=dcols[:, 1, ks:ks + 1])
                if ks == qt // 4:
                    for hh in range(2):
                        p.v("tensor_reduce", dlow[:, hh, qt:qt + 1], dcols[:, hh, 0:qt // 4 + 1], AX, Op.add)
            for qs in range(16):
                kts = list(range(4 * qs, 64))

                def emit_kq_B(i, qs=qs):
                    kt = kts[i]
                    pat = kt - 4 * qs
                    b0, b1 = bankB[0][i % 3], bankB[1][i % 3]
                    if pat <= 3:
                        p.t("matmul", b0[:], identb[:], bmask[:, pat, :], start=True, stop=False)
                        p.t("matmul", b1[:], identb[:], bmask[:, pat, :], start=True, stop=False)
                    p.t("matmul", b0[:], kTs[0:64, kt * 128:(kt + 1) * 128],
                        qTs[0:64, qs * 512:(qs + 1) * 512], start=(pat > 3), stop=True,
                        tile_position=(0, 0))
                    p.t("matmul", b1[:], kTs[64:128, kt * 128:(kt + 1) * 128],
                        qTs[64:128, qs * 512:(qs + 1) * 512], start=(pat > 3), stop=True,
                        tile_position=(64, 0))

                emit_kq_B(0)
                if len(kts) > 1:
                    emit_kq_B(1)
                for i, kt in enumerate(kts):
                    if i + 2 < len(kts):
                        emit_kq_B(i + 2)
                    par4 = kt % 4
                    p.a("activation", pbuf[:, 0, par4, :], bankB[0][i % 3][:], AF.Exp, scale=SQS)
                    p.a("activation", pbuf[:, 1, par4, :], bankB[1][i % 3][:], AF.Exp, scale=SQS)
                    p.t("matmul", pz[0][:], vtok[:, 0, kt, :], pbuf[:, 0, par4, :],
                        start=(kt == 4 * qs), stop=(kt == 63))
                    p.t("matmul", pz[1][:], vtok[:, 1, kt, :], pbuf[:, 1, par4, :],
                        start=(kt == 4 * qs), stop=(kt == 63))
                for hh, jj in ((0, jA), (1, jB)):
                    p.v("tensor_copy", zs[:], pz[hh][:])
                    for q4 in range(4):
                        qt = qs * 4 + q4
                        p.t("transpose", ptr, zs[:, q4 * 128:(q4 + 1) * 128], identf[0:65, 0:65])
                        p.v("tensor_copy", ztok[:, 0:65], ptr)
                        p.v("tensor_tensor", ztok[:, 65:66], ztok[:, 64:65], dlow[:, hh, qt:qt + 1], Op.add)
                        p.v("reciprocal", rec[:], ztok[:, 65:66])
                        p.v("tensor_scalar", ztok[:, 0:64], ztok[:, 0:64], rec[:], None, Op.mult)
                        p.dma(zout[jj, qt * 128:(qt + 1) * 128, :], ztok[:, 0:64])

def build_s2():
    p = Prog()
    xnb = p.din("xnb", [S, D], BF16)
    Whead = p.din("Whead", [4, 3, 64, D], F32)
    bhead = p.din("bhead", [4, 3, 64], F32)
    bmask_in = p.din("bmask", [4, 128, 512], BF16)
    amask_in = p.din("amask", [4, 128, 512], BF16)
    identb_in = p.din("identb", [128, 128], BF16)
    identf_in = p.din("identf", [128, 128], F32)
    zout = p.dout("z4", [4, S, 64], F32)
    emit_s2(p, "s2_", xnb, Whead, bhead, bmask_in, amask_in, identb_in, identf_in, zout)
    p.finish()
    return p

def stage2(xnb_full, Wq, bq, Wk, bk, Wv, bv):
    bm, am = attn_masks()
    identb = np.eye(128).astype(BF)
    identf = np.eye(128, dtype=np.float32)
    xnb3 = xnb_full.reshape(B, S, D)
    in_maps = []
    for c in range(NCORES):
        hs = [(c * 4 + j) % 16 for j in range(4)]
        b = (c * 4) // 16
        Wh = np.stack([np.stack([W[h * 64:(h + 1) * 64] for W in (Wq, Wk, Wv)]) for h in hs])
        bh_ = np.stack([np.stack([bb[h * 64:(h + 1) * 64] for bb in (bq, bk, bv)]) for h in hs])
        in_maps.append({"xnb": np.ascontiguousarray(xnb3[b]), "Whead": np.ascontiguousarray(Wh),
                        "bhead": np.ascontiguousarray(bh_), "bmask": bm, "amask": am,
                        "identb": identb, "identf": identf})
    p = build_s2()
    res = run(p, in_maps)
    z = np.zeros((B, S, 16, 64), np.float32)
    for c in range(NCORES):
        b = (c * 4) // 16
        for j in range(4):
            h = (c * 4 + j) % 16
            z[b, :, h, :] = res[c]["z4"][j]
    return z.reshape(B, S, D)




def emit_s3(p, g, xn, z16, gam, bet, s_out, rows):
    with p.scope():
        xt = p.sb(g + "xt", [128, D], F32)
        zt = p.sb(g + "zt", [128, D], F32)
        sq = p.sb(g + "sq", [128, D], F32)
        ot = p.sb(g + "ot", [128, D], F32)
        st = p.sb(g + "st", [128, 8], F32)
        nb = p.sb(g + "nb", [128, 1], F32)
        grow = p.sb(g + "grow", [1, 2, D], F32)
        gbc = p.sb(g + "gbc", [128, 2, D], F32)
        pb = p.ps(g + "pb", [128, 512])
        ones1 = p.sb(g + "ones1", [1, 128], F32)
        p.op("vector", "memset", ones1[:], 1.0)
        p.dma(grow[:, 0, :], gam.unsqueeze(0))
        p.dma(grow[:, 1, :], bet.unsqueeze(0))
        for m in range(2):
            for hf in range(2):
                p.t("matmul", pb[:], ones1[:], grow[:, m, hf * 512:(hf + 1) * 512], start=True, stop=True)
                p.v("tensor_copy", gbc[:, m, hf * 512:(hf + 1) * 512], pb[:])
        nt = rows // 128
        for i in range(nt):
            r = slice(i * 128, (i + 1) * 128)
            p.dma(xt[:], xn[r, :])
            for h16 in range(16):
                p.dma(zt[:, h16 * 64:(h16 + 1) * 64], z16[h16, r, :])
            p.v("tensor_tensor", xt[:], xt[:], zt[:], Op.add)
            p.a("activation", sq[:], xt[:], AF.Square, accum_out=st[:, 1:2])
            p.v("tensor_reduce", st[:, 0:1], xt[:], AX, Op.add)
            p.v("tensor_scalar", st[:, 2:3], st[:, 0:1], 1.0 / D, None, Op.mult)
            p.v("tensor_scalar", st[:, 3:4], st[:, 1:2], 1.0 / D, None, Op.mult)
            p.v("tensor_tensor", st[:, 4:5], st[:, 2:3], st[:, 2:3], Op.mult)
            p.v("tensor_tensor", st[:, 5:6], st[:, 3:4], st[:, 4:5], Op.subtract)
            p.v("tensor_scalar", st[:, 5:6], st[:, 5:6], 1e-5, None, Op.add)
            p.a("activation", st[:, 6:7], st[:, 5:6], AF.Sqrt)
            p.v("reciprocal", st[:, 7:8], st[:, 6:7])
            p.v("tensor_scalar", nb[:], st[:, 2:3], -1.0, None, Op.mult)
            p.v("tensor_scalar", ot[:], xt[:], nb[:], st[:, 7:8], Op.add, Op.mult)
            p.v("tensor_tensor", ot[:], ot[:], gbc[:, 0, :], Op.mult)
            p.v("tensor_tensor", ot[:], ot[:], gbc[:, 1, :], Op.add)
            p.dma(s_out[r, :], ot[:])


def build_s3(rows):
    p = Prog()
    xn = p.din("xn", [rows, D], F32)
    zi = p.din("z", [16, rows, 64], F32)
    gam = p.din("gamma", [D], F32)
    bet = p.din("beta", [D], F32)
    s_out = p.dout("s0", [rows, D], F32)
    emit_s3(p, "s3_", xn, zi, gam, bet, s_out, rows)
    p.finish()
    return p


def stage3(xn, z, gamma, beta):
    p = build_s3(TPC)
    xs = shard_rows(xn.reshape(T, D))
    z4 = z.reshape(B, S, 16, 64)
    zsh = [np.ascontiguousarray(z4[c // 4, (c % 4) * TPC:(c % 4 + 1) * TPC].transpose(1, 0, 2))
           for c in range(NCORES)]
    res = run(p, [{"xn": a, "z": b, "gamma": gamma, "beta": beta} for a, b in zip(xs, zsh)])
    return np.concatenate([r["s0"] for r in res], 0)



def build_s4(NTT=T // 128):
    p = Prog()
    xnb = p.din("xnb", [T, D], BF16)
    xnlo = p.din("xnlo", [T, D], BF16)
    gw = p.din("gw", [64, D], F32)
    iota_tok = p.din("iota_tok", [128, 64], F32)
    idin = p.din("identf", [128, 128], F32)
    table = p.dout("table", [T, 1], U32)
    gbidx = p.dout("gbidx", [T, 4], U32)
    wts_o = p.dout("wts", [T, 4], F32)

    gwraw = p.sb("gwraw", [64, D], F32)
    gwbf = p.sb("gwbf", [64, D], BF16)
    gwlosc = p.sb("gwlosc", [64, D], F32)
    gwlo = p.sb("gwlo", [64, D], BF16)
    gwT = p.sb("gwT", [128, 8, 64], BF16)
    gwloT = p.sb("gwloT", [128, 8, 64], BF16)
    xnT = p.sb("xnT", [128, 8, 128], BF16)
    xnloT = p.sb("xnloT", [128, 8, 128], BF16)
    lwork = p.sb("lwork", [128, 64], F32)
    iot = p.sb("iot", [128, 64], F32)
    oh = p.sb("oh", [128, 64], F32)
    mx = p.sb("mx", [128, 4], F32)
    idx_all = p.sb("idx_all", [128, T // 128, 4], F32)
    ew = p.sb("ew", [128, 8], F32)
    wtt = p.sb("wtt", [128, 4], F32)
    selT_all = p.sb("selT_all", [64, T], F32)
    zerT = p.sb("zerT", [64, 2048], F32)
    posT = p.sb("posT", [64, T], F32)
    carry = p.sb("carry", [64, 1], F32)
    sel = p.sb("sel", [128, 64], F32)
    pos_tok = p.sb("pos_tok", [128, 64], F32)
    poscol = p.sb("poscol", [128, 4], F32)
    base = p.sb("base", [128, 1], F32)
    km = p.sb("km", [128, 1], F32)
    ge = p.sb("ge", [128, 1], F32)
    t1 = p.sb("t1", [128, 1], F32)
    gidxf = p.sb("gidxf", [128, 1], F32)
    gidxu = p.sb("gidxu", [128, 1], U32)
    gbu = p.sb("gbu", [128, 4], U32)
    tid = p.sb("tid", [128, 1], U32)
    identf = p.sb("identf2", [128, 128], F32)
    zl = p.sb("zl", [128, 128], U32)
    pl = p.ps("pl", [128, 64])
    pt2 = p.ps("pt2", [64, 128])
    ptp = p.ps("ptp", [128, 64])

    p.dma(identf[:], idin)
    p.dma(iot[:], iota_tok)
    p.op("vector", "memset", zerT[:], 0.0)
    p.op("vector", "memset", zl[:], 0)
    p.dma(table.rearrange("(q a) x -> q (a x)", q=128), zl[:])
    p.dma(gwraw[:], gw)
    p.v("tensor_copy", gwbf[:], gwraw[:])
    p.v("tensor_tensor", gwlosc[:], gwraw[:], gwbf[:], Op.subtract)
    p.v("tensor_copy", gwlo[:], gwlosc[:])
    for dc in range(8):
        p.dma(gwT[:, dc, :], gwbf[:, dc * 128:(dc + 1) * 128], q="scalar", transpose=True)
        p.dma(gwloT[:, dc, :], gwlo[:, dc * 128:(dc + 1) * 128], q="scalar", transpose=True)

    breg = p.nc.gpsimd.alloc_register()
    p.raw("gpsimd", lambda e: e.reg_mov(breg, T - 1))
    for tt in range(NTT):
        r = slice(tt * 128, (tt + 1) * 128)
        for dc in range(8):
            p.dma(xnT[:, dc, :], xnb[r, dc * 128:(dc + 1) * 128], q="scalar", transpose=True)
            p.dma(xnloT[:, dc, :], xnlo[r, dc * 128:(dc + 1) * 128], q="scalar", transpose=True)
        for dc in range(8):
            p.t("matmul", pl[:], xnT[:, dc, :], gwT[:, dc, :], start=(dc == 0), stop=False)
            p.t("matmul", pl[:], xnT[:, dc, :], gwloT[:, dc, :], start=False, stop=False)
            p.t("matmul", pl[:], xnloT[:, dc, :], gwT[:, dc, :], start=False, stop=(dc == 7))
        p.v("tensor_copy", lwork[:], pl[:])
        p.op("vector", "memset", sel[:], 0.0)
        for k in range(4):
            p.v("tensor_reduce", mx[:, k:k + 1], lwork[:], AX, Op.max)
            p.v("tensor_scalar", oh[:], lwork[:], mx[:, k:k + 1], None, Op.is_equal)
            p.v("scalar_tensor_tensor", lwork[:], oh[:], -1e30, lwork[:], Op.mult, Op.add)
            p.v("tensor_tensor", sel[:], sel[:], oh[:], Op.add)
            p.v("tensor_tensor", oh[:], oh[:], iot[:], Op.mult)
            p.v("tensor_reduce", idx_all[:, tt, k:k + 1], oh[:], AX, Op.add)
        p.v("tensor_scalar", ew[:, 6:7], mx[:, 0:1], -1.0, None, Op.mult)
        p.v("tensor_scalar", ew[:, 0:4], mx[:], ew[:, 6:7], None, Op.add)
        p.a("activation", ew[:, 0:4], ew[:, 0:4], AF.Exp)
        p.v("tensor_reduce", ew[:, 4:5], ew[:, 0:4], AX, Op.add)
        p.v("reciprocal", ew[:, 5:6], ew[:, 4:5])
        p.v("tensor_scalar", wtt[:], ew[:, 0:4], ew[:, 5:6], None, Op.mult)
        p.dma(wts_o[r, :], wtt[:])
        p.t("transpose", pt2[:], sel[:, 0:64], identf[:])
        p.v("tensor_copy", selT_all[:, r], pt2[:])
    p.op("vector", "memset", carry[:], 0.0)
    for ch in range(T // 2048):
        r = slice(ch * 2048, (ch + 1) * 2048)
        p.v("tensor_tensor_scan", posT[:, r], zerT[:], selT_all[:, r], carry[:], Op.add, Op.add)
        p.v("tensor_copy", carry[:], posT[:, ch * 2048 + 2047:ch * 2048 + 2048])
    for tt in range(NTT):
        r = slice(tt * 128, (tt + 1) * 128)
        p.t("transpose", ptp[:], posT[:, r], identf[0:64, 0:64])
        p.v("tensor_copy", pos_tok[:, 0:64], ptp[:])
        p.g("iota", tid[:], pattern=[[0, 1]], base=tt * 128, channel_multiplier=1)
        for k in range(4):
            p.v("tensor_scalar", oh[:], iot[:], idx_all[:, tt, k:k + 1], None, Op.is_equal)
            p.v("tensor_tensor", oh[:], oh[:], pos_tok[:, 0:64], Op.mult)
            p.v("tensor_reduce", poscol[:, k:k + 1], oh[:], AX, Op.add)
        for k in range(4):
            p.v("tensor_scalar", poscol[:, k:k + 1], poscol[:, k:k + 1], -1.0, None, Op.add)
            p.v("tensor_scalar", base[:], idx_all[:, tt, k:k + 1], 256.0, None, Op.mult)
            p.v("tensor_tensor", base[:], base[:], poscol[:, k:k + 1], Op.add)
            p.v("tensor_scalar", ge[:], poscol[:, k:k + 1], 256.0, None, Op.is_ge)
            p.v("scalar_tensor_tensor", gidxf[:], ge[:], 100000.0, base[:], Op.mult, Op.add)
            p.v("tensor_copy", gidxu[:], gidxf[:])
            p.idma("gpsimd", out=table, out_offset=IndirectOffsetOnAxis(ap=gidxu[:], axis=0),
                   in_=tid[:], in_offset=None, bounds_check=breg, oob_is_err=False)
            p.v("tensor_scalar", km[:], poscol[:, k:k + 1], 256.0, None, Op.is_lt)
            p.v("tensor_tensor", t1[:], base[:], km[:], Op.mult)
            p.v("scalar_tensor_tensor", gidxf[:], ge[:], float(T), t1[:], Op.mult, Op.add)
            p.v("tensor_copy", gbu[:, k:k + 1], gidxf[:])
        p.dma(gbidx[r, :], gbu[:])
    p.finish()
    return p

def stage4(xnb_full, xnlo_full, gate_w, ntt=T // 128):
    p = build_s4(ntt)
    iota_tok = np.broadcast_to(np.arange(64, dtype=np.float32), (128, 64)).copy()
    identf = np.eye(128, dtype=np.float32)
    im = [{"xnb": xnb_full, "xnlo": xnlo_full, "gw": gate_w, "iota_tok": iota_tok, "identf": identf}
          for _ in range(NCORES)]
    res = run(p, im)
    r = res[0]
    return r["table"][:, 0], r["gbidx"], r["wts"]





def emit_dequant(p, c_t, scexp, t0, t1, t2, dec, W):
    for nib in (0, 1):
        if nib == 0:
            p.v("tensor_scalar", t1[:, 0:W], c_t[:], 7, 22, Op.bitwise_and, Op.logical_shift_left)
            p.v("tensor_scalar", t0[:, 0:W], c_t[:], 8, 28, Op.bitwise_and, Op.logical_shift_left)
        else:
            p.v("tensor_scalar", t1[:, 0:W], c_t[:], 4, 7, Op.logical_shift_right, Op.bitwise_and)
            p.v("tensor_scalar", t1[:, 0:W], t1[:, 0:W], 22, None, Op.logical_shift_left)
            p.v("tensor_scalar", t0[:, 0:W], c_t[:], 128, 24, Op.bitwise_and, Op.logical_shift_left)
        p.v("tensor_scalar", t1[:, 0:W], t1[:, 0:W], 0x3F000000, None, Op.add)
        p.v("tensor_tensor", t0[:, 0:W], t0[:, 0:W], scexp[:, 0:W].bitcast(I32), Op.bitwise_or)
        f0 = t1[:, 0:W].bitcast(F32)
        p.v("tensor_scalar", t2[:, 0:W], f0, 1.0, -1.0, Op.min, Op.add)
        p.v("tensor_tensor", t2[:, 0:W], t2[:, 0:W], f0, Op.add)
        p.v("tensor_tensor", dec[:, nib:2 * W:2], t2[:, 0:W], t0[:, 0:W].bitcast(F32), Op.mult)

def emit_scale_expand(p, scl, scexp, G8):
    p.v("tensor_copy", scexp[:, 0:G8 * 64].rearrange("p (g k) -> p g k", k=64),
        scl[:, 0:G8].unsqueeze(2).broadcast_to([128, G8, 64]))

def emit_s5(p, g, xnb, tbl, gp, gs, up, us, dp, ds, idb_in, eo_out):
    with p.scope():
        ixt = p.sb(g + "ixt", [128, 2], U32)
        buftok = p.sb(g + "buftok", [128, 2, D], BF16)
        bufT = p.sb(g + "bufT", [128, 8, 256], BF16)
        identb = p.sb(g + "identb_s", [128, 128], BF16)
        pk = p.sb(g + "pk", [128, 512], I32)
        scl = p.sb(g + "scl", [128, 8], F32)
        scexp = p.sb(g + "scexp", [128, 512], F32)
        t0 = p.sb(g + "t0", [128, 512], I32)
        t1 = p.sb(g + "t1", [128, 512], I32)
        t2 = p.sb(g + "t2", [128, 512], F32)
        dec = p.sb(g + "dec", [128, 1024], BF16)
        wgT = p.sb(g + "wgT", [128, 8, 768], BF16)
        wuT = p.sb(g + "wuT", [128, 8, 768], BF16)
        wdT = p.sb(g + "wdT", [128, 6, 1024], BF16)
        gsl = p.sb(g + "gsl", [128, 256], F32)
        hsb = p.sb(g + "hsb", [128, 6, 256], BF16)
        eos = p.sb(g + "eos", [128, 1024], BF16)
        ptb = p.ps(g + "ptb", [128, 128], BF16)
        pg = p.ps(g + "pg", [128, 256])
        pu = p.ps(g + "pu", [128, 256])
        pdn = p.ps(g + "pdn", [128, 512])

        p.dma(identb[:], idb_in)
        for ee in range(8):
            for h in range(2):
                p.dma(ixt[:, h:h + 1], tbl[ee, h * 128:(h + 1) * 128].unsqueeze(1))
            for h in range(2):
                p.idma("gpsimd", out=buftok[:, h, :], out_offset=None, in_=xnb,
                       in_offset=IndirectOffsetOnAxis(ap=ixt[:, h:h + 1], axis=0))
            for h in range(2):
                for dc in range(8):
                    p.t("transpose", ptb[:], buftok[:, h, dc * 128:(dc + 1) * 128], identb[:])
                    p.v("tensor_copy", bufT[:, dc, h * 128:(h + 1) * 128], ptb[:])
            for (pkd, sdd, wdst) in ((gp, gs, wgT), (up, us, wuT)):
                for ic in range(6):
                    p.dma(pk[:], pkd[ee, ic * 128:(ic + 1) * 128, :])
                    p.dma(scl[:], sdd[ee, ic * 128:(ic + 1) * 128, :])
                    emit_scale_expand(p, scl, scexp, 8)
                    emit_dequant(p, pk[:], scexp, t0, t1, t2, dec, 512)
                    for dc in range(8):
                        p.t("transpose", ptb[:], dec[:, dc * 128:(dc + 1) * 128], identb[:])
                        p.v("tensor_copy", wdst[:, dc, ic * 128:(ic + 1) * 128], ptb[:])
            for dc in range(8):
                p.dma(pk[:, 0:384], dp[ee, dc * 128:(dc + 1) * 128, :])
                p.dma(scl[:, 0:6], ds[ee, dc * 128:(dc + 1) * 128, :])
                emit_scale_expand(p, scl, scexp, 6)
                emit_dequant(p, pk[:, 0:384], scexp, t0, t1, t2, dec, 384)
                for ic in range(6):
                    p.t("transpose", ptb[:], dec[:, ic * 128:(ic + 1) * 128], identb[:])
                    p.v("tensor_copy", wdT[:, ic, dc * 128:(dc + 1) * 128], ptb[:])
            for ic in range(6):
                for dc in range(8):
                    p.t("matmul", pg[:], wgT[:, dc, ic * 128:(ic + 1) * 128], bufT[:, dc, :],
                        start=(dc == 0), stop=(dc == 7))
                for dc in range(8):
                    p.t("matmul", pu[:], wuT[:, dc, ic * 128:(ic + 1) * 128], bufT[:, dc, :],
                        start=(dc == 0), stop=(dc == 7))
                p.a("activation", gsl[:], pg[:], AF.Silu)
                p.v("tensor_tensor", hsb[:, ic, :], gsl[:], pu[:], Op.mult)
            for c2 in range(2):
                for dh in range(2):
                    for ic in range(6):
                        p.t("matmul", pdn[:], hsb[:, ic, c2 * 128:(c2 + 1) * 128],
                            wdT[:, ic, dh * 512:(dh + 1) * 512], start=(ic == 0), stop=(ic == 5))
                    p.v("tensor_copy", eos[:, dh * 512:(dh + 1) * 512], pdn[:])
                p.dma(eo_out[ee, c2 * 128:(c2 + 1) * 128, :], eos[:])


def build_s5():
    p = Prog()
    xnb = p.din("xnb", [T, D], BF16)
    tbl = p.din("tbl", [8, 256], U32)
    gp = p.din("gp", [8, 768, 512], I32)
    gs = p.din("gs", [8, 768, 8], F32)
    up = p.din("up", [8, 768, 512], I32)
    us = p.din("us", [8, 768, 8], F32)
    dp = p.din("dp", [8, 1024, 384], I32)
    ds = p.din("ds", [8, 1024, 6], F32)
    idb_in = p.din("identb", [128, 128], BF16)
    eo_out = p.dout("eo", [8, 256, D], BF16)
    emit_s5(p, "s5_", xnb, tbl, gp, gs, up, us, dp, ds, idb_in, eo_out)
    p.finish()
    return p


def stage5(xnb_full, table, packs):
    gp, gs_, up, us, dp, ds = packs
    p = build_s5()
    idb = np.eye(128).astype(BF)
    t64 = table.reshape(64, 256).astype(np.uint32)
    im = []
    for c in range(NCORES):
        sl = slice(c * 8, (c + 1) * 8)
        im.append({"xnb": xnb_full, "tbl": np.ascontiguousarray(t64[sl]),
                   "gp": np.ascontiguousarray(gp[sl]), "gs": np.ascontiguousarray(gs_[sl]),
                   "up": np.ascontiguousarray(up[sl]), "us": np.ascontiguousarray(us[sl]),
                   "dp": np.ascontiguousarray(dp[sl]), "ds": np.ascontiguousarray(ds[sl]),
                   "identb": idb})
    res = run(p, im)
    eo = np.concatenate([r["eo"] for r in res], 0)
    return eo





def emit_s6(p, g, eoall, gbix, wtsi, xi, s0i, xnbr, sgp, sgs, sup, sus, sdp, sds, idb_in, x1_out, rows):
    with p.scope():
        identb = p.sb(g + "identb_s", [128, 128], BF16)
        pk = p.sb(g + "pk", [128, 512], I32)
        scl = p.sb(g + "scl", [128, 8], F32)
        scexp = p.sb(g + "scexp", [128, 512], F32)
        t0 = p.sb(g + "t0", [128, 512], I32)
        t1 = p.sb(g + "t1", [128, 512], I32)
        t2 = p.sb(g + "t2", [128, 512], F32)
        dec = p.sb(g + "dec", [128, 1024], BF16)
        wsgT = p.sb(g + "wsgT", [128, 8, 768], BF16)
        wsuT = p.sb(g + "wsuT", [128, 8, 768], BF16)
        wsdT = p.sb(g + "wsdT", [128, 6, 1024], BF16)
        xnT = p.sb(g + "xnT", [128, 8, 512], BF16)
        gsl = p.sb(g + "gsl", [128, 512], F32)
        hshT = p.sb(g + "hshT", [128, 6, 512], BF16)
        shseg = p.sb(g + "shseg", [128, 512], BF16)
        sh_tok = p.sb(g + "sh_tok", [128, 4, D], BF16)
        acc = p.sb(g + "acc", [128, D], F32)
        gbuf = p.sb(g + "gbuf", [128, D], BF16)
        ixc = p.sb(g + "ixc", [128, 4], U32)
        wtc = p.sb(g + "wtc", [128, 4], F32)
        g0t = p.sb(g + "g0t", [128, D], F32)
        xt = p.sb(g + "xt", [128, D], F32)
        s0t = p.sb(g + "s0t", [128, D], F32)
        ptb = p.ps(g + "ptb", [128, 128], BF16)
        pg = p.ps(g + "pg", [128, 512])
        pu = p.ps(g + "pu", [128, 512])
        pdn = p.ps(g + "pdn", [128, 512])

        p.dma(identb[:], idb_in)
        for (pkd, sdd, wdst) in ((sgp, sgs, wsgT), (sup, sus, wsuT)):
            for ic in range(6):
                p.dma(pk[:], pkd[ic * 128:(ic + 1) * 128, :])
                p.dma(scl[:], sdd[ic * 128:(ic + 1) * 128, :])
                emit_scale_expand(p, scl, scexp, 8)
                emit_dequant(p, pk[:], scexp, t0, t1, t2, dec, 512)
                for dc in range(8):
                    p.t("transpose", ptb[:], dec[:, dc * 128:(dc + 1) * 128], identb[:])
                    p.v("tensor_copy", wdst[:, dc, ic * 128:(ic + 1) * 128], ptb[:])
        for dc in range(8):
            p.dma(pk[:, 0:384], sdp[dc * 128:(dc + 1) * 128, :])
            p.dma(scl[:, 0:6], sds[dc * 128:(dc + 1) * 128, :])
            emit_scale_expand(p, scl, scexp, 6)
            emit_dequant(p, pk[:, 0:384], scexp, t0, t1, t2, dec, 384)
            for ic in range(6):
                p.t("transpose", ptb[:], dec[:, ic * 128:(ic + 1) * 128], identb[:])
                p.v("tensor_copy", wsdT[:, ic, dc * 128:(dc + 1) * 128], ptb[:])
        nch = rows // 512
        for ch in range(nch):
            trows = slice(ch * 512, (ch + 1) * 512)
            for dc in range(8):
                p.dma(xnT[:, dc, :], xnbr[trows, dc * 128:(dc + 1) * 128], q="scalar", transpose=True)
            for ic in range(6):
                for dc in range(8):
                    p.t("matmul", pg[:], wsgT[:, dc, ic * 128:(ic + 1) * 128], xnT[:, dc, :],
                        start=(dc == 0), stop=(dc == 7))
                for dc in range(8):
                    p.t("matmul", pu[:], wsuT[:, dc, ic * 128:(ic + 1) * 128], xnT[:, dc, :],
                        start=(dc == 0), stop=(dc == 7))
                p.a("activation", gsl[:], pg[:], AF.Silu)
                p.v("tensor_tensor", hshT[:, ic, :], gsl[:], pu[:], Op.mult)
            for dc8 in range(8):
                for ic in range(6):
                    p.t("matmul", pdn[:], wsdT[:, ic, dc8 * 128:(dc8 + 1) * 128], hshT[:, ic, :],
                        start=(ic == 0), stop=(ic == 5))
                p.v("tensor_copy", shseg[:], pdn[:])
                for sub in range(4):
                    p.t("transpose", ptb[:], shseg[:, sub * 128:(sub + 1) * 128], identb[:])
                    p.v("tensor_copy", sh_tok[:, sub, dc8 * 128:(dc8 + 1) * 128], ptb[:])
            for sub in range(4):
                tt = ch * 4 + sub
                r = slice(tt * 128, (tt + 1) * 128)
                p.v("tensor_copy", acc[:], sh_tok[:, sub, :])
                p.dma(ixc[:], gbix[r, :])
                p.dma(wtc[:], wtsi[r, :])
                for k in range(4):
                    p.idma("gpsimd", out=gbuf[:], out_offset=None, in_=eoall,
                           in_offset=IndirectOffsetOnAxis(ap=ixc[:, k:k + 1], axis=0))
                    p.v("scalar_tensor_tensor", acc[:], gbuf[:], wtc[:, k:k + 1], acc[:], Op.mult, Op.add)
                p.a("activation", g0t[:], acc[:], AF.Silu)
                p.dma(xt[:], xi[r, :])
                p.dma(s0t[:], s0i[r, :])
                p.v("tensor_tensor", g0t[:], g0t[:], s0t[:], Op.mult)
                p.v("tensor_tensor", g0t[:], g0t[:], xt[:], Op.add)
                p.dma(x1_out[r, :], g0t[:])


def build_s6(rows):
    p = Prog()
    eoall = p.din("eoall", [T + 1, D], BF16)
    gbix = p.din("gbix", [rows, 4], U32)
    wtsi = p.din("wtsi", [rows, 4], F32)
    xi = p.din("x", [rows, D], F32)
    s0i = p.din("s0", [rows, D], F32)
    xnbr = p.din("xnbr", [rows, D], BF16)
    sgp = p.din("sgp", [768, 512], I32)
    sgs = p.din("sgs", [768, 8], F32)
    sup = p.din("sup", [768, 512], I32)
    sus = p.din("sus", [768, 8], F32)
    sdp = p.din("sdp", [1024, 384], I32)
    sds = p.din("sds", [1024, 6], F32)
    idb_in = p.din("identb", [128, 128], BF16)
    x1_out = p.dout("x1", [rows, D], F32)
    emit_s6(p, "s6_", eoall, gbix, wtsi, xi, s0i, xnbr, sgp, sgs, sup, sus, sdp, sds, idb_in, x1_out, rows)
    p.finish()
    return p


def stage6(eo, gbidx, wts, x, s0, xnb, shpacks):
    sgp, sgs, sup, sus, sdp, sds = shpacks
    eoall = np.concatenate([eo.reshape(T, D), np.zeros((1, D), eo.dtype)], 0)
    p = build_s6(TPC)
    idb = np.eye(128).astype(BF)
    gsh = shard_rows(gbidx); wsh = shard_rows(wts)
    xsh = shard_rows(x.reshape(T, D)); ssh = shard_rows(s0.reshape(T, D)); bsh = shard_rows(xnb.reshape(T, D))
    im = [{"eoall": eoall, "gbix": gsh[c], "wtsi": wsh[c], "x": xsh[c], "s0": ssh[c],
           "xnbr": bsh[c], "sgp": sgp, "sgs": sgs, "sup": sup, "sus": sus,
           "sdp": sdp, "sds": sds, "identb": idb} for c in range(NCORES)]
    res = run(p, im)
    return np.concatenate([r["x1"] for r in res], 0)





def emit_s8(p, g, xb, Wh, bac, adt, lnw_in, onw_in, msu_in, miu_in, idf_in, idb_in, og_out):
    with p.scope():
        NC = S // 128
        identf = p.sb(g + "identf_s", [128, 128], F32)
        identb = p.sb(g + "identb_s", [128, 128], BF16)
        msu = p.sb(g + "msu_s", [128, 128], F32)
        miu = p.sb(g + "miu_s", [128, 128], F32)
        ones1 = p.sb(g + "ones1", [1, 128], F32)
        onwB = p.sb(g + "onwB", [128, 128], F32)
        lnwt = p.sb(g + "lnwt", [128, 8], F32)
        pmm2 = [[p.ps(g + f"pmm{j}_{cp}", [128, 128]) for cp in range(2)] for j in range(2)]
        pmm = [pmm2[0][0], pmm2[1][0]]
        pA = [p.ps(g + f"pA{j}", [128, 128]) for j in range(2)]
        pB = pA
        pxs = [p.ps(g + f"px{j}", [128, 256]) for j in range(2)]
        ppool = pxs
        pr1 = pmm[0][0:1, 0:128]

        p.dma(identf[:], idf_in)
        p.dma(identb[:], idb_in)
        p.dma(msu[:], msu_in)
        p.dma(miu[:], miu_in)
        p.op("vector", "memset", ones1[:], 1.0)
        for c8 in range(8):
            p.dma(lnwt[:, c8:c8 + 1], lnw_in[c8 * 128:(c8 + 1) * 128].unsqueeze(1))
        p.dma(onwB[0:1, :], onw_in)
        p.t("matmul", pmm[0][:], ones1[:], onwB[0:1, :], start=True, stop=True)
        p.v("tensor_copy", onwB[:], pmm[0][:])

        J = []
        for j in range(2):
            d = {}
            sb = lambda nm, shp, dt: p.sb(g + nm + str(j), shp, dt)
            d["nAcol"] = sb("nAcol", [128, 1], F32)
            d["dtbcol"] = sb("dtbcol", [128, 1], F32)
            d["rawq"] = sb("rawq", [128, NC, 128], BF16)
            d["rawk"] = sb("rawk", [128, NC, 128], BF16)
            d["rawv"] = sb("rawv", [128, NC, 128], BF16)
            d["rawg"] = sb("rawg", [128, NC, 128], BF16)
            d["q_tok"] = d["rawq"]
            d["k_tok"] = d["rawk"]
            d["v_tok"] = d["rawv"]
            d["g_tok"] = d["rawg"]
            d["sg"] = sb("sg", [128, 128], F32)
            d["bt_all"] = sb("bt_all", [128, NC], F32)
            d["st8"] = sb("st8", [128, 8], F32)
            d["sx"] = sb("sx", [128, 1], F32)
            d["G_tok"] = sb("G_tok", [128, NC], F32)
            d["carry"] = sb("carry", [1, NC + 1], F32)
            d["growt"] = sb("growt", [1, 128], F32)
            for nm, shp in (("Uf", [128, 128]), ("S0f", [128, 128]), ("S0s", [128, 128])):
                d[nm] = sb(nm, shp, F32)
            d["ogt2"] = sb("ogt2", [128, 2, 128], BF16)
            for nm, shp in (("grel", [1, 128]), ("gpn", [1, 1]), ("gcoln", [128, 1]),
                            ("gamr", [1, 128]), ("gamc", [128, 1]), ("decr", [1, 128]),
                            ("decc", [128, 1]), ("gCc", [128, 1])):
                d[nm] = sb(nm, shp, F32)
            d["px"] = pxs[j]
            d["otokR"] = sb("otokR", [128, 2, 128], F32)
            d["tb1"] = sb("tb1", [128, 2, 256], BF16)
            d["qf32"] = sb("qf32", [128, 128], F32)
            J.append(d)

        def emit_sectionA():
            for j, d in enumerate(J):
                for m in range(4):
                    for dc in range(8):
                        p.dma(d["sg"][:], Wh[j, m, dc * 128:(dc + 1) * 128, :], q=("sync", "scalar")[dc % 2])
                        p.v("tensor_scalar", d["sg"][:], d["sg"][:], lnwt[:, dc:dc + 1], None, Op.mult)
                        p.v("tensor_copy", d["wcb"][:, dc, m, :], d["sg"][:])
                for dc in range(8):
                    p.dma(d["sg"][:, 0:2], bac[j, dc * 128:(dc + 1) * 128, :])
                    p.v("tensor_scalar", d["sg"][:, 0:2], d["sg"][:, 0:2], lnwt[:, dc:dc + 1], None, Op.mult)
                    p.v("tensor_copy", d["bab"][:, dc, :], d["sg"][:, 0:2])
                p.dma(d["adts"][:], adt[j].unsqueeze(0))
                p.a("activation", d["asc"][:, 0:1], d["adts"][:, 0:1], AF.Exp)
                p.v("tensor_scalar", d["asc"][:, 1:2], d["asc"][:, 0:1], -1.0, None, Op.mult)
                p.t("matmul", pmm[j][0:128, 0:1], ones1[:], d["asc"][:, 1:2], start=True, stop=True)
                p.v("tensor_copy", d["nAcol"][:], pmm[j][0:128, 0:1])
                p.t("matmul", pmm[j][0:128, 0:1], ones1[:], d["adts"][:, 1:2], start=True, stop=True)
                p.v("tensor_copy", d["dtbcol"][:], pmm[j][0:128, 0:1])

            d0 = J[0]
            def a1_dma(tt):
                p.dma(xrowS[:, tt % 2, :], xb[tt * 128:(tt + 1) * 128, :],
                      q=("sync", "scalar")[tt % 2])
            def a1_trans(tt):
                par = tt % 2
                p.a("activation", sqshared_s8[:], xrowS[:, par, :], AF.Square, accum_out=d0["st8"][:, 0:1])
                p.v("tensor_scalar", d0["st8"][:, 1:2], d0["st8"][:, 0:1], 1.0 / D, 1e-6, Op.mult, Op.add)
                p.a("activation", d0["st8"][:, 2:3], d0["st8"][:, 1:2], AF.Ln)
                p.a("activation", rstdS[:, tt:tt + 1], d0["st8"][:, 2:3], AF.Exp, scale=-0.5)
                for dc in range(8):
                    pb = pA[dc % 2]
                    p.t("matmul", pb[:], xrowS[:, par, dc * 128:(dc + 1) * 128], identb[:],
                        start=True, stop=True)
                    if dc % 4 < 2:
                        p.v("tensor_copy", xnTs[:, par, dc, :], pb[:])
                    else:
                        p.a("activation", xnTs[:, par, dc, :], pb[:], AF.Copy)
            a1_dma(0)
            a1_trans(0)
            for tt in range(NC):
                par = tt % 2
                if tt + 1 < NC:
                    a1_dma(tt + 1)
                for j, d in enumerate(J):
                    for m, dst in ((0, "rawq"), (1, "rawk"), (2, "rawv"), (3, "rawg")):
                        for dc in range(8):
                            p.t("matmul", ppool[j][:, 0:128], xnTs[:, par, dc, :], d["wcb"][:, dc, m, :],
                                start=(dc == 0), stop=(dc == 7))
                        p.v("tensor_scalar", d[dst][:, tt, :], ppool[j][:, 0:128],
                            rstdS[:, tt:tt + 1], None, Op.mult)
                    for dc in range(8):
                        p.t("matmul", ppool[j][:, 128:130], xnTs[:, par, dc, :], d["bab"][:, dc, :],
                            start=(dc == 0), stop=(dc == 7))
                    p.v("tensor_scalar", d["bt_all"][:, tt:tt + 1], ppool[j][:, 128:129],
                        rstdS[:, tt:tt + 1], None, Op.mult)
                    p.v("tensor_scalar", d["arw"][:, tt:tt + 1], ppool[j][:, 129:130],
                        rstdS[:, tt:tt + 1], None, Op.mult)
                if tt + 1 < NC:
                    a1_trans(tt + 1)
            for tt in range(NC):
                for j, d in enumerate(J):
                    for src_ in ("rawq", "rawk", "rawv", "rawg"):
                        p.a("activation", d["sg"][:], d[src_][:, tt, :], AF.Sigmoid)
                        p.v("tensor_tensor", d[src_][:, tt, :], d[src_][:, tt, :], d["sg"][:], Op.mult)
                    p.a("activation", d["bt_all"][:, tt:tt + 1], d["bt_all"][:, tt:tt + 1], AF.Sigmoid)
            for tt in range(NC):
                for j, d in enumerate(J):
                    for nm in ("q_tok", "k_tok"):
                        p.a("activation", d["sg"][:], d[nm][:, tt, :], AF.Square, accum_out=d["st8"][:, 4:5])
                        p.v("tensor_scalar", d["st8"][:, 5:6], d["st8"][:, 4:5], 1e-6, None, Op.add)
                        p.a("activation", d["st8"][:, 6:7], d["st8"][:, 5:6], AF.Ln)
                        p.a("activation", d["st8"][:, 7:8], d["st8"][:, 6:7], AF.Exp, scale=-0.5)
                        p.v("tensor_scalar", d[nm][:, tt, :], d[nm][:, tt, :], d["st8"][:, 7:8], None, Op.mult)
                    p.v("tensor_scalar", d["sx"][:], d["arw"][:, tt:tt + 1], d["dtbcol"][:], None, Op.add)
                    p.a("activation", d["st8"][:, 4:5], d["sx"][:], AF.Abs)
                    p.a("activation", d["st8"][:, 5:6], d["st8"][:, 4:5], AF.Exp, scale=-1.0)
                    p.a("activation", d["st8"][:, 6:7], d["st8"][:, 5:6], AF.Ln, bias=1.0)
                    p.a("activation", d["st8"][:, 7:8], d["sx"][:], AF.Relu)
                    p.v("tensor_tensor", d["sx"][:], d["st8"][:, 6:7], d["st8"][:, 7:8], Op.add)
                    p.v("tensor_scalar", d["sx"][:], d["sx"][:], d["nAcol"][:], None, Op.mult)
                    p.t("transpose", pr1, d["sx"][:], identf[:])
                    if tt == 0:
                        p.op("vector", "memset", d["carry"][:, 0:1], 0.0)
                    p.v("tensor_tensor_scan", d["growt"][:], d["carry"][:, 0:1].broadcast_to([1, 128]) if False else d["growt"][:],
                        pr1, d["carry"][:, tt:tt + 1], Op.add, Op.add) if False else None
                    p.op("vector", "memset", d["sg"][0:1, 0:128].bitcast(F32), 0.0)
                    p.v("tensor_tensor_scan", d["growt"][:], d["sg"][0:1, 0:128].bitcast(F32), pr1,
                        d["carry"][:, tt:tt + 1], Op.add, Op.add)
                    p.v("tensor_copy", d["carry"][:, tt + 1:tt + 2], d["growt"][:, 127:128])
                    p.t("transpose", pmm[1][0:128, 0:1], d["growt"][:], identf[0:1, 0:1])
                    p.v("tensor_copy", d["G_tok"][:, tt:tt + 1], pmm[1][0:128, 0:1])
            for j, d in enumerate(J):
                p.op("vector", "memset", d["S0f"][:], 0.0)

        def emit_sectionB():
            def gen_B1(c, j):
                d = J[j]
                s8r = c % 4
                cp = c % 2
                PM = pmm2[j][cp]
                KT, QTb = d["KTb"][:, cp, :], d["QTb"][:, cp, :]
                QT = d["QT"][:, cp, :]
                KbT, ktf, vtf, kbf = d["KbTb"][:, cp, :], d["ktf"][:, cp, :], d["vtf"][:, cp, :], d["kbfb"][:, cp, :]
                Gdiff, expSU, expIU = d["Gdiff"][:, cp, :], d["expSU"][:, cp, :], d["expIU"][:, cp, :]
                LtN, Lcur, LcurT = d["LtNb"][:, cp, :], d["Lcurb"][:, cp, :], d["LcurTb"][:, cp, :]
                rhs0, Xf = d["rhs0"][:, cp, :], d["Xfb"][:, cp, :]
                tb1 = d["tb1"][:, cp, :]
                grel, gamr, decr = d["grel2"][0:1, cp, :], d["gamr2"][0:1, cp, :], d["decr2"][0:1, cp, :]
                gpn = d["gpn2"][0:1, cp, :]
                gcoln, gamc, decc = d["gcoln2"][:, cp, :], d["gamc2"][:, cp, :], d["decc2"][:, cp, :]
                gamB = d["gamB"][:, cp, :]
                PMc = PM[0:128, 0:1]
                PMr = PM[0:1, 0:128]
                p.v("tensor_copy", ktf, d["k_tok"][:, c, :])
                p.v("tensor_copy", vtf, d["v_tok"][:, c, :])
                p.t("matmul", PM[:], d["k_tok"][:, c, :], identb[:], start=True, stop=True)
                p.v("tensor_copy", KT, PM[:])
                yield
                p.t("matmul", PM[:], d["q_tok"][:, c, :], identb[:], start=True, stop=True)
                p.v("tensor_copy", QT, PM[:])
                p.v("tensor_copy", QTb, PM[:])
                yield
                p.v("tensor_scalar", gpn, d["carry"][:, c:c + 1], -1.0, None, Op.mult)
                p.t("matmul", PMc, ones1[:], gpn, start=True, stop=True)
                p.v("tensor_copy", gamB, PMc)
                p.v("tensor_scalar", gcoln, d["G_tok"][:, c:c + 1], gamB, None, Op.add)
                p.a("activation", gamc, gcoln, AF.Exp)
                yield
                p.t("transpose", PMr, gcoln, identf[:])
                p.v("tensor_copy", grel, PMr)
                p.a("activation", gamr, grel, AF.Exp)
                yield
                p.v("tensor_scalar", gcoln, gcoln, -1.0, None, Op.mult)
                p.t("matmul", PM[:], ones1[:], grel, start=True, stop=True)
                p.v("tensor_scalar", Gdiff, PM[:], gcoln, None, Op.add)
                p.v("tensor_tensor", expSU, Gdiff, msu[:], Op.add)
                p.a("activation", expSU, expSU, AF.Exp)
                yield
                p.v("tensor_tensor", expIU, Gdiff, miu[:], Op.add)
                p.a("activation", expIU, expIU, AF.Exp)
                yield
                p.v("tensor_scalar", kbf, ktf, d["bt_all"][:, c:c + 1], None, Op.mult)
                p.t("matmul", PM[:], kbf, identb[:], start=True, stop=True)
                p.v("tensor_copy", KbT, PM[:])
                yield
                p.t("matmul", PM[:], KT, KbT, start=True, stop=True)
                p.v("tensor_tensor", Lcur, expSU, PM[:], Op.mult)
                p.v("tensor_scalar", LtN, Lcur, -1.0, None, Op.mult)
                yield
                p.t("matmul", PM[:], KT, QTb, start=True, stop=True)
                p.v("tensor_tensor", d["PqtR"][:, s8r, :], expIU, PM[:], Op.mult)
                yield
                p.v("tensor_scalar", rhs0[:, 0:128], ktf, d["bt_all"][:, c:c + 1],
                    gamc, Op.mult, Op.mult)
                p.v("tensor_scalar", rhs0[:, 128:256], vtf, d["bt_all"][:, c:c + 1], None, Op.mult)
                p.v("tensor_copy", Xf, rhs0)
                yield
                p.t("matmul", d["px"][:], LtN, Xf, start=True, stop=True)
                p.a("activation", tb1, d["px"][:], AF.Copy)
                p.v("tensor_tensor", Xf, Xf, tb1, Op.add)
                yield
                p.t("matmul", d["px"][:, 0:128], Lcur, identb[:], start=True, stop=True)
                p.v("tensor_copy", LcurT, d["px"][:, 0:128])
                yield
                for lev in range(6):
                    p.t("matmul", PM[:], LcurT, Lcur, start=True, stop=True)
                    p.t("matmul", d["px"][:, 0:128], Lcur, LcurT, start=True, stop=True)
                    p.v("tensor_copy", Lcur, PM[:])
                    p.v("tensor_copy", LcurT, d["px"][:, 0:128])
                    yield
                    p.t("matmul", d["px"][:], LcurT, Xf, start=True, stop=True)
                    p.a("activation", tb1, d["px"][:], AF.Copy)
                    p.v("tensor_tensor", Xf, Xf, tb1, Op.add)
                    yield
                p.v("tensor_copy", d["ZR"][:, s8r, :], Xf[:, 128:256])
                p.t("matmul", d["px"][:, 0:128], Xf[:, 0:128], identb[:], start=True, stop=True)
                p.a("activation", d["WtnR"][:, s8r, :], d["px"][:, 0:128], AF.Copy, scale=-1.0)
                yield
                p.t("matmul", PM[:], ones1[:], gamr, start=True, stop=True)
                p.v("tensor_tensor", d["QTgR"][:, s8r, :], QT, PM[:], Op.mult)
                yield
                p.v("tensor_scalar", decr, grel, -1.0, grel[0:1, 127:128], Op.mult, Op.add)
                p.a("activation", decr, decr, AF.Exp)
                p.t("transpose", PMc, decr, identf[0:1, 0:1])
                p.v("tensor_copy", decc, PMc)
                yield
                p.t("matmul", PMc, ones1[:], gamr[0:1, 127:128], start=True, stop=True)
                p.v("tensor_copy", d["gCcR"][:, s8r:s8r + 1], PMc)
                p.v("tensor_scalar", d["kdecR"][:, s8r, :], ktf, decc, None, Op.mult)

            def gen_B2(c, j):
                d = J[j]
                s8r = c % 4
                opar = c % 2
                r = slice(c * 128, (c + 1) * 128)
                p.t("matmul", pA[j][:], d["WtnR"][:, s8r, :], d["S0f"][:], start=True, stop=False)
                p.t("matmul", pA[j][:], identf[:], d["ZR"][:, s8r, :], start=False, stop=True)
                p.a("activation", d["Uf"][:], pA[j][:], AF.Copy)
                yield
                p.t("matmul", pA[j][:], d["QTgR"][:, s8r, :], d["S0f"][:], start=True, stop=False)
                p.t("matmul", pA[j][:], d["PqtR"][:, s8r, :], d["Uf"][:], start=False, stop=True)
                p.a("activation", d["otokR"][:, opar, :], pA[j][:], AF.Copy)
                yield
                p.a("activation", d["S0s"][:], d["S0f"][:], AF.Copy, scale=d["gCcR"][:, s8r:s8r + 1])
                yield
                p.t("matmul", pA[j][:], d["kdecR"][:, s8r, :], d["Uf"][:], start=True, stop=False)
                p.t("matmul", pA[j][:], identf[:], d["S0s"][:], start=False, stop=True)
                p.a("activation", d["S0f"][:], pA[j][:], AF.Copy)
                yield
                p.a("activation", d["sg"][:], d["otokR"][:, opar, :], AF.Square, accum_out=d["st8"][:, 0:1])
                p.g("tensor_scalar", d["st8"][:, 1:2], d["st8"][:, 0:1], 1.0 / 128.0, 1e-6, Op.mult, Op.add)
                yield
                p.a("activation", d["st8"][:, 2:3], d["st8"][:, 1:2], AF.Ln)
                p.a("activation", d["st8"][:, 3:4], d["st8"][:, 2:3], AF.Exp, scale=-0.5)
                p.g("tensor_scalar", d["otokR"][:, opar, :], d["otokR"][:, opar, :], d["st8"][:, 3:4], None, Op.mult)
                p.g("tensor_tensor", d["otokR"][:, opar, :], d["otokR"][:, opar, :], onwB[:], Op.mult)
                yield
                p.g("tensor_tensor", d["ogt2"][:, opar, :], d["otokR"][:, opar, :], d["g_tok"][:, c, :], Op.mult)
                p.dma(og_out[j, r, :], d["ogt2"][:, opar, :], q=("sync", "scalar")[j])

            LOOKAHEAD = 3
            for c in range(NC + LOOKAHEAD):
                gens = []
                if c < NC:
                    gens += [gen_B1(c, 0), gen_B1(c, 1)]
                if c >= LOOKAHEAD:
                    gens += [gen_B2(c - LOOKAHEAD, 0), gen_B2(c - LOOKAHEAD, 1)]
                _end = object()
                while gens:
                    gens = [gg for gg in gens if next(gg, _end) is not _end]


        with p.scope():
            for j, d in enumerate(J):
                d["wcb"] = p.sb(g + "wcb" + str(j), [128, 8, 4, 128], BF16)
                d["bab"] = p.sb(g + "bab" + str(j), [128, 8, 2], BF16)
                d["adts"] = p.sb(g + "adts" + str(j), [1, 2], F32)
                d["asc"] = p.sb(g + "asc" + str(j), [1, 4], F32)
                d["arw"] = p.sb(g + "arw" + str(j), [128, NC], F32)
            xnTs = p.sb(g + "xnTs", [128, 2, 8, 128], BF16)
            xrowS = p.sb(g + "xrowS", [128, 2, D], BF16)
            rstdS = p.sb(g + "rstdS", [128, NC], F32)
            global sqshared_s8
            sqshared_s8 = p.sb(g + "sqshared", [128, D], BF16)
            emit_sectionA()
        for j, d in enumerate(J):
            for nm, shp in (("QT", [128, 2, 128]),
                            ("ktf", [128, 2, 128]), ("vtf", [128, 2, 128]),
                            ("Gdiff", [128, 2, 128]), ("expSU", [128, 2, 128]), ("expIU", [128, 2, 128]),
                            ("rhs0", [128, 2, 256]), ("gamB", [128, 2, 1])):
                d[nm] = p.sb(g + nm + str(j), shp, F32)
            for nm, shp in (("KTb", [128, 2, 128]), ("QTb", [128, 2, 128]), ("KbTb", [128, 2, 128]),
                            ("kbfb", [128, 2, 128]), ("LtNb", [128, 2, 128]), ("Lcurb", [128, 2, 128]),
                            ("LcurTb", [128, 2, 128]), ("Xfb", [128, 2, 256])):
                d[nm] = p.sb(g + nm + str(j), shp, BF16)
            for nm in ("grel", "gamr", "decr"):
                d[nm + "2"] = p.sb(g + nm + "2" + str(j), [1, 2, 128], F32)
            d["gpn2"] = p.sb(g + "gpn2" + str(j), [1, 2, 1], F32)
            for nm in ("gcoln2", "gamc2", "decc2"):
                d[nm] = p.sb(g + nm + str(j), [128, 2, 1], F32)
            d["WtnR"] = p.sb(g + "WtnR" + str(j), [128, 4, 128], F32)
            d["ZR"] = p.sb(g + "ZR" + str(j), [128, 4, 128], F32)
            d["PqtR"] = p.sb(g + "PqtR" + str(j), [128, 4, 128], F32)
            d["QTgR"] = p.sb(g + "QTgR" + str(j), [128, 4, 128], F32)
            d["kdecR"] = p.sb(g + "kdecR" + str(j), [128, 4, 128], F32)
            d["gCcR"] = p.sb(g + "gCcR" + str(j), [128, 4], F32)
        emit_sectionB()

def build_s8():
    p = Prog()
    xb = p.din("xb", [S, D], BF16)
    Wh = p.din("Wh", [2, 4, D, 128], F32)
    bac = p.din("bac", [2, D, 2], F32)
    adt = p.din("adt", [2, 2], F32)
    lnw_in = p.din("lnw", [D], F32)
    onw_in = p.din("onw", [1, 128], F32)
    msu_in = p.din("msu", [128, 128], F32)
    miu_in = p.din("miu", [128, 128], F32)
    idf_in = p.din("identf", [128, 128], F32)
    idb_in = p.din("identb", [128, 128], BF16)
    og_out = p.dout("og", [2, S, 128], BF16)
    emit_s8(p, "s8_", xb, Wh, bac, adt, lnw_in, onw_in, msu_in, miu_in, idf_in, idb_in, og_out)
    p.finish()
    return p


def stage8(xn2b_full, Wq1, Wk1, Wv1, Wg1, bw, aw, A_log, dt_bias, lnw, onw):
    x3 = xn2b_full.reshape(B, S, D)
    msu = np.where(np.arange(128)[:, None] < np.arange(128)[None, :], 0.0, NEG).astype(np.float32)
    miu = np.where(np.arange(128)[:, None] <= np.arange(128)[None, :], 0.0, NEG).astype(np.float32)
    identf = np.eye(128, dtype=np.float32)
    idb = np.eye(128).astype(BF)
    p = build_s8()
    im = []
    for c in range(NCORES):
        bhs = [2 * c, 2 * c + 1]
        b = bhs[0] // 8
        Whl, bacl, adtl = [], [], []
        for bh in bhs:
            h = bh % 8
            cs = slice(h * 128, (h + 1) * 128)
            Whl.append(np.stack([Wq1[:, cs], Wk1[:, cs], Wv1[:, cs], Wg1[:, cs]]))
            bacl.append(np.stack([bw[:, h], aw[:, h]], 1))
            adtl.append(np.array([A_log[h], dt_bias[h]], np.float32))
        im.append({"xb": np.ascontiguousarray(x3[b]), "Wh": np.ascontiguousarray(np.stack(Whl)),
                   "bac": np.ascontiguousarray(np.stack(bacl)), "adt": np.stack(adtl),
                   "lnw": lnw, "onw": onw.reshape(1, 128), "msu": msu, "miu": miu,
                   "identf": identf, "identb": idb})
    res = run(p, im)
    og = np.zeros((B, S, 8, 128), np.float32)
    for c in range(NCORES):
        for jj in range(2):
            bh = 2 * c + jj
            og[bh // 8, :, bh % 8, :] = res[c]["og"][jj].astype(np.float32)
    return og.reshape(B, S, D)





def emit_s9(p, g, og8, xn2r, ow_in, ln2_in, fcw_in, pw_in, idb_in, idf_in, s1_out, x1b_out, rows):
    with p.scope():
        identb = p.sb(g + "identb_s", [128, 128], BF16)
        ones1 = p.sb(g + "ones1", [1, 128], F32)
        wld = p.sb(g + "wld", [128, 1024], F32)
        lnwt = p.sb(g + "lnwt", [128, 8], F32)
        owb = p.sb(g + "owb", [128, 8, D], BF16)
        fcb = p.sb(g + "fcb", [128, 8, 4 * D], BF16)
        pwb = p.sb(g + "pwb", [128, 32, D], BF16)
        ogT = p.sb(g + "ogT", [128, 8, 128], BF16)
        xt = p.sb(g + "xt", [128, D], F32)
        x1b = p.sb(g + "x1bt", [128, D], F32)
        x1bb = p.sb(g + "x1bb", [128, D], BF16)
        x1bs = p.sb(g + "x1bs", [128, 8, 128], BF16)
        sqscr = p.sb(g + "sqscr", [128, D], F32)
        st9 = p.sb(g + "st9", [128, 8], F32)
        rrow = p.sb(g + "rrow", [1, 128], F32)
        rstB = p.sb(g + "rstB", [128, 128], F32)
        rstBb = p.sb(g + "rstBb", [128, 128], BF16)
        ub = p.sb(g + "ub", [128, 32, 128], BF16)
        yt = p.sb(g + "yt", [128, D], F32)
        pj = p.ps(g + "pj", [128, 512])
        ptb = p.ps(g + "ptb", [128, 128], BF16)
        pc1 = p.ps(g + "pc1", [1, 128])
        pB = p.ps(g + "pB", [128, 128])

        p.dma(identb[:], idb_in)
        p.op("vector", "memset", ones1[:], 1.0)
        for c8 in range(8):
            p.dma(lnwt[:, c8:c8 + 1], ln2_in[c8 * 128:(c8 + 1) * 128].unsqueeze(1))
        for dc in range(8):
            p.dma(wld[:], ow_in[dc * 128:(dc + 1) * 128, :])
            p.v("tensor_copy", owb[:, dc, :], wld[:])
        for dc in range(8):
            p.dma(wld[:], fcw_in[dc * 128:(dc + 1) * 128, 0:1024])
            p.v("tensor_scalar", wld[:], wld[:], lnwt[:, dc:dc + 1], None, Op.mult)
            p.v("tensor_copy", fcb[:, dc, 0:1024], wld[:])
            for hh in range(1, 4):
                p.dma(wld[:], fcw_in[dc * 128:(dc + 1) * 128, hh * 1024:(hh + 1) * 1024])
                p.v("tensor_scalar", wld[:], wld[:], lnwt[:, dc:dc + 1], None, Op.mult)
                p.v("tensor_copy", fcb[:, dc, hh * 1024:(hh + 1) * 1024], wld[:])
        for ic in range(32):
            p.dma(wld[:], pw_in[ic * 128:(ic + 1) * 128, :])
            p.v("tensor_copy", pwb[:, ic, :], wld[:])
        identf = p.sb(g + "identf_s", [128, 128], F32)
        p.dma(identf[:], idf_in)
        nt = rows // 128
        for tt in range(nt):
            r = slice(tt * 128, (tt + 1) * 128)
            for dc in range(8):
                p.dma(ogT[:, dc, :], og8[dc, r, :], q="scalar", transpose=True)
            p.dma(xt[:], xn2r[r, :])
            for dh in range(2):
                for dc in range(8):
                    p.t("matmul", pj[:], ogT[:, dc, :], owb[:, dc, dh * 512:(dh + 1) * 512],
                        start=(dc == 0), stop=(dc == 7))
                p.v("scalar_tensor_tensor", x1b[:, dh * 512:(dh + 1) * 512], pj[:], 0.5,
                    xt[:, dh * 512:(dh + 1) * 512], Op.mult, Op.add)
            p.dma(x1b_out[r, :], x1b[:])
            p.a("activation", sqscr[:], x1b[:], AF.Square, accum_out=st9[:, 0:1])
            p.v("tensor_scalar", st9[:, 1:2], st9[:, 0:1], 1.0 / D, 1e-6, Op.mult, Op.add)
            p.a("activation", st9[:, 2:3], st9[:, 1:2], AF.Sqrt)
            p.v("reciprocal", st9[:, 3:4], st9[:, 2:3])
            p.v("tensor_copy", x1bb[:], x1b[:])
            p.t("transpose", pc1[:], st9[:, 3:4], identf[:])
            p.v("tensor_copy", rrow[:], pc1[:])
            p.t("matmul", pB[:], ones1[:], rrow[:], start=True, stop=True)
            p.v("tensor_copy", rstBb[:], pB[:])
            for dc in range(8):
                p.t("transpose", ptb[:], x1bb[:, dc * 128:(dc + 1) * 128], identb[:])
                p.v("tensor_tensor", x1bs[:, dc, :], ptb[:], rstBb[:], Op.mult)
            for ic in range(32):
                for dc in range(8):
                    p.t("matmul", pB[:], fcb[:, dc, ic * 128:(ic + 1) * 128], x1bs[:, dc, :],
                        start=(dc == 0), stop=(dc == 7))
                p.a("activation", ub[:, ic, :], pB[:], AF.Silu)
            for dh in range(2):
                for ic in range(32):
                    p.t("matmul", pj[:], ub[:, ic, :], pwb[:, ic, dh * 512:(dh + 1) * 512],
                        start=(ic == 0), stop=(ic == 31))
                p.v("scalar_tensor_tensor", yt[:, dh * 512:(dh + 1) * 512], pj[:], 0.5,
                    x1b[:, dh * 512:(dh + 1) * 512], Op.mult, Op.add)
            p.dma(s1_out[r, :], yt[:])


def build_s9(rows):
    p = Prog()
    ogr = p.din("ogr", [8, rows, 128], BF16)
    xn2r = p.din("xn2", [rows, D], F32)
    ow_in = p.din("ow", [D, D], F32)
    ln2_in = p.din("ln2w", [D], F32)
    fcw_in = p.din("fcw", [D, 4 * D], F32)
    pw_in = p.din("pw", [4 * D, D], F32)
    idb_in = p.din("identb", [128, 128], BF16)
    idf_in = p.din("identf", [128, 128], F32)
    s1_out = p.dout("s1", [rows, D], F32)
    x1b_out = p.dout("x1b", [rows, D], F32)
    emit_s9(p, "s9_", ogr, xn2r, ow_in, ln2_in, fcw_in, pw_in, idb_in, idf_in, s1_out, x1b_out, rows)
    p.finish()
    return p


def stage9(og_full, xn2, ow, ln2w, fcw, pw):
    p = build_s9(TPC)
    idb = np.eye(128).astype(BF)
    idf = np.eye(128, dtype=np.float32)
    og4 = og_full.reshape(B, S, 8, 128).astype(BF)
    osh = [np.ascontiguousarray(og4[c // 4, (c % 4) * TPC:(c % 4 + 1) * TPC].transpose(1, 0, 2))
           for c in range(NCORES)]
    xsh = shard_rows(xn2.reshape(T, D))
    im = [{"ogr": osh[c], "xn2": xsh[c], "ow": ow, "ln2w": ln2w, "fcw": fcw, "pw": pw,
           "identb": idb, "identf": idf} for c in range(NCORES)]
    res = run(p, im)
    s1 = np.concatenate([r["s1"] for r in res], 0)
    x1b = np.concatenate([r["x1b"] for r in res], 0)
    return s1, x1b





def emit_s10(p, g, xnbr, xnlor, gw_in, w1_in, w2_in, w3_in, ws1_in, ws2_in, ws3_in, x1r, s1r, iot16_in, out, rows):
    with p.scope():
        gwraw = p.sb(g + "gwraw", [16, D], F32)
        gwbf = p.sb(g + "gwbf", [16, D], BF16)
        gwlos = p.sb(g + "gwlos", [16, D], F32)
        gwlo = p.sb(g + "gwlo", [16, D], BF16)
        gwT = p.sb(g + "gwT", [128, 8, 16], BF16)
        gwloT = p.sb(g + "gwloT", [128, 8, 16], BF16)
        xnTlo = p.sb(g + "xnTlo", [128, 8, 128], BF16)
        wld = p.sb(g + "wld", [128, 1024], F32)
        wld2 = p.sb(g + "wld2", [128, 1024], F32)
        wbuf1 = p.sb(g + "wbuf1", [128, 2, 8, D], BF16)
        wbuf3 = p.sb(g + "wbuf3", [128, 2, 8, D], BF16)
        wbuf2 = p.sb(g + "wbuf2", [128, 2, 8, D], BF16)
        abuf = p.sb(g + "abuf", [128, 1, 8, 512], BF16)
        w1b_ = [wbuf1[:, i].rearrange("p a (x d) -> p (a x) d", x=2) for i in range(2)]
        w3b_ = [wbuf3[:, i].rearrange("p a (x d) -> p (a x) d", x=2) for i in range(2)]
        w2b_ = [wbuf2[:, i] for i in range(2)]
        ab_ = [abuf[:, 0], abuf[:, 0]]
        gsl = p.sb(g + "gsl", [128, 512], F32)
        yacc = p.sb(g + "yacc", [128, 16, D], BF16)
        c_all = p.sb(g + "c_all", [128, 16, 16], F32)
        lg = p.sb(g + "lg", [128, 16], F32)
        sc = p.sb(g + "sc", [128, 16], F32)
        gsc = p.sb(g + "gsc", [128, 4], F32)
        gm = p.sb(g + "gm", [128, 16], F32)
        msk = p.sb(g + "msk", [128, 16], F32)
        st = p.sb(g + "st", [128, 8], F32)
        iot16 = p.sb(g + "iot16_s", [128, 16], F32)
        xt = p.sb(g + "xt", [128, D], F32)
        s1t = p.sb(g + "s1t", [128, D], F32)
        g1t = p.sb(g + "g1t", [128, D], F32)
        pl = p.ps(g + "pl", [128, 16])
        ph_ = [p.ps(g + f"ph{i}", [128, 512]) for i in range(2)]
        ph3_ = [p.ps(g + f"ph3{i}", [128, 512]) for i in range(2)]
        py_ = [p.ps(g + f"py{i}", [128, 512]) for i in range(2)]

        p.dma(iot16[:], iot16_in)
        p.dma(gwraw[:], gw_in)
        p.v("tensor_copy", gwbf[:], gwraw[:])
        p.v("tensor_tensor", gwlos[:], gwraw[:], gwbf[:], Op.subtract)
        p.v("tensor_copy", gwlo[:], gwlos[:])
        for dc in range(8):
            p.dma(gwT[:, dc, :], gwbf[:, dc * 128:(dc + 1) * 128], q="scalar", transpose=True)
            p.dma(gwloT[:, dc, :], gwlo[:, dc * 128:(dc + 1) * 128], q="scalar", transpose=True)
        nt = rows // 128
        NCH = rows // 512
        xnTall = p.sb(g + "xnTall", [128, 8, rows], BF16)
        for ch in range(NCH):
            for dc in range(8):
                p.dma(xnTall[:, dc, ch * 512:(ch + 1) * 512],
                      xnbr[ch * 512:(ch + 1) * 512, dc * 128:(dc + 1) * 128],
                      q=("scalar", "sync")[(ch + dc) % 2], transpose=True)
        for tt in range(nt):
            r = slice(tt * 128, (tt + 1) * 128)
            ts128 = slice(tt * 128, (tt + 1) * 128)
            for dc in range(8):
                p.dma(xnTlo[:, dc, :], xnlor[r, dc * 128:(dc + 1) * 128], q="scalar", transpose=True)
            for dc in range(8):
                p.t("matmul", pl[:], xnTall[:, dc, ts128], gwT[:, dc, :], start=(dc == 0), stop=False)
                p.t("matmul", pl[:], xnTall[:, dc, ts128], gwloT[:, dc, :], start=False, stop=False)
                p.t("matmul", pl[:], xnTlo[:, dc, :], gwT[:, dc, :], start=False, stop=(dc == 7))
            p.v("tensor_copy", lg[:], pl[:])
            p.v("tensor_reduce", st[:, 0:1], lg[:], AX, Op.max)
            p.v("tensor_scalar", st[:, 1:2], st[:, 0:1], -1.0, None, Op.mult)
            p.v("tensor_scalar", sc[:], lg[:], st[:, 1:2], None, Op.add)
            p.a("activation", sc[:], sc[:], AF.Exp)
            p.v("tensor_reduce", st[:, 2:3], sc[:], AX, Op.add)
            p.v("reciprocal", st[:, 3:4], st[:, 2:3])
            p.v("tensor_scalar", sc[:], sc[:], st[:, 3:4], None, Op.mult)
            for g4 in range(4):
                p.v("tensor_reduce", gsc[:, g4:g4 + 1], sc[:, g4 * 4:(g4 + 1) * 4], AX, Op.max)
            p.v("tensor_reduce", st[:, 4:5], gsc[:], AX, Op.max)
            p.v("tensor_scalar", gm[:, 0:4], gsc[:], st[:, 4:5], None, Op.is_equal)
            p.v("scalar_tensor_tensor", gsc[:], gm[:, 0:4], -1e30, gsc[:], Op.mult, Op.add)
            p.v("tensor_reduce", st[:, 5:6], gsc[:], AX, Op.max)
            for g4 in range(4):
                p.v("tensor_reduce", gsc[:, g4:g4 + 1], sc[:, g4 * 4:(g4 + 1) * 4], AX, Op.max)
            p.v("tensor_scalar", gm[:, 0:4], gsc[:], st[:, 5:6], None, Op.is_ge)
            p.v("tensor_copy", msk[:].rearrange("p (g k) -> p g k", k=4),
                gm[:, 0:4].unsqueeze(2).broadcast_to([128, 4, 4]))
            p.v("tensor_tensor", msk[:], msk[:], sc[:], Op.mult)
            p.v("tensor_copy", lg[:], msk[:])
            for k in range(3):
                p.v("tensor_reduce", st[:, 6:7], lg[:], AX, Op.max)
                p.v("tensor_scalar", gm[:], lg[:], st[:, 6:7], None, Op.is_equal)
                p.v("scalar_tensor_tensor", lg[:], gm[:], -1e30, lg[:], Op.mult, Op.add)
            p.v("tensor_reduce", st[:, 6:7], lg[:], AX, Op.max)
            p.v("tensor_scalar", gm[:], msk[:], st[:, 6:7], None, Op.is_ge)
            p.v("tensor_tensor", c_all[:, tt, :], gm[:], sc[:], Op.mult)
            p.op("vector", "memset", yacc[:, tt, :], 0.0)
        for e in range(16):
            eb = e % 2
            w1b, w3b, w2b, ab = w1b_[eb], w3b_[eb], w2b_[eb], ab_[eb]
            ph, ph3, py = ph_[eb], ph3_[eb], py_[eb]
            for dc in range(8):
                p.dma(wld[:, 0:512], w1_in[e, dc * 128:(dc + 1) * 128, :], q=("sync", "gpsimd")[dc % 2])
                p.v("tensor_copy", w1b[:, 2 * dc, :], wld[:, 0:512])
                p.dma(wld2[:, 0:512], w3_in[e, dc * 128:(dc + 1) * 128, :], q=("scalar", "gpsimd")[dc % 2])
                p.v("tensor_copy", w3b[:, 2 * dc, :], wld2[:, 0:512])
            for ic in range(4):
                p.dma(wld[:], w2_in[e, ic * 128:(ic + 1) * 128, :])
                p.v("tensor_copy", w2b[:, ic, :], wld[:])
            for ch in range(NCH):
                tsl = slice(ch * 512, (ch + 1) * 512)
                for ic in range(4):
                    for dc in range(8):
                        p.t("matmul", ph[:], w1b[:, 2 * dc, ic * 128:(ic + 1) * 128], xnTall[:, dc, tsl],
                            start=(dc == 0), stop=(dc == 7))
                    for dc in range(8):
                        p.t("matmul", ph3[:], w3b[:, 2 * dc, ic * 128:(ic + 1) * 128], xnTall[:, dc, tsl],
                            start=(dc == 0), stop=(dc == 7))
                    p.a("activation", gsl[:], ph[:], AF.Silu)
                    p.v("tensor_tensor", ab[:, ic, :], gsl[:], ph3[:], Op.mult)
                for sub in range(4):
                    tt = ch * 4 + sub
                    for dh in range(2):
                        for ic in range(4):
                            p.t("matmul", py[:], ab[:, ic, sub * 128:(sub + 1) * 128],
                                w2b[:, ic, dh * 512:(dh + 1) * 512], start=(ic == 0), stop=(ic == 3))
                        p.v("scalar_tensor_tensor", yacc[:, tt, dh * 512:(dh + 1) * 512], py[:],
                            c_all[:, tt, e:e + 1], yacc[:, tt, dh * 512:(dh + 1) * 512], Op.mult, Op.add)
        ws1b = wbuf1[:, 0]
        ws3b = wbuf3[:, 0]
        ws2b = wbuf2[:, 0]
        abs_ = abuf[:, 0]
        for dc in range(8):
            p.dma(wld[:], ws1_in[dc * 128:(dc + 1) * 128, :])
            p.v("tensor_copy", ws1b[:, dc, :], wld[:])
            p.dma(wld[:], ws3_in[dc * 128:(dc + 1) * 128, :])
            p.v("tensor_copy", ws3b[:, dc, :], wld[:])
            p.dma(wld[:], ws2_in[dc * 128:(dc + 1) * 128, :])
            p.v("tensor_copy", ws2b[:, dc, :], wld[:])
        for ch in range(NCH):
            tsl = slice(ch * 512, (ch + 1) * 512)
            for ic in range(8):
                ph, ph3 = ph_[ic % 2], ph3_[ic % 2]
                for dc in range(8):
                    p.t("matmul", ph[:], ws1b[:, dc, ic * 128:(ic + 1) * 128], xnTall[:, dc, tsl],
                        start=(dc == 0), stop=(dc == 7))
                for dc in range(8):
                    p.t("matmul", ph3[:], ws3b[:, dc, ic * 128:(ic + 1) * 128], xnTall[:, dc, tsl],
                        start=(dc == 0), stop=(dc == 7))
                p.a("activation", gsl[:], ph[:], AF.Silu)
                p.v("tensor_tensor", abs_[:, ic, :], gsl[:], ph3[:], Op.mult)
            for sub in range(4):
                tt = ch * 4 + sub
                for dh in range(2):
                    py = py_[dh]
                    for ic in range(8):
                        p.t("matmul", py[:], abs_[:, ic, sub * 128:(sub + 1) * 128],
                            ws2b[:, ic, dh * 512:(dh + 1) * 512], start=(ic == 0), stop=(ic == 7))
                    p.v("scalar_tensor_tensor", yacc[:, tt, dh * 512:(dh + 1) * 512], py[:], 1.0,
                        yacc[:, tt, dh * 512:(dh + 1) * 512], Op.mult, Op.add)
                r = slice(tt * 128, (tt + 1) * 128)
                p.a("activation", g1t[:], yacc[:, tt, :], AF.Silu)
                p.dma(xt[:], x1r[r, :])
                p.dma(s1t[:], s1r[r, :])
                p.v("tensor_tensor", g1t[:], g1t[:], s1t[:], Op.mult)
                p.v("tensor_tensor", g1t[:], g1t[:], xt[:], Op.add)
                p.dma(out[r, :], g1t[:])


def build_s10(rows):
    p = Prog()
    xnbr = p.din("xnbr", [rows, D], BF16)
    xnlor = p.din("xnlor", [rows, D], BF16)
    gw_in = p.din("gw1", [16, D], F32)
    w1_in = p.din("w1", [16, D, 512], F32)
    w2_in = p.din("w2", [16, 512, D], F32)
    w3_in = p.din("w3", [16, D, 512], F32)
    ws1_in = p.din("ws1", [D, D], F32)
    ws2_in = p.din("ws2", [D, D], F32)
    ws3_in = p.din("ws3", [D, D], F32)
    x1r = p.din("x1", [rows, D], F32)
    s1r = p.din("s1", [rows, D], F32)
    iot16_in = p.din("iot16", [128, 16], F32)
    out = p.dout("out", [rows, D], F32)
    emit_s10(p, "s10_", xnbr, xnlor, gw_in, w1_in, w2_in, w3_in, ws1_in, ws2_in, ws3_in, x1r, s1r, iot16_in, out, rows)
    p.finish()
    return p


def stage10(xn2b, xn2lo, gw1, w1, w2, w3, ws1, ws2, ws3, x1, s1):
    p = build_s10(TPC)
    iot16 = np.broadcast_to(np.arange(16, dtype=np.float32), (128, 16)).copy()
    bsh = shard_rows(xn2b.reshape(T, D)); lsh = shard_rows(xn2lo.reshape(T, D))
    xsh = shard_rows(x1.reshape(T, D)); ssh = shard_rows(s1.reshape(T, D))
    im = [{"xnbr": bsh[c], "xnlor": lsh[c], "gw1": gw1, "w1": w1, "w2": w2, "w3": w3,
           "ws1": ws1, "ws2": ws2, "ws3": ws3, "x1": xsh[c], "s1": ssh[c], "iot16": iot16}
          for c in range(NCORES)]
    res = run(p, im)
    return np.concatenate([r["out"] for r in res], 0)





def emit_s4_sharded(p, g, xnbr, xnlor, gw, iota_tok_in, idf_in, maskvec_in, cofff_in,
                    countsl, counts_all, table_loc, table_red, table_u32,
                    gbidx_out, wts_out, cc_counts_hook, cc_table_hook, rows):
    with p.scope():
        NTT = rows // 128
        gwraw = p.sb(g + "gwraw", [64, D], F32)
        gwbf = p.sb(g + "gwbf", [64, D], BF16)
        gwlosc = p.sb(g + "gwlosc", [64, D], F32)
        gwlo = p.sb(g + "gwlo", [64, D], BF16)
        gwT = p.sb(g + "gwT", [128, 8, 64], BF16)
        gwloT = p.sb(g + "gwloT", [128, 8, 64], BF16)
        xnT = p.sb(g + "xnT", [128, 8, 128], BF16)
        xnloT = p.sb(g + "xnloT", [128, 8, 128], BF16)
        lwork = p.sb(g + "lwork", [128, 64], F32)
        iot = p.sb(g + "iot", [128, 64], F32)
        oh = p.sb(g + "oh", [128, 64], F32)
        mx = p.sb(g + "mx", [128, 4], F32)
        idx_all = p.sb(g + "idx_all", [128, NTT, 4], F32)
        ew = p.sb(g + "ew", [128, 8], F32)
        wtt = p.sb(g + "wtt", [128, 4], F32)
        selT_all = p.sb(g + "selT_all", [64, rows], F32)
        zerT = p.sb(g + "zerT", [64, rows], F32)
        posT = p.sb(g + "posT", [64, rows], F32)
        sel = p.sb(g + "sel", [128, 64], F32)
        pos_tok = p.sb(g + "pos_tok", [128, 64], F32)
        poscol = p.sb(g + "poscol", [128, 4], F32)
        base = p.sb(g + "base", [128, 1], F32)
        km = p.sb(g + "km", [128, 1], F32)
        ge = p.sb(g + "ge", [128, 1], F32)
        t1 = p.sb(g + "t1", [128, 1], F32)
        gidxf = p.sb(g + "gidxf", [128, 1], F32)
        gidxu = p.sb(g + "gidxu", [128, 1], U32)
        gbu = p.sb(g + "gbu", [128, 4], U32)
        tidi = p.sb(g + "tidi", [128, 1], I32)
        tidf = p.sb(g + "tidf", [128, 1], F32)
        tids = p.sb(g + "tids", [128, 1], F32)
        identf = p.sb(g + "identf_s", [128, 128], F32)
        zl = p.sb(g + "zl", [128, 128], F32)
        cnt_sb = p.sb(g + "cnt_sb", [8, 64], F32)
        mvec = p.sb(g + "mvec", [8, 1], F32)
        brow = p.sb(g + "brow", [1, 64], F32)
        bcol = p.sb(g + "bcol", [64, 1], F32)
        coffc = p.sb(g + "coffc", [1, 1], F32)
        coffcol = p.sb(g + "coffcol", [128, 1], F32)
        ones1 = p.sb(g + "ones1", [1, 128], F32)
        tbl_cv = p.sb(g + "tbl_cv", [128, 128], F32)
        tbl_cu = p.sb(g + "tbl_cu", [128, 128], U32)
        pl = p.ps(g + "pl", [128, 64])
        pt2 = p.ps(g + "pt2", [64, 128])
        ptp = p.ps(g + "ptp", [128, 64])
        pb1 = p.ps(g + "pb1", [1, 64])
        pcc = p.ps(g + "pcc", [128, 1])

        p.dma(identf[:], idf_in)
        p.dma(iot[:], iota_tok_in)
        p.dma(mvec[:], maskvec_in)
        p.dma(coffc[:], cofff_in)
        p.op("vector", "memset", zerT[:], 0.0)
        p.op("vector", "memset", zl[:], 0.0)
        p.op("vector", "memset", ones1[:], 1.0)
        p.dma(table_loc.rearrange("(q a) x -> q (a x)", q=128), zl[:].bitcast(F32))
        p.dma(gwraw[:], gw)
        p.v("tensor_copy", gwbf[:], gwraw[:])
        p.v("tensor_tensor", gwlosc[:], gwraw[:], gwbf[:], Op.subtract)
        p.v("tensor_copy", gwlo[:], gwlosc[:])
        for dc in range(8):
            p.dma(gwT[:, dc, :], gwbf[:, dc * 128:(dc + 1) * 128], q="scalar", transpose=True)
            p.dma(gwloT[:, dc, :], gwlo[:, dc * 128:(dc + 1) * 128], q="scalar", transpose=True)
        breg = p.nc.gpsimd.alloc_register()
        p.raw("gpsimd", lambda e: e.reg_mov(breg, T - 1))
        p.g("iota", tidi[:], pattern=[[0, 1]], base=0, channel_multiplier=1)
        p.v("tensor_copy", tidf[:], tidi[:])
        p.t("matmul", pcc[:], ones1[:], coffc[:], start=True, stop=True)
        p.v("tensor_copy", coffcol[:], pcc[:])
        p.v("tensor_tensor", tidf[:], tidf[:], coffcol[:], Op.add)
        for tt in range(NTT):
            r = slice(tt * 128, (tt + 1) * 128)
            for dc in range(8):
                p.dma(xnT[:, dc, :], xnbr[r, dc * 128:(dc + 1) * 128], q="scalar", transpose=True)
                p.dma(xnloT[:, dc, :], xnlor[r, dc * 128:(dc + 1) * 128], q="scalar", transpose=True)
            for dc in range(8):
                p.t("matmul", pl[:], xnT[:, dc, :], gwT[:, dc, :], start=(dc == 0), stop=False)
                p.t("matmul", pl[:], xnT[:, dc, :], gwloT[:, dc, :], start=False, stop=False)
                p.t("matmul", pl[:], xnloT[:, dc, :], gwT[:, dc, :], start=False, stop=(dc == 7))
            p.v("tensor_copy", lwork[:], pl[:])
            p.op("vector", "memset", sel[:], 0.0)
            for k in range(4):
                p.v("tensor_reduce", mx[:, k:k + 1], lwork[:], AX, Op.max)
                p.v("tensor_scalar", oh[:], lwork[:], mx[:, k:k + 1], None, Op.is_equal)
                p.v("scalar_tensor_tensor", lwork[:], oh[:], -1e30, lwork[:], Op.mult, Op.add)
                p.v("tensor_tensor", sel[:], sel[:], oh[:], Op.add)
                p.v("tensor_tensor", oh[:], oh[:], iot[:], Op.mult)
                p.v("tensor_reduce", idx_all[:, tt, k:k + 1], oh[:], AX, Op.add)
            p.v("tensor_scalar", ew[:, 6:7], mx[:, 0:1], -1.0, None, Op.mult)
            p.v("tensor_scalar", ew[:, 0:4], mx[:], ew[:, 6:7], None, Op.add)
            p.a("activation", ew[:, 0:4], ew[:, 0:4], AF.Exp)
            p.v("tensor_reduce", ew[:, 4:5], ew[:, 0:4], AX, Op.add)
            p.v("reciprocal", ew[:, 5:6], ew[:, 4:5])
            p.v("tensor_scalar", wtt[:], ew[:, 0:4], ew[:, 5:6], None, Op.mult)
            p.dma(wts_out[r, :], wtt[:])
            p.t("transpose", pt2[:], sel[:, 0:64], identf[:])
            p.v("tensor_copy", selT_all[:, r], pt2[:])
        p.v("tensor_tensor_scan", posT[:], zerT[:], selT_all[:], 0.0, Op.add, Op.add)
        p.dma(countsl, posT[:, rows - 1:rows])
        cc_counts_hook(p)
        p.dma(cnt_sb[:], counts_all.rearrange("(a b) x -> a (b x)", a=8))
        p.t("matmul", pb1[:], mvec[:], cnt_sb[:], start=True, stop=True)
        p.v("tensor_copy", brow[:], pb1[:])
        p.t("transpose", pcc[0:64, :], brow[:], identf[0:1, 0:1])
        p.v("tensor_copy", bcol[:], pcc[0:64, :])
        p.v("tensor_scalar", posT[:], posT[:], bcol[:], None, Op.add)
        for tt in range(NTT):
            r = slice(tt * 128, (tt + 1) * 128)
            p.t("transpose", ptp[:], posT[:, r], identf[0:64, 0:64])
            p.v("tensor_copy", pos_tok[:, 0:64], ptp[:])
            p.v("tensor_scalar", tids[:], tidf[:], float(tt * 128), None, Op.add)
            for k in range(4):
                p.v("tensor_scalar", oh[:], iot[:], idx_all[:, tt, k:k + 1], None, Op.is_equal)
                p.v("tensor_tensor", oh[:], oh[:], pos_tok[:, 0:64], Op.mult)
                p.v("tensor_reduce", poscol[:, k:k + 1], oh[:], AX, Op.add)
            for k in range(4):
                p.v("tensor_scalar", poscol[:, k:k + 1], poscol[:, k:k + 1], -1.0, None, Op.add)
                p.v("tensor_scalar", base[:], idx_all[:, tt, k:k + 1], 256.0, None, Op.mult)
                p.v("tensor_tensor", base[:], base[:], poscol[:, k:k + 1], Op.add)
                p.v("tensor_scalar", ge[:], poscol[:, k:k + 1], 256.0, None, Op.is_ge)
                p.v("scalar_tensor_tensor", gidxf[:], ge[:], 100000.0, base[:], Op.mult, Op.add)
                p.v("tensor_copy", gidxu[:], gidxf[:])
                p.idma("gpsimd", out=table_loc, out_offset=IndirectOffsetOnAxis(ap=gidxu[:], axis=0),
                       in_=tids[:], in_offset=None, bounds_check=breg, oob_is_err=False)
                p.v("tensor_scalar", km[:], poscol[:, k:k + 1], 256.0, None, Op.is_lt)
                p.v("tensor_tensor", t1[:], base[:], km[:], Op.mult)
                p.v("scalar_tensor_tensor", gidxf[:], ge[:], float(T), t1[:], Op.mult, Op.add)
                p.v("tensor_copy", gbu[:, k:k + 1], gidxf[:])
            p.dma(gbidx_out[r, :], gbu[:])
        cc_table_hook(p)
        p.dma(tbl_cv[:], table_red.rearrange("(q a) x -> q (a x)", q=128))
        p.v("tensor_copy", tbl_cu[:], tbl_cv[:])
        p.dma(table_u32.rearrange("(q a) x -> q (a x)", q=128), tbl_cu[:])



def build_fused():
    from concourse.expressions import RuntimeValue
    p = Prog()
    xloc = p.din("xloc", [TPC, D], F32)
    Whead = p.din("Whead", [4, 3, 64, D], F32)
    bhead = p.din("bhead", [4, 3, 64], F32)
    bmask_in = p.din("bmask", [4, 128, 512], BF16)
    amask_in = p.din("amask", [4, 128, 512], BF16)
    identb_in = p.din("identb", [128, 128], BF16)
    identf_in = p.din("identf", [128, 128], F32)
    gw64 = p.din("gw64", [64, D], F32)
    iota_tok = p.din("iota_tok", [128, 64], F32)
    maskvec = p.din("maskvec", [8, 1], F32)
    cofff = p.din("cofff", [1, 1], F32)
    coffi = p.din("coffi", [1, 2], I32)
    gp = p.din("gp", [8, 768, 512], I32)
    gs = p.din("gs", [8, 768, 8], F32)
    up = p.din("up", [8, 768, 512], I32)
    us = p.din("us", [8, 768, 8], F32)
    dp = p.din("dp", [8, 1024, 384], I32)
    ds_ = p.din("ds", [8, 1024, 6], F32)
    sgp = p.din("sgp", [768, 512], I32)
    sgs = p.din("sgs", [768, 8], F32)
    sup = p.din("sup", [768, 512], I32)
    sus = p.din("sus", [768, 8], F32)
    sdp = p.din("sdp", [1024, 384], I32)
    sds = p.din("sds", [1024, 6], F32)
    Wh8 = p.din("Wh8", [2, 4, D, 128], F32)
    bac = p.din("bac", [2, D, 2], F32)
    adt = p.din("adt", [2, 2], F32)
    lnw = p.din("lnw", [D], F32)
    onw = p.din("onw", [1, 128], F32)
    msu = p.din("msu", [128, 128], F32)
    miu = p.din("miu", [128, 128], F32)
    ow_in = p.din("ow", [D, D], F32)
    ln2w = p.din("ln2w", [D], F32)
    fcw_in = p.din("fcw", [D, 4 * D], F32)
    pw_in = p.din("pw", [4 * D, D], F32)
    gw16 = p.din("gw1", [16, D], F32)
    w1_in = p.din("w1", [16, D, 512], F32)
    w2_in = p.din("w2", [16, 512, D], F32)
    w3_in = p.din("w3", [16, D, 512], F32)
    ws1_in = p.din("ws1", [D, D], F32)
    ws2_in = p.din("ws2", [D, D], F32)
    ws3_in = p.din("ws3", [D, D], F32)
    iot16 = p.din("iot16", [128, 16], F32)
    out = p.dout("out", [TPC, D], F32)

    xn_l = p.dint("xn_l", [TPC, D], F32)
    xnb_l = p.dint("xnb_l", [TPC, D], BF16)
    xnlo_l = p.dint("xnlo_l", [TPC, D], BF16)
    xnb_batch = p.dint("xnb_batch", [S, D], BF16)
    xnb_full = p.dint("xnb_full", [T, D], BF16)
    z4_l = p.dint("z4_l", [4, S, 64], F32)
    zAG = p.dint("zAG", [16, S, 64], F32)
    z_loc = p.dint("z_loc", [16, TPC, 64], F32)
    s0_l = p.dint("s0_l", [TPC, D], F32)
    countsl = p.dint("countsl", [64, 1], F32)
    counts_all = p.dint("counts_all", [512, 1], F32)
    table_loc = p.dint("table_loc", [T, 1], F32)
    table_red = p.dint("table_red", [T, 1], F32)
    table_u32 = p.dint("table_u32", [T, 1], U32)
    tbl_loc = p.dint("tbl_loc", [8, 256], U32)
    gbidx_l = p.dint("gbidx_l", [TPC, 4], U32)
    wts_l = p.dint("wts_l", [TPC, 4], F32)
    eo_l = p.dint("eo_l", [8, 256, D], BF16)
    eoall = p.dint("eoall", [T + 1, D], BF16)
    x1_l = p.dint("x1_l", [TPC, D], F32)
    xn2_l = p.dint("xn2_l", [TPC, D], F32)
    xn2b_l = p.dint("xn2b_l", [TPC, D], BF16)
    xn2lo_l = p.dint("xn2lo_l", [TPC, D], BF16)
    xn2b_batch = p.dint("xn2b_batch", [S, D], BF16)
    og_l = p.dint("og_l", [2, S, 128], BF16)
    ogAG = p.dint("ogAG", [8, S, 128], BF16)
    og_loc = p.dint("og_loc", [8, TPC, 128], BF16)
    s1_l = p.dint("s1_l", [TPC, D], F32)
    x1b_l = p.dint("x1b_l", [TPC, D], F32)

    GRP2 = [[0, 1, 2, 3], [4, 5, 6, 7]]
    GRP1 = [list(range(8))]

    with p.scope():
        coffs = p.sb("f_coffs", [1, 2], I32)
        p.dma(coffs[:], coffi)
        r_sync = p.nc.sync.alloc_register()
        r_sync2 = p.nc.sync.alloc_register()
        p.emit("sync", lambda e: p.nc.reg_load(r_sync, coffs[0:1, 1:2]), "comp",
               reads=[Prog._region(coffs[0:1, :])])
        p.emit("sync", lambda e: p.nc.reg_load(r_sync2, coffs[0:1, 0:1]), "comp",
               reads=[Prog._region(coffs[0:1, :])])

        emit_s1(p, "a_", xloc, xn_l, xnb_l, xnlo_l, TPC)
        p.emit("gpsimd", lambda e: e.collective_compute("AllGather", Op.bypass,
               replica_groups=GRP2, ins=[xnb_l.opt()], outs=[xnb_batch.opt()]), "cc",
               reads=[Prog._region(xnb_l.opt())], writes=[Prog._region(xnb_batch.opt())])
        p.emit("gpsimd", lambda e: e.collective_compute("AllGather", Op.bypass,
               replica_groups=GRP1, ins=[xnb_l.opt()], outs=[xnb_full.opt()]), "cc",
               reads=[Prog._region(xnb_l.opt())], writes=[Prog._region(xnb_full.opt())])
        emit_s2(p, "b_", xnb_batch, Whead, bhead, bmask_in, amask_in, identb_in, identf_in, z4_l)
        p.emit("gpsimd", lambda e: e.collective_compute("AllGather", Op.bypass,
               replica_groups=GRP2, ins=[z4_l.opt()], outs=[zAG.opt()]), "cc",
               reads=[Prog._region(z4_l.opt())], writes=[Prog._region(zAG.opt())])
        p.dma(z_loc.opt(), zAG[:, bass.ds(RuntimeValue(r_sync), TPC), :])
        emit_s4_sharded(p, "d_", xnb_l, xnlo_l, gw64, iota_tok, identf_in, maskvec, cofff,
                        countsl, counts_all, table_loc, table_red, table_u32,
                        gbidx_l, wts_l,
                        lambda pp: pp.emit("gpsimd", lambda e: e.collective_compute(
                            "AllGather", Op.bypass, replica_groups=GRP1,
                            ins=[countsl.opt()], outs=[counts_all.opt()]), "cc",
                            reads=[Prog._region(countsl)], writes=[Prog._region(counts_all)]),
                        lambda pp: pp.emit("gpsimd", lambda e: e.collective_compute(
                            "AllReduce", Op.add, replica_groups=GRP1,
                            ins=[table_loc.opt()], outs=[table_red.opt()]), "cc",
                            reads=[Prog._region(table_loc)], writes=[Prog._region(table_red)]),
                        TPC)
        emit_s3(p, "c_", xn_l, z_loc, p.din("gamma", [D], F32), p.din("beta", [D], F32), s0_l, TPC)
        p.dma(tbl_loc.rearrange("a b -> (a b)").unsqueeze(1), table_u32[bass.ds(RuntimeValue(r_sync2), TPC), :])
        emit_s5(p, "e_", xnb_full, tbl_loc, gp, gs, up, us, dp, ds_, identb_in, eo_l)
        p.emit("gpsimd", lambda e: e.collective_compute("AllGather", Op.bypass,
               replica_groups=GRP1, ins=[eo_l.opt()], outs=[eoall[0:T, :].opt()]), "cc",
               reads=[Prog._region(eo_l.opt())], writes=[Prog._region(eoall[0:T, :].opt())])
        with p.scope():
            zrow1 = p.sb("f_zrow1", [1, D], BF16)
            p.op("vector", "memset", zrow1[:], 0.0)
            p.dma(eoall[T:T + 1, :], zrow1[:])
        emit_s6(p, "f_", eoall, gbidx_l, wts_l, xloc, s0_l, xnb_l,
                sgp, sgs, sup, sus, sdp, sds, identb_in, x1_l, TPC)
        emit_s1(p, "g_", x1_l, xn2_l, xn2b_l, xn2lo_l, TPC)
        p.emit("gpsimd", lambda e: e.collective_compute("AllGather", Op.bypass,
               replica_groups=GRP2, ins=[xn2b_l.opt()], outs=[xn2b_batch.opt()]), "cc",
               reads=[Prog._region(xn2b_l.opt())], writes=[Prog._region(xn2b_batch.opt())])
        emit_s8(p, "h_", xn2b_batch, Wh8, bac, adt, lnw, onw, msu, miu, identf_in, identb_in, og_l)
        p.emit("gpsimd", lambda e: e.collective_compute("AllGather", Op.bypass,
               replica_groups=GRP2, ins=[og_l.opt()], outs=[ogAG.opt()]), "cc",
               reads=[Prog._region(og_l.opt())], writes=[Prog._region(ogAG.opt())])
        p.dma(og_loc.opt(), ogAG[:, bass.ds(RuntimeValue(r_sync), TPC), :])
        emit_s9(p, "i_", og_loc, xn2_l, ow_in, ln2w, fcw_in, pw_in, identb_in, identf_in,
                s1_l, x1b_l, TPC)
        emit_s10(p, "j_", xn2b_l, xn2lo_l, gw16, w1_in, w2_in, w3_in, ws1_in, ws2_in, ws3_in,
                 x1_l, s1_l, iot16, out, TPC)
    p.finish()
    return p


def kernel(**inputs):
    z = inputs
    x = np.ascontiguousarray(z["L0_attn_embedded"].astype(np.float32, copy=False)).reshape(T, D)
    bm, am = attn_masks()
    identb = np.eye(128).astype(BF)
    identf = np.eye(128, dtype=np.float32)
    iota64 = np.broadcast_to(np.arange(64, dtype=np.float32), (128, 64)).copy()
    iota16 = np.broadcast_to(np.arange(16, dtype=np.float32), (128, 16)).copy()
    msu = np.where(np.arange(128)[:, None] < np.arange(128)[None, :], 0.0, NEG).astype(np.float32)
    miu = np.where(np.arange(128)[:, None] <= np.arange(128)[None, :], 0.0, NEG).astype(np.float32)
    Wq, Wk, Wv = z["L0_attn_Wq"], z["L0_attn_Wk"], z["L0_attn_Wv"]
    bq, bk, bv = z["L0_attn_bq"], z["L0_attn_bk"], z["L0_attn_bv"]
    p = build_fused()
    im = []
    for c in range(NCORES):
        hs = [(c * 4 + j) % 16 for j in range(4)]
        Wh = np.stack([np.stack([W[h * 64:(h + 1) * 64] for W in (Wq, Wk, Wv)]) for h in hs])
        bh_ = np.stack([np.stack([bb[h * 64:(h + 1) * 64] for bb in (bq, bk, bv)]) for h in hs])
        sl8 = slice(c * 8, (c + 1) * 8)
        Whl, bacl, adtl = [], [], []
        for bhp in (2 * c, 2 * c + 1):
            h = bhp % 8
            cs = slice(h * 128, (h + 1) * 128)
            Whl.append(np.stack([z["L1_attn_qw"][:, cs], z["L1_attn_kw"][:, cs],
                                 z["L1_attn_vw"][:, cs], z["L1_attn_gw"][:, cs]]))
            bacl.append(np.stack([z["L1_attn_bw"][:, h], z["L1_attn_aw"][:, h]], 1))
            adtl.append(np.array([z["L1_attn_A_log"][h], z["L1_attn_dt_bias"][h]], np.float32))
        im.append({
            "xloc": np.ascontiguousarray(x[c * TPC:(c + 1) * TPC]),
            "Whead": np.ascontiguousarray(Wh), "bhead": np.ascontiguousarray(bh_),
            "bmask": bm, "amask": am, "identb": identb, "identf": identf,
            "gw64": z["L0_ffn_gate_w"], "iota_tok": iota64,
            "maskvec": (np.arange(8) < c).astype(np.float32).reshape(8, 1),
            "cofff": np.array([[c * TPC]], np.float32),
            "coffi": np.array([[c * TPC, (c % 4) * TPC]], np.int32),
            "gp": np.ascontiguousarray(z["L0_ffn_gate_packed"][sl8]),
            "gs": np.ascontiguousarray(z["L0_ffn_gate_scales"][sl8]),
            "up": np.ascontiguousarray(z["L0_ffn_up_packed"][sl8]),
            "us": np.ascontiguousarray(z["L0_ffn_up_scales"][sl8]),
            "dp": np.ascontiguousarray(z["L0_ffn_down_packed"][sl8]),
            "ds": np.ascontiguousarray(z["L0_ffn_down_scales"][sl8]),
            "sgp": z["L0_ffn_sh_gate_packed"], "sgs": z["L0_ffn_sh_gate_scales"],
            "sup": z["L0_ffn_sh_up_packed"], "sus": z["L0_ffn_sh_up_scales"],
            "sdp": z["L0_ffn_sh_down_packed"], "sds": z["L0_ffn_sh_down_scales"],
            "Wh8": np.ascontiguousarray(np.stack(Whl)), "bac": np.ascontiguousarray(np.stack(bacl)),
            "adt": np.stack(adtl), "lnw": z["L1_attn_ln1_w"],
            "onw": z["L1_attn_onorm_w"].reshape(1, 128), "msu": msu, "miu": miu,
            "ow": z["L1_attn_ow"], "ln2w": z["L1_attn_ln2_w"], "fcw": z["L1_attn_fcw"],
            "pw": z["L1_attn_pw"], "gw1": z["L1_ffn_gate_w"],
            "w1": z["L1_ffn_w1"], "w2": z["L1_ffn_w2"], "w3": z["L1_ffn_w3"],
            "ws1": z["L1_ffn_ws1"], "ws2": z["L1_ffn_ws2"], "ws3": z["L1_ffn_ws3"],
            "iot16": iota16,
            "gamma": z["L0_attn_gamma"], "beta": z["L0_attn_beta"],
        })
    res = run(p, im)
    outp = np.concatenate([r["out"] for r in res], 0)
    return outp.reshape(B, S, D).astype(np.float32, copy=False)




